# version 10
# baseline (speedup 1.0000x reference)
import numpy as np
import ml_dtypes
from contextlib import ExitStack

import concourse.bacc as bacc
import concourse.mybir as mybir
from concourse.bass_utils import run_bass_kernel_spmd
from concourse import dve_ops
from concourse.dve_op import DveOp
from concourse.dve_spec import Spec, Src0, Src1, C0, C1, C2, maxx, minn
from concourse.dve_spec import lower as dve_lower
from concourse.dve_table_gen import dve_ver_for

F32 = mybir.dt.float32
BF16 = mybir.dt.bfloat16
AF = mybir.ActivationFunctionType
ALU = mybir.AluOpType
AX = mybir.AxisListType

B, T, D = 16, 2048, 1024
NCORES, BPC = 8, 2
NT = T // 128
NH, HD = 16, 64
II = 2816
NKC = D // 128
NIC = II // 128
EPS = 1e-5
PIECES = [(0, 512), (512, 512), (1024, 512), (1536, 512), (2048, 512), (2560, 256)]

C_RND1 = float(1.5 * 2.0**23)
C_RND2 = float(1.5 * 2.0**24)


def _rnd(x, g):
    return np.round(x / g) * g


def _ref_t1(in0, in1, s0, s1, imm2):
    return _rnd(np.clip(in0, -s0, s0), 1.0).astype(in0.dtype)


def _ref_t12(in0, in1, s0, s1, imm2):
    m4 = np.clip(in0, -s0, s0)
    return (_rnd(np.clip(in0 - m4, -s0, s0), 2.0) + in1).astype(in0.dtype)


def _ref_fin(in0, in1, s0, s1, imm2):
    t3 = ((in0 > s0).astype(in0.dtype) - (in0 < -s0).astype(in0.dtype)) * imm2
    return (t3 + in1).astype(in0.dtype)


_VER = dve_ver_for("TRN2")


def _mk_op(name, spec, ref):
    sha = dve_lower(spec, ver=_VER, perf_en=False, name=name).sha(_VER)
    return DveOp(name, spec, subdim=False, uops_sha={_VER: sha}, reference=ref)


_c4 = minn(maxx(Src0, C2), C0)
OP_T1 = _mk_op("ANT_FP4_T1", Spec(body=(_c4 + C1) - C1), _ref_t1)
_d2 = minn(maxx(Src0 - _c4, C2), C0)
OP_T12 = _mk_op("ANT_FP4_T12", Spec(body=((_d2 + C1) - C1) + Src1), _ref_t12)
_g3 = (Src0 > C0) - (Src0 < C1)
OP_FIN = _mk_op("ANT_FP4_FIN", Spec(body=_g3 * C2 + Src1), _ref_fin)


def _register_ops():
    victims = ["DVEHASH32L_SEED", "DVEHASH32L_BODY", "DVEHASH32L_DRAIN"]
    for op, victim in zip((OP_T1, OP_T12, OP_FIN), victims):
        if op.name in dve_ops._SUB_OPCODE_FOR_NAME:
            continue
        row = dve_ops._SUB_OPCODE_FOR_NAME.pop(victim)
        idx = next(i for i, o in enumerate(dve_ops.OPS) if o.name == victim)
        dve_ops.OPS[idx] = op
        dve_ops._SUB_OPCODE_FOR_NAME[op.name] = row
        dve_ops.CUSTOM_DVE_SPECS.pop(victim, None)
        dve_ops.CUSTOM_DVE_REFS.pop(victim, None)
        dve_ops.CUSTOM_DVE_SPECS[op.name] = op.spec
        dve_ops.CUSTOM_DVE_REFS[op.name] = op.ref


_register_ops()


_OUT_KW = ("out", "accum_out")


def _ap_rng(x):
    t = getattr(x, "tensor", None)
    name = getattr(t, "name", None)
    if name is None:
        return None
    try:
        off = x.offset
        reach = 1
        for step, count in list(x.ap)[1:]:
            reach += (count - 1) * step
        return (name, off, off + reach)
    except Exception:
        return (name, 0, 1 << 60)


def _overlaps(a, bs):
    for b in bs:
        if a[0] == b[0] and a[1] < b[2] and b[1] < a[2]:
            return True
    return False


class DrainWrap:

    _OPS = frozenset((
        "tensor_tensor", "tensor_scalar", "tensor_copy", "tensor_reduce",
        "reduce_max", "reduce_sum", "reciprocal", "memset", "_custom_dve",
        "scalar_tensor_tensor", "activation", "select", "copy",
    ))

    def __init__(self, eng, state):
        object.__setattr__(self, "_e", eng)
        object.__setattr__(self, "_st", state)

    def __getattr__(self, name):
        a = getattr(self._e, name)
        if name not in self._OPS:
            return a

        def f(*args, **kw):
            st = self._st
            rngs = [r for r in (_ap_rng(x) for x in args) if r] +                    [r for r in (_ap_rng(v) for v in kw.values()) if r]
            if name == "_custom_dve":
                outs = [_ap_rng(kw[k]) for k in _OUT_KW if kw.get(k) is not None]
            else:
                outs = [_ap_rng(args[0])]
                if kw.get("accum_out") is not None:
                    outs.append(_ap_rng(kw["accum_out"]))
            outs = [o for o in outs if o]
            ins = [r for r in rngs if r not in outs]
            if (any(_overlaps(o, st["w"]) or _overlaps(o, st["r"]) for o in outs)
                    or any(_overlaps(i, st["w"]) for i in ins)):
                self._e.drain()
                st["w"].clear()
                st["r"].clear()
            st["w"] += outs
            st["r"] += ins
            return a(*args, **kw)
        return f


class Ctr:
    def __init__(self, handle, step=1):
        self.h, self.v, self.step = handle, 0, step

    def inc(self, inst):
        inst.then_inc(self.h, self.step)
        self.v += self.step


def pv_off(h):
    b = h // 7
    return b * 512 + (h - 7 * b) * 65


class KB:

    def __init__(self, debug_stop=None):
        self.debug_stop = debug_stop
        self.nc = bacc.Bacc("TRN2", target_bir_lowering=False)
        self.es = ExitStack()
        self.blk = None
        self._haz = {e: {"w": [], "r": []} for e in ("v", "s", "p")}

    def te(self, f):
        self.blk.tensor(f)

    def ve(self, f):
        self.blk.vector(lambda e: f(DrainWrap(e, self._haz["v"])))

    def se(self, f):
        self.blk.scalar(lambda e: f(DrainWrap(e, self._haz["s"])))

    def po(self, f):
        self.blk.gpsimd(lambda e: f(DrainWrap(e, self._haz["p"])))

    def qe(self, f):
        self.blk.sync(f)

    def sbuf(self, name, shape, dt=F32, stack=None):
        return (stack or self.es).enter_context(self.nc.sbuf_tensor(name, shape, dt))

    def psum(self, name, shape, dt=F32):
        return self.es.enter_context(self.nc.psum_tensor(name, shape, dt))

    def sem(self, name, step=1):
        return Ctr(self.es.enter_context(self.nc.semaphore(name)), step)

    def declare_io(self):
        nc = self.nc
        self.x_in = nc.dram_tensor("x", [BPC * T, D], F32, kind="ExternalInput").ap()
        self.wq_in = nc.dram_tensor("wq", [D, D], F32, kind="ExternalInput").ap()
        self.wk_in = nc.dram_tensor("wk", [HD, D], F32, kind="ExternalInput").ap()
        self.wv_in = nc.dram_tensor("wv", [HD, D], F32, kind="ExternalInput").ap()
        self.wf_in = nc.dram_tensor("wf", [D, D], F32, kind="ExternalInput").ap()
        self.bf_in = nc.dram_tensor("bfv", [1, D], F32, kind="ExternalInput").ap()
        self.wg_in = nc.dram_tensor("wg", [II, D], F32, kind="ExternalInput").ap()
        self.wu_in = nc.dram_tensor("wu", [II, D], F32, kind="ExternalInput").ap()
        self.wd_in = nc.dram_tensor("wd", [D, II], F32, kind="ExternalInput").ap()
        self.c_in = nc.dram_tensor("c", [BPC, D], F32, kind="ExternalInput").ap()
        self.wmod_in = nc.dram_tensor("wmod", [D, 6 * D], F32, kind="ExternalInput").ap()
        self.ident_in = nc.dram_tensor("ident", [128, 128], BF16, kind="ExternalInput").ap()
        self.mnew_in = nc.dram_tensor("mnew", [128, 128], BF16, kind="ExternalInput").ap()
        self.mold_in = nc.dram_tensor("mold", [128, 128], BF16, kind="ExternalInput").ap()
        self.identf_in = nc.dram_tensor("identf", [BPC, BPC], F32, kind="ExternalInput").ap()
        self.out_d = nc.dram_tensor("out", [BPC * T, D], F32, kind="ExternalOutput").ap()

    def declare_persistent(self):
        sbuf, psum, sem = self.sbuf, self.psum, self.sem
        self.wq_T = sbuf("wq_T", [128, 64 * 128], BF16)
        self.kv_T = sbuf("kv_T", [128, 8 * 128], BF16)
        self.kv2_T = sbuf("kv2_T", [128, 8 * 128], BF16)
        self.wf_T = sbuf("wf_T", [128, 8 * 1024], BF16)
        self.wg_q = sbuf("wg_q", [128, 8 * II], BF16)
        self.wu_q = sbuf("wu_q", [128, 8 * II], BF16)
        self.wd_q = sbuf("wd_q", [128, NIC * 1024], BF16)
        self.ident = sbuf("ident_s", [128, 128], BF16)
        self.mnew = sbuf("mnew_s", [128, 128], BF16)
        self.mold = sbuf("mold_s", [128, 128], BF16)
        self.onesb = self.mnew[0:1, :]
        self.identf = sbuf("identf_s", [BPC, BPC], F32)
        self.bfrow = sbuf("bfrow", [1, D], BF16)
        self.consts = sbuf("consts", [128, 4], F32)
        self.ca1c = sbuf("ca1c", [128, NKC * BPC], F32)
        self.sac = sbuf("sac", [128, NKC * BPC], F32)
        self.rows_d = self.nc.dram_tensor("rows_d", [BPC, 4 * D], BF16).ap()
        self.cbb = sbuf("cbb", [128, D], BF16)
        self.sbb = sbuf("sbb", [128, D], BF16)
        self.gab = sbuf("gab", [128, D], BF16)
        self.gbb = sbuf("gbb", [128, D], BF16)
        self.ps_A = psum("ps_A", [128, 1024], F32)
        self.ps_B = psum("ps_B", [128, 1024], F32)
        self.ps_pv = psum("ps_pv", [128, 1280], F32)
        self.ps_T = psum("ps_T", [128, 512], BF16)
        self.sLD = sem("sLD", 16)
        self.sL2 = sem("sL2", 16)
        self.sL3 = sem("sL3", 16)
        self.sX = sem("sX", 16)
        self.sX2 = sem("sX2", 16)
        self.sOUT = sem("sOUT", 16)
        self.sPE = sem("sPE")
        self.sV = sem("sV")
        self.sS = sem("sS")
        self.sP = sem("sP")

    def declare_main_bufs(self):
        sbuf = self.sbuf
        self.xbuf = sbuf("xbuf", [128, D], F32)
        self.xbuf2 = sbuf("xbuf2", [128, D], F32)
        self.xb = [self.xbuf, self.xbuf2]
        self.x_ld = [0, 0]
        self.res1_cnt = {}
        self.x1buf = sbuf("x1buf", [128, D], F32)
        self.h2buf = sbuf("h2buf", [128, D], F32)
        self.bfA = sbuf("bfA", [128, D], BF16)
        self.bfB = sbuf("bfB", [128, D], BF16)
        self.bfC = sbuf("bfC", [128, D], BF16)
        self.bfD = sbuf("bfD", [128, D], BF16)
        self.bfE = sbuf("bfE", [128, D], BF16)
        self.probsT = None
        self.kk = sbuf("kk", [128, 4 * 128], BF16)
        self.v_all = sbuf("v_all", [128, 4 * 65], BF16)
        self.stat = sbuf("stat", [128, 8], F32)
        self.bmm = sbuf("bmm", [128, 64], F32)
        self.qr2 = sbuf("qr2", [128, 64], F32)
        self.qsh = sbuf("qsh", [128, 64], BF16)
        self.sums = self.bmm[:, 48:64]
        self.rsum = self.qr2[:, 48:64]
        self.probsT2 = self.bfD
        self.probsT = self.bfB
        self.tf_a = self.h2buf[:, 512:1024]
        self.tf_b = self.h2buf[:, 0:512]
        self.hmT = self.bfD[:, 0:512]
        self.hmq = self.bfD[:, 512:1024]

    def phase0(self):
        s = self
        with s.nc.Block() as blk:
            s.blk = blk

            def _(q):
                q.dma_start(s.ident[:], s.ident_in).then_inc(s.sLD.h, 16)
                q.dma_start(s.mnew[:], s.mnew_in).then_inc(s.sLD.h, 16)
                q.dma_start(s.mold[:], s.mold_in).then_inc(s.sLD.h, 16)
                q.dma_start(s.identf[:], s.identf_in).then_inc(s.sLD.h, 16)
                s.sLD.v += 64
            s.qe(_)

            def _(sc):
                sc.preload_act_table(AF.Exp)
            s.se(_)

            def _(v):
                v.memset(s.consts[:, 0:1], -30.0)
                v.memset(s.consts[:, 1:2], -0.5)
            s.ve(_)

    def phase1_mod(self):
        s = self
        with ExitStack() as ph:
            c_sb = s.sbuf("c_sb", [BPC, D], F32, ph)
            sct = s.sbuf("sct", [BPC, D], F32, ph)
            sc_bf = s.sbuf("sc_bf", [BPC, D], BF16, ph)
            rowstg = s.sbuf("rowstg", [BPC, D], BF16, ph)
            scT = s.sbuf("scT", [128, NKC * BPC], F32, ph)
            wmod_f = s.sbuf("wmod_f", [128, 1024], F32, ph)
            wmod_g = s.sbuf("wmod_g", [128, 1024], F32, ph)
            wmod_h = s.sbuf("wmod_h", [128, 1024], F32, ph)
            tmpr = wmod_f[0:BPC, :]

            with s.nc.Block() as blk:
                s.blk = blk

                def _(q):
                    q.dma_start(c_sb[:], s.c_in).then_inc(s.sLD.h, 16)
                    q.dma_start(sct[0:1, :], s.bf_in).then_inc(s.sLD.h, 16)
                    s.sLD.v += 32
                s.qe(_)

                def _(v):
                    v.wait_ge(s.sLD.h, s.sLD.v)
                    s.sV.inc(v.tensor_copy(s.bfrow[:], sct[0:1, :]))
                s.ve(_)

                def _(sc):
                    sc.wait_ge(s.sV.h, s.sV.v)
                    s.sS.inc(sc.activation(sct[:], c_sb[:], AF.Tanh, scale=0.5))
                s.se(_)

                def _(v):
                    v.wait_ge(s.sS.h, s.sS.v)
                    v.tensor_scalar(sct[:], sct[:], 0.5, 0.5, ALU.mult, ALU.add)
                    s.sV.inc(v.tensor_tensor(sct[:], sct[:], c_sb[:], ALU.mult))
                s.ve(_)

                def _(t):
                    t.wait_ge(s.sV.h, s.sV.v)
                    a = None
                    for kc in range(NKC):
                        a = t.transpose(s.ps_A[:, kc * BPC:(kc + 1) * BPC],
                                        sct[:, kc * 128:(kc + 1) * 128],
                                        s.identf[:])
                    s.sPE.inc(a)
                s.te(_)

                def _(v):
                    v.wait_ge(s.sPE.h, s.sPE.v)
                    s.sV.inc(v.tensor_copy(scT[:], s.ps_A[:, 0:NKC * BPC]))
                s.ve(_)

                WNB = 3
                wbufs = [wmod_f, wmod_g, wmod_h]
                wlds = [s.sLD, s.sL2, s.sL3]
                wqes = [s.qe, s.po, s.se]
                wm_ld = [0] * WNB
                wm_pe = [0] * WNB
                wi = 0
                for pair in range(3):
                    for sub in range(2):
                        vec = pair * 2 + sub
                        ps = s.ps_A if sub == 0 else s.ps_B
                        for kc in range(NKC):
                            buf, ld = wbufs[wi % WNB], wlds[wi % WNB]
                            sv_now, ss_now = s.sV.v, s.sS.v

                            def _(q, vec=vec, kc=kc, wi=wi, ld=ld, buf=buf,
                                  sv_now=sv_now, ss_now=ss_now):
                                q.wait_ge(s.sPE.h, wm_pe[wi % WNB])
                                q.wait_ge(s.sV.h, sv_now)
                                q.wait_ge(s.sS.h, ss_now)
                                q.dma_start(buf[:],
                                            s.wmod_in[kc * 128:(kc + 1) * 128,
                                                      vec * D:(vec + 1) * D]
                                            ).then_inc(ld.h, 16)
                                ld.v += 16
                            wqes[wi % WNB](_)
                            wm_ld[wi % WNB] = ld.v

                            def _(t, ps=ps, kc=kc, buf=buf, wi=wi, ld=ld,
                                  sv_now=sv_now):
                                t.wait_ge(ld.h, wm_ld[wi % WNB])
                                t.wait_ge(s.sV.h, sv_now)
                                a = None
                                for piece in range(2):
                                    a = t.matmul(ps[0:BPC, piece * 512:(piece + 1) * 512],
                                                 scT[:, kc * BPC:(kc + 1) * BPC],
                                                 buf[:, piece * 512:(piece + 1) * 512],
                                                 start=(kc == 0), stop=(kc == NKC - 1),
                                                 skip_group_check=True)
                                s.sPE.inc(a)
                            s.te(_)
                            wm_pe[wi % WNB] = s.sPE.v
                            wi += 1

                    def _(v):
                        v.wait_ge(s.sPE.h, s.sPE.v)
                        v.tensor_copy(c_sb[:], s.ps_A[0:BPC, :])
                        s.sV.inc(v.tensor_copy(sct[:], s.ps_B[0:BPC, :]))
                    s.ve(_)

                    if pair == 0:
                        for rowbuf, dst, add1 in ((c_sb, s.sac, 0.0), (sct, s.ca1c, 1.0)):
                            for kc in range(NKC):
                                def _(t, rowbuf=rowbuf, kc=kc):
                                    t.wait_ge(s.sV.h, s.sV.v)
                                    s.sPE.inc(t.transpose(
                                        s.ps_B[:, 512:512 + BPC],
                                        rowbuf[:, kc * 128:(kc + 1) * 128],
                                        s.identf[:]))
                                s.te(_)

                                def _(v, dst=dst, add1=add1, kc=kc):
                                    v.wait_ge(s.sPE.h, s.sPE.v)
                                    a = None
                                    for sq in range(BPC):
                                        a = v.tensor_scalar(
                                            dst[:, kc * BPC + sq: kc * BPC + sq + 1],
                                            s.ps_B[:, 512 + sq:512 + sq + 1],
                                            add1, None, ALU.add)
                                    s.sV.inc(a)
                                s.ve(_)
                    elif pair == 1:
                        def _(sc):
                            sc.wait_ge(s.sV.h, s.sV.v)
                            s.sS.inc(sc.activation(tmpr, c_sb[:], AF.Sign))
                        s.se(_)

                        def _(v):
                            v.wait_ge(s.sS.h, s.sS.v)
                            v.tensor_tensor(c_sb[:], c_sb[:], tmpr, ALU.mult)
                            v.tensor_scalar(c_sb[:], c_sb[:], 0.25, None, ALU.max)
                            v.tensor_tensor(sc_bf[:], c_sb[:], tmpr, ALU.mult)
                            s.sV.inc(v.tensor_copy(rowstg[:], sct[:]))
                        s.ve(_)

                        def _(q):
                            q.wait_ge(s.sV.h, s.sV.v)
                            q.dma_start(s.rows_d[:, 2 * D:3 * D], sc_bf[:]).then_inc(s.sX.h, 16)
                            q.dma_start(s.rows_d[:, 1 * D:2 * D], rowstg[:]).then_inc(s.sX.h, 16)
                            s.sX.v += 32
                            q.wait_ge(s.sX.h, s.sX.v)
                        s.qe(_)
                    else:
                        def _(sc):
                            sc.wait_ge(s.sV.h, s.sV.v)
                            s.sS.inc(sc.activation(tmpr, sct[:], AF.Sign))
                        s.se(_)

                        def _(v):
                            v.wait_ge(s.sS.h, s.sS.v)
                            v.wait_ge(s.sX.h, s.sX.v)
                            v.tensor_scalar(sc_bf[:], c_sb[:], 1.0, None, ALU.add)
                            v.tensor_tensor(sct[:], sct[:], tmpr, ALU.mult)
                            v.tensor_scalar(sct[:], sct[:], 0.25, None, ALU.max)
                            s.sV.inc(v.tensor_tensor(rowstg[:], sct[:], tmpr, ALU.mult))
                        s.ve(_)

                        def _(q):
                            q.wait_ge(s.sV.h, s.sV.v)
                            q.dma_start(s.rows_d[:, 0 * D:1 * D], sc_bf[:]).then_inc(s.sX.h, 16)
                            q.dma_start(s.rows_d[:, 3 * D:4 * D], rowstg[:]).then_inc(s.sX.h, 16)
                            s.sX.v += 32
                            q.wait_ge(s.sX.h, s.sX.v)
                        s.qe(_)

    def emit_quant(self, v, src_f32, F, out_bf, w_scr, t1_bf, t12_bf,
                   bm, r2, sh, fold=1.0):
        nb = F // 16
        src = src_f32[:, 0:F]
        v.tensor_reduce(bm[:, 0:nb], src.rearrange("p (n k) -> p n k", k=16),
                        AX.X, ALU.max, apply_absolute_value=True)
        v.tensor_scalar(bm[:, 0:nb], bm[:, 0:nb], 1.0 / 12.0, 5e-13, ALU.mult, ALU.max)
        v.tensor_scalar(sh[:, 0:nb], bm[:, 0:nb], fold, None, ALU.mult)
        v.reciprocal(r2[:, 0:nb], bm[:, 0:nb])
        rb = r2[:, 0:nb].unsqueeze(2).broadcast_to([128, nb, 16])
        v.tensor_tensor(w_scr[:, 0:F].rearrange("p (n k) -> p n k", k=16),
                        src.rearrange("p (n k) -> p n k", k=16), rb, ALU.mult)
        v._custom_dve(OP_T1, out=t1_bf[:, 0:F], in0=w_scr[:, 0:F],
                      s0=4.0, s1=C_RND1, imm2=-4.0)
        v._custom_dve(OP_T12, out=t12_bf[:, 0:F], in0=w_scr[:, 0:F],
                      in1=t1_bf[:, 0:F], s0=4.0, s1=C_RND2, imm2=-4.0)
        v._custom_dve(OP_FIN, out=t1_bf[:, 0:F], in0=w_scr[:, 0:F],
                      in1=t12_bf[:, 0:F], s0=10.0, s1=-10.0, imm2=4.0)
        shb = sh[:, 0:nb].unsqueeze(2).broadcast_to([128, nb, 16])
        return v.tensor_tensor(out_bf[:, 0:F].rearrange("p (n k) -> p n k", k=16),
                               t1_bf[:, 0:F].rearrange("p (n k) -> p n k", k=16),
                               shb, ALU.mult)

    def emit_quant2(self, src_f32, F, out_bf, w_scr, t1_bf, t12_bf,
                    bm, r2, sh, fold=1.0, pre=None):
        s = self
        nb = F // 16

        def _(v):
            if pre is not None:
                pre(v)
            srcv = src_f32[:, 0:F]
            v.tensor_reduce(bm[:, 0:nb], srcv.rearrange("p (n k) -> p n k", k=16),
                            AX.X, ALU.max, apply_absolute_value=True)
            v.tensor_scalar(bm[:, 0:nb], bm[:, 0:nb], 1.0 / 12.0, 5e-13,
                            ALU.mult, ALU.max)
            v.tensor_scalar(sh[:, 0:nb], bm[:, 0:nb], fold, None, ALU.mult)
            s.sV.inc(v.reciprocal(r2[:, 0:nb], bm[:, 0:nb]))
        s.ve(_)

        pe_now = s.sPE.v

        def _(p):
            p.wait_ge(s.sV.h, s.sV.v)
            p.wait_ge(s.sPE.h, pe_now)
            rb = r2[:, 0:nb].unsqueeze(2).broadcast_to([128, nb, 16])
            s.sP.inc(p.tensor_tensor(
                w_scr[:, 0:F].rearrange("p (n k) -> p n k", k=16),
                src_f32[:, 0:F].rearrange("p (n k) -> p n k", k=16),
                rb, ALU.mult))
        s.po(_)

        def _(v):
            v.wait_ge(s.sP.h, s.sP.v)
            v._custom_dve(OP_T1, out=t1_bf[:, 0:F], in0=w_scr[:, 0:F],
                          s0=4.0, s1=C_RND1, imm2=-4.0)
            v._custom_dve(OP_T12, out=t12_bf[:, 0:F], in0=w_scr[:, 0:F],
                          in1=t1_bf[:, 0:F], s0=4.0, s1=C_RND2, imm2=-4.0)
            s.sV.inc(v._custom_dve(OP_FIN, out=t1_bf[:, 0:F], in0=w_scr[:, 0:F],
                                   in1=t12_bf[:, 0:F], s0=10.0, s1=-10.0, imm2=4.0))
        s.ve(_)

        def _(p):
            p.wait_ge(s.sV.h, s.sV.v)
            p.wait_ge(s.sPE.h, pe_now)
            shb = sh[:, 0:nb].unsqueeze(2).broadcast_to([128, nb, 16])
            s.sP.inc(p.tensor_tensor(
                out_bf[:, 0:F].rearrange("p (n k) -> p n k", k=16),
                t1_bf[:, 0:F].rearrange("p (n k) -> p n k", k=16),
                shb, ALU.mult))
        s.po(_)
        return s.sP.v

    def transpose_blocks(self, src_bf, n_blocks, dst, dst_offs, evac="v",
                         wait_p=None):
        s = self
        for g0 in range(0, n_blocks, 4):
            gn = min(4, n_blocks - g0)

            def _(t, g0=g0, gn=gn):
                t.wait_ge(s.sV.h, s.sV.v)
                if wait_p is not None:
                    t.wait_ge(s.sP.h, wait_p)
                if evac == "s":
                    t.wait_ge(s.sS.h, s.sS.v)
                a = None
                for j in range(gn):
                    a = t.transpose(s.ps_T[:, j * 128:(j + 1) * 128],
                                    src_bf[:, (g0 + j) * 128:(g0 + j + 1) * 128],
                                    s.ident[:])
                s.sPE.inc(a)
            s.te(_)

            def _(e, g0=g0, gn=gn):
                e.wait_ge(s.sPE.h, s.sPE.v)
                cp = e.tensor_copy if evac == "v" else e.copy
                a = None
                for j in range(gn):
                    a = cp(dst[:, dst_offs[g0 + j]:dst_offs[g0 + j] + 128],
                           s.ps_T[:, j * 128:(j + 1) * 128])
                (s.sV if evac == "v" else s.sS).inc(a)
            (s.ve if evac == "v" else s.se)(_)

    def phase2_weights(self):
        s = self
        with ExitStack() as ph:
            WHALF = II // 2
            stg = s.sbuf("stg", [128, WHALF], F32, ph)
            stg2 = s.sbuf("stg2", [128, WHALF], F32, ph)
            stg3 = s.sbuf("stg3", [128, WHALF], F32, ph)
            stq = s.sbuf("stq", [128, WHALF], BF16, ph)
            qt1 = s.sbuf("qt1", [128, WHALF], BF16, ph)
            qt12 = s.sbuf("qt12", [128, WHALF], BF16, ph)
            bmw = s.sbuf("bmw", [128, WHALF // 16], F32, ph)
            qr2w = s.sbuf("qr2w", [128, WHALF // 16], F32, ph)
            qshw = s.sbuf("qshw", [128, WHALF // 16], BF16, ph)

            with s.nc.Block() as blk:
                s.blk = blk
                NB = 3
                bufs = [stg, stg2, stg3]
                lds = [s.sLD, s.sL2, s.sL3]
                qes = [s.qe, s.po, s.se]
                st_state = {"i": 0, "ld": [0] * NB, "c": [None] * NB}

                def load_rows(src_ap, ncols):
                    i = st_state["i"]
                    buf, ld = bufs[i % NB], lds[i % NB]
                    gate = st_state["c"][i % NB]
                    if gate is None:
                        gate = s.sV.v

                    def _(q, ld=ld, gate=gate):
                        q.wait_ge(s.sV.h, gate)
                        q.dma_start(buf[:, 0:ncols], src_ap).then_inc(ld.h, 16)
                        ld.v += 16
                    qes[i % NB](_)
                    st_state["ld"][i % NB] = ld.v
                    st_state["i"] += 1
                    return buf

                def cast_stage(buf, ncols):
                    i = st_state["i"] - 1
                    ld = lds[i % NB]
                    thr = st_state["ld"][i % NB]
                    pe_now = s.sPE.v

                    def _(v, ld=ld, thr=thr, pe_now=pe_now):
                        v.wait_ge(ld.h, thr)
                        v.wait_ge(s.sPE.h, pe_now)
                        s.sV.inc(v.tensor_copy(stq[:, 0:ncols], buf[:, 0:ncols]))
                    s.ve(_)
                    st_state["c"][i % NB] = s.sV.v

                def quant_stage(buf, ncols, fold):
                    i = st_state["i"] - 1
                    ld = lds[i % NB]
                    thr = st_state["ld"][i % NB]
                    sp_now = s.sP.v

                    def _pre(v, ld=ld, thr=thr, sp_now=sp_now):
                        v.wait_ge(ld.h, thr)
                        v.wait_ge(s.sP.h, sp_now)
                    qw = s.emit_quant2(buf, ncols, stq, buf, qt1, qt12,
                                       bmw, qr2w, qshw, fold, pre=_pre)
                    st_state["c"][i % NB] = s.sV.v
                    return qw

                for oc in range(NKC):
                    b = load_rows(s.wq_in[oc * 128:(oc + 1) * 128, :], D)
                    cast_stage(b, D)
                    s.transpose_blocks(stq, NKC, s.wq_T,
                                       [(oc * 8 + kc) * 128 for kc in range(NKC)],
                                       evac="s")

                kvi = st_state["i"]
                kvbuf, kvld = bufs[kvi % NB], lds[kvi % NB]
                kvgate = st_state["c"][kvi % NB]
                if kvgate is None:
                    kvgate = s.sV.v

                def _(q, kvld=kvld, kvgate=kvgate):
                    q.wait_ge(s.sV.h, kvgate)
                    q.dma_start(kvbuf[0:HD, 0:D], s.wk_in).then_inc(kvld.h, 16)
                    q.dma_start(kvbuf[64:128, 0:D], s.wv_in).then_inc(kvld.h, 16)
                    kvld.v += 32
                qes[kvi % NB](_)
                st_state["ld"][kvi % NB] = kvld.v
                st_state["i"] += 1
                cast_stage(kvbuf, D)
                for kc in range(NKC):
                    def _(t, kc=kc):
                        t.wait_ge(s.sV.h, s.sV.v)
                        t.wait_ge(s.sS.h, s.sS.v)
                        s.sPE.inc(t.transpose(s.ps_T[:, 0:128],
                                              stq[:, kc * 128:(kc + 1) * 128],
                                              s.ident[:]))
                    s.te(_)

                    def _(v, kc=kc):
                        v.wait_ge(s.sPE.h, s.sPE.v)
                        v.tensor_copy(s.kv_T[:, kc * 128:(kc + 1) * 128], s.ps_T[:, 0:128])
                        v.tensor_copy(s.kv2_T[:, kc * 128:kc * 128 + 64],
                                      s.ps_T[:, 64:128])
                        s.sV.inc(v.tensor_copy(s.kv2_T[:, kc * 128 + 64:(kc + 1) * 128],
                                               s.ps_T[:, 0:64]))
                    s.ve(_)

                for oc in range(NKC):
                    b = load_rows(s.wf_in[oc * 128:(oc + 1) * 128, :], D)
                    cast_stage(b, D)
                    s.transpose_blocks(stq, NKC, s.wf_T,
                                       [kc * 1024 + oc * 128 for kc in range(NKC)],
                                       evac="s")

                for w_src, w_dst, fold in ((s.wg_in, s.wg_q, 1.0), (s.wu_in, s.wu_q, 0.5)):
                    for ic in range(NIC):
                        b = load_rows(w_src[ic * 128:(ic + 1) * 128, :], D)
                        qw = quant_stage(b, D, fold)
                        s.transpose_blocks(stq, NKC, w_dst,
                                           [kc * II + ic * 128 for kc in range(NKC)],
                                           evac="s", wait_p=qw)

                for oc in range(NKC):
                    for hf in range(2):
                        b = load_rows(s.wd_in[oc * 128:(oc + 1) * 128,
                                              hf * WHALF:(hf + 1) * WHALF], WHALF)
                        qw = quant_stage(b, WHALF, 1.0)
                        s.transpose_blocks(stq, WHALF // 128, s.wd_q,
                                           [(hf * 11 + ic) * 1024 + oc * 128
                                            for ic in range(WHALF // 128)],
                                           evac="s", wait_p=qw)

    def emit_ln(self, src, dst, wait_out=False):
        s = self

        def _(v):
            s.sV.inc(v.tensor_reduce(s.stat[:, 0:1], src[:], AX.X, ALU.add))
        s.ve(_)

        def _(sc):
            sc.wait_ge(s.sV.h, s.sV.v)
            s.sS.inc(sc.activation(s.bfE[:], src[:], AF.Square,
                                   accum_out=s.stat[:, 3:4]))
        s.se(_)

        def _(v):
            v.wait_ge(s.sS.h, s.sS.v)
            v.tensor_scalar(s.stat[:, 1:2], s.stat[:, 0:1], 1.0 / D, None, ALU.mult)
            v.tensor_scalar(s.stat[:, 2:3], s.stat[:, 1:2], s.stat[:, 1:2], -EPS,
                            ALU.mult, ALU.add)
            s.sV.inc(v.tensor_scalar(s.stat[:, 4:5], s.stat[:, 3:4], 1.0 / D,
                                     s.stat[:, 2:3], ALU.mult, ALU.subtract))
        s.ve(_)

        def _(p):
            p.wait_ge(s.sV.h, s.sV.v)
            s.sP.inc(p.tensor_tensor(s.stat[:, 5:6], s.stat[:, 4:5],
                                     s.consts[:, 1:2], ALU.pow))
        s.po(_)

        def _(v):
            v.wait_ge(s.sP.h, s.sP.v)
            if wait_out:
                v.wait_ge(s.sOUT.h, s.sOUT.v)
            s.sV.inc(v.tensor_scalar(dst[:], src[:], s.stat[:, 1:2], s.stat[:, 5:6],
                                     ALU.subtract, ALU.mult))
        s.ve(_)


    def _dbg_dump(self, src_ap, row0):
        s = self

        def _(v):
            s.sV.inc(v.tensor_copy(s.h2buf[:], src_ap))
        s.ve(_)

        def _(q):
            q.wait_ge(s.sV.h, s.sV.v)
            q.wait_ge(s.sOUT.h, s.sOUT.v)
            q.dma_start(s.out_d[row0:row0 + 128, :], s.h2buf[:]).then_inc(s.sOUT.h, 16)
            s.sOUT.v += 16
        s.qe(_)

    def emit_tile(self, seq, qt):
        s = self
        row0 = seq * T + qt * 128
        ti = seq * NT + qt
        kts = [k for k in (qt - 2, qt - 1, qt) if k >= 0]
        slot = qt % 4
        s.xbuf = s.xb[ti % 2]

        if ti + 1 < BPC * NT:
            nti = ti + 1
            nrow = (nti // NT) * T + (nti % NT) * 128
            gate = s.res1_cnt.get(ti - 1)
            sem2 = s.sX if nti % 2 == 0 else s.sX2

            def _(q, nti=nti, nrow=nrow, gate=gate, sem2=sem2):
                if gate is None:
                    q.wait_ge(s.sV.h, s.sV.v)
                else:
                    q.wait_ge(s.sP.h, gate)
                q.dma_start(s.xb[nti % 2][:],
                            s.x_in[nrow:nrow + 128, :]).then_inc(sem2.h, 16)
                sem2.v += 16
            s.qe(_)
            s.x_ld[nti % 2] = sem2.v

        xsem = s.sX if ti % 2 == 0 else s.sX2
        xw = s.x_ld[ti % 2]

        def _(v):
            v.wait_ge(xsem.h, xw)
        s.ve(_)

        def _(sc):
            sc.wait_ge(xsem.h, xw)
        s.se(_)

        if s.debug_stop == "noop":
            def _(v):
                s.sV.inc(v.tensor_copy(s.h2buf[:], s.xbuf[:]))
            s.ve(_)

            def _(q):
                q.wait_ge(s.sV.h, s.sV.v)
                q.wait_ge(s.sOUT.h, s.sOUT.v)
                q.dma_start(s.out_d[row0:row0 + 128, :], s.h2buf[:]).then_inc(s.sOUT.h, 16)
                s.sOUT.v += 16
            s.qe(_)
            return

        self.emit_ln(s.xbuf, s.bfA)

        s.transpose_blocks(s.bfA, NKC, s.bfB, [kc * 128 for kc in range(NKC)],
                           evac="s")

        def _(v):
            v.wait_ge(s.sS.h, s.sS.v)
            a = None
            for kc in range(NKC):
                a = v.tensor_scalar(s.bfB[:, kc * 128:(kc + 1) * 128],
                                    s.bfB[:, kc * 128:(kc + 1) * 128],
                                    s.ca1c[:, kc * BPC + seq:kc * BPC + seq + 1],
                                    s.sac[:, kc * BPC + seq:kc * BPC + seq + 1],
                                    ALU.mult, ALU.add)
            s.sV.inc(a)
        s.ve(_)

        if s.debug_stop == "ln1":
            self._dbg_dump(s.bfB[:], row0)
            return

        def _(t):
            t.wait_ge(s.sV.h, s.sV.v)
            a = None
            for mc in range(8):
                for kc in range(NKC):
                    a = t.matmul(s.ps_A[:, mc * 128:(mc + 1) * 128],
                                 s.wq_T[:, (mc * 8 + kc) * 128:(mc * 8 + kc + 1) * 128],
                                 s.bfB[:, kc * 128:(kc + 1) * 128],
                                 start=(kc == 0 and mc % 4 == 0), stop=True,
                                 skip_group_check=True)
            for kc in range(NKC):
                a = t.matmul(s.ps_B[:, 0:128],
                             s.kv_T[:, kc * 128:(kc + 1) * 128],
                             s.bfB[:, kc * 128:(kc + 1) * 128],
                             start=(kc == 0), stop=True, skip_group_check=True)
            for kc in range(NKC):
                a = t.matmul(s.ps_B[:, 128:256],
                             s.kv2_T[:, kc * 128:(kc + 1) * 128],
                             s.bfB[:, kc * 128:(kc + 1) * 128],
                             start=False, stop=True, skip_group_check=True)
            s.sPE.inc(a)
        s.te(_)

        def _(e):
            e.wait_ge(s.sPE.h, s.sPE.v)
            e.copy(s.bfC[:], s.ps_A[:])
            e.copy(s.kk[0:64, slot * 128:(slot + 1) * 128], s.ps_B[0:64, 0:128])
            e.copy(s.kk[64:128, slot * 128:(slot + 1) * 128],
                   s.ps_B[64:128, 128:256])
            s.sS.inc(e.copy(s.bfA[64:128, 0:128], s.ps_B[64:128, 0:128]))
        s.se(_)

        def _(t):
            t.wait_ge(s.sS.h, s.sS.v)
            s.sPE.inc(t.transpose(s.ps_T[:, 0:64], s.bfA[64:128, 0:128],
                                  s.ident[64:128, 64:128]))
        s.te(_)

        def _(e):
            e.wait_ge(s.sPE.h, s.sPE.v)
            s.sS.inc(e.copy(s.v_all[:, slot * 65:slot * 65 + 64],
                            s.ps_T[:, 0:64]))
        s.se(_)

        if s.debug_stop == "qkv":
            self._dbg_dump(s.bfC[:], row0)
            return

        groups = []
        for kt in kts:
            mask = s.mnew if kt == qt else (s.mold if kt == qt - 2 else None)
            for grp in range(2):
                groups.append((kt, grp, mask))
        exp_w, msk_w = {}, {}

        def emit_scores(gi):
            kt, grp, mask = groups[gi]
            kslot = kt % 4
            ps = s.ps_A if gi % 2 == 0 else s.ps_B
            pbuf = s.probsT[:, 0:1024] if gi % 2 == 0 else s.probsT2[:, 0:1024]

            def _(t):
                if gi == 0:
                    t.wait_ge(s.sV.h, s.sV.v)
                    t.wait_ge(s.sS.h, s.sS.v)
                if gi >= 2:
                    t.wait_ge(s.sS.h, exp_w[gi - 2])
                a = None
                for h8 in range(8):
                    h = grp * 8 + h8
                    base = (h % 2) * 64
                    col = (h8 % 2) * 512 + (h8 // 2) * 128
                    a = t.matmul(ps[:, col:col + 128],
                                 s.kk[base:base + 64,
                                      kslot * 128:(kslot + 1) * 128],
                                 s.bfC[base:base + 64,
                                       (h // 2) * 128:(h // 2 + 1) * 128],
                                 start=(h8 in (0, 1)), stop=True,
                                 skip_group_check=True)
                s.sPE.inc(a)
            s.te(_)

            def _(sc):
                sc.wait_ge(s.sPE.h, s.sPE.v)
                s.sS.inc(sc.activation(pbuf, ps[:], AF.Exp,
                                       scale=0.125, bias=s.consts[:, 0:1]))
            s.se(_)
            exp_w[gi] = s.sS.v

            if mask is not None:
                def _(v):
                    v.wait_ge(s.sS.h, s.sS.v)
                    mb = mask[:].unsqueeze(1).broadcast_to([128, 8, 128])
                    pvw = pbuf.rearrange("p (h c) -> p h c", c=128)
                    s.sV.inc(v.tensor_tensor(pvw, pvw, mb, ALU.mult))
                s.ve(_)
                msk_w[gi] = s.sV.v

        def emit_pv(gi):
            kt, grp, mask = groups[gi]
            kslot = kt % 4
            pbuf = s.probsT[:, 0:1024] if gi % 2 == 0 else s.probsT2[:, 0:1024]

            def _(t):
                if mask is not None:
                    t.wait_ge(s.sV.h, msk_w[gi])
                else:
                    t.wait_ge(s.sS.h, exp_w[gi])
                a = None
                for h8 in range(8):
                    h = grp * 8 + h8
                    col = (h8 % 2) * 512 + (h8 // 2) * 128
                    a = t.matmul(s.ps_pv[:, pv_off(h):pv_off(h) + 65],
                                 pbuf[:, col:col + 128],
                                 s.v_all[:, kslot * 65:(kslot + 1) * 65],
                                 start=(gi in (0, 1) and h in (0, 7, 14)),
                                 stop=True, skip_group_check=True)
                s.sPE.inc(a)
            s.te(_)

        for gi in range(len(groups)):
            emit_scores(gi)
            if gi >= 1:
                emit_pv(gi - 1)
        emit_pv(len(groups) - 1)

        def _(v):
            v.wait_ge(s.sPE.h, s.sPE.v)
            for b_ in range(3):
                n = (7, 7, 2)[b_]
                src = s.ps_pv[:, b_ * 512 + 64: b_ * 512 + 64 + n * 65]
                v.tensor_copy(s.sums[:, b_ * 7:b_ * 7 + n].unsqueeze(2),
                              src.rearrange("p (h c) -> p h c", c=65)[:, :, 0:1])
            v.reciprocal(s.rsum[:], s.sums[:])
            a = None
            for b_ in range(3):
                n = (7, 7, 2)[b_]
                pvv = s.ps_pv[:, b_ * 512: b_ * 512 + n * 65].rearrange(
                    "p (h c) -> p h c", c=65)[:, :, 0:64]
                rb = s.rsum[:, b_ * 7:b_ * 7 + n].unsqueeze(2).broadcast_to([128, n, 64])
                dst = s.bfA[:, b_ * 7 * 64: b_ * 7 * 64 + n * 64].rearrange(
                    "p (h c) -> p h c", c=64)
                a = v.tensor_tensor(dst, pvv, rb, ALU.mult)
            s.sV.inc(a)
        s.ve(_)

        if s.debug_stop in ("smax", "smk"):
            def _(v):
                v.wait_ge(s.sS.h, s.sS.v)
                pass
            s.ve(_)
            self._dbg_dump(s.probsT[:].rearrange("p (a b) -> p (a b)") if False else s.probsT[:], row0)
            return

        if s.debug_stop == "pv":
            self._dbg_dump(s.bfA[:], row0)
            return

        s.transpose_blocks(s.bfA, NKC, s.bfB, [kc * 128 for kc in range(NKC)],
                           evac="s")

        def _(t):
            t.wait_ge(s.sV.h, s.sV.v)
            t.wait_ge(s.sS.h, s.sS.v)
            a = None
            for kc in range(NKC):
                for half in range(2):
                    a = t.matmul(s.ps_B[:, half * 512:(half + 1) * 512],
                                 s.bfB[:, kc * 128:(kc + 1) * 128],
                                 s.wf_T[:, kc * 1024 + half * 512:
                                        kc * 1024 + (half + 1) * 512],
                                 start=(kc == 0), stop=True, skip_group_check=True)
            for half in range(2):
                a = t.matmul(s.ps_B[:, half * 512:(half + 1) * 512],
                             s.onesb[:], s.bfrow[:, half * 512:(half + 1) * 512],
                             start=False, stop=True, skip_group_check=True)
            s.sPE.inc(a)
        s.te(_)

        def _(v):
            v.wait_ge(s.sPE.h, s.sPE.v)
            v.wait_ge(s.sOUT.h, s.sOUT.v)
            s.sV.inc(v.tensor_tensor(s.h2buf[:], s.ps_B[:], s.gab[:], ALU.mult))
        s.ve(_)
        r1v = s.sV.v

        def _(p):
            p.wait_ge(s.sV.h, r1v)
            s.sP.inc(p.tensor_tensor(s.x1buf[:], s.h2buf[:], s.xbuf[:], ALU.add))
        s.po(_)
        s.res1_cnt[ti] = s.sP.v
        res1_p = s.sP.v

        if s.debug_stop == "attn":
            def _(q):
                q.wait_ge(s.sV.h, s.sV.v)
                q.wait_ge(s.sP.h, s.sP.v)
                q.dma_start(s.out_d[row0:row0 + 128, :], s.x1buf[:]).then_inc(s.sOUT.h, 16)
                s.sOUT.v += 16
            s.qe(_)

            def _(v):
                v.wait_ge(s.sOUT.h, s.sOUT.v)
            s.ve(_)
            return

        def _(v):
            v.wait_ge(s.sP.h, res1_p)
            s.sV.inc(v.tensor_reduce(s.stat[:, 0:1], s.x1buf[:], AX.X, ALU.add))
        s.ve(_)

        def _(sc):
            sc.wait_ge(s.sP.h, res1_p)
            s.sS.inc(sc.activation(s.bfE[:], s.x1buf[:], AF.Square,
                                   accum_out=s.stat[:, 3:4]))
        s.se(_)

        def _(v):
            v.wait_ge(s.sS.h, s.sS.v)
            v.tensor_scalar(s.stat[:, 1:2], s.stat[:, 0:1], 1.0 / D, None, ALU.mult)
            v.tensor_scalar(s.stat[:, 2:3], s.stat[:, 1:2], s.stat[:, 1:2], -EPS,
                            ALU.mult, ALU.add)
            s.sV.inc(v.tensor_scalar(s.stat[:, 4:5], s.stat[:, 3:4], 1.0 / D,
                                     s.stat[:, 2:3], ALU.mult, ALU.subtract))
        s.ve(_)

        def _(p):
            p.wait_ge(s.sV.h, s.sV.v)
            s.sP.inc(p.tensor_tensor(s.stat[:, 5:6], s.stat[:, 4:5],
                                     s.consts[:, 1:2], ALU.pow))
        s.po(_)

        def _(v):
            v.wait_ge(s.sP.h, s.sP.v)
            v.wait_ge(s.sOUT.h, s.sOUT.v)
            v.scalar_tensor_tensor(s.h2buf[:], s.x1buf[:], s.stat[:, 1:2],
                                   s.cbb[:], ALU.subtract, ALU.mult)
            s.sV.inc(v.scalar_tensor_tensor(s.h2buf[:], s.h2buf[:], s.stat[:, 5:6],
                                            s.sbb[:], ALU.mult, ALU.add))
        s.ve(_)

        h2q_w = s.emit_quant2(s.h2buf, D, s.bfE, s.h2buf, s.bfD, s.bfE,
                              s.bmm, s.qr2, s.qsh, 1.0)

        s.transpose_blocks(s.bfE, NKC, s.bfC, [kc * 128 for kc in range(NKC)],
                           evac="s", wait_p=h2q_w)

        if s.debug_stop == "h2q":
            self._dbg_dump(s.bfC[:], row0)
            return

        pieces = PIECES
        if s.debug_stop == "gate1":
            pieces = PIECES[:1]
        nP = len(pieces)
        ps_free = {}
        quant_w = {}
        evac_w = {}
        gu_w = {}
        up_w = {}
        hT_w = {}

        def gu_ps(idx):
            return s.ps_A if idx % 2 == 0 else s.ps_pv

        def emit_gateup(idx):
            p0_, pn = pieces[idx]

            def _(t):
                if idx < 2:
                    t.wait_ge(s.sV.h, s.sV.v)
                    t.wait_ge(s.sS.h, s.sS.v)
                else:
                    t.wait_ge(s.sV.h, ps_free[idx - 2])
                a = None
                ps = gu_ps(idx)
                for kc in range(NKC):
                    a = t.matmul(ps[:, 0:pn],
                                 s.bfC[:, kc * 128:(kc + 1) * 128],
                                 s.wg_q[:, kc * II + p0_: kc * II + p0_ + pn],
                                 start=(kc == 0), stop=True, skip_group_check=True)
                s.sPE.inc(a)
            s.te(_)
            gu_w[idx] = s.sPE.v

            def _(t):
                a = None
                ps = gu_ps(idx)
                for kc in range(NKC):
                    a = t.matmul(ps[:, 512:512 + pn],
                                 s.bfC[:, kc * 128:(kc + 1) * 128],
                                 s.wu_q[:, kc * II + p0_: kc * II + p0_ + pn],
                                 start=(kc == 0), stop=True, skip_group_check=True)
                s.sPE.inc(a)
            s.te(_)
            up_w[idx] = s.sPE.v

        tanh_w = {}

        def emit_tanh(idx):
            p0_, pn = pieces[idx]

            def _(sc):
                sc.wait_ge(s.sPE.h, gu_w[idx])
                if idx >= 1:
                    sc.wait_ge(s.sP.h, quant_w[idx - 1])
                s.sS.inc(sc.activation(s.tf_a[:, 0:pn], gu_ps(idx)[:, 0:pn],
                                       AF.Tanh, scale=0.5))
            s.se(_)
            tanh_w[idx] = s.sS.v

        def emit_siluquant(idx):
            p0_, pn = pieces[idx]

            def _(v):
                v.wait_ge(s.sS.h, tanh_w[idx])
                v.wait_ge(s.sPE.h, up_w[idx])
                ps = gu_ps(idx)
                v.scalar_tensor_tensor(s.tf_b[:, 0:pn], s.tf_a[:, 0:pn], 1.0,
                                       ps[:, 0:pn], ALU.add, ALU.mult)
                s.sV.inc(v.tensor_tensor(s.tf_b[:, 0:pn], s.tf_b[:, 0:pn],
                                         ps[:, 512:512 + pn], ALU.mult))
            s.ve(_)
            ps_free[idx] = s.sV.v

            quant_w[idx] = s.emit_quant2(s.tf_b, pn, s.hmT, s.tf_a,
                                         s.bfE[:, 512:1024], s.bfE[:, 0:512],
                                         s.bmm, s.qr2, s.qsh, 1.0)

        def emit_hmidT(idx):
            p0_, pn = pieces[idx]
            nsub = pn // 128

            def _(t):
                t.wait_ge(s.sP.h, quant_w[idx])
                t.wait_ge(s.sS.h, s.sS.v)
                a = None
                for j in range(nsub):
                    a = t.transpose(s.ps_T[:, j * 128:(j + 1) * 128],
                                    s.hmT[:, j * 128:(j + 1) * 128],
                                    s.ident[:])
                s.sPE.inc(a)
            s.te(_)
            hT_w[idx] = s.sPE.v

            def _(e):
                e.wait_ge(s.sPE.h, hT_w[idx])
                a = None
                for j in range(nsub):
                    a = e.copy(s.hmq[:, j * 128:(j + 1) * 128],
                               s.ps_T[:, j * 128:(j + 1) * 128])
                s.sS.inc(a)
            s.se(_)
            evac_w[idx] = s.sS.v

        def emit_down(idx):
            p0_, pn = pieces[idx]

            def _(t):
                t.wait_ge(s.sS.h, evac_w[idx])
                a = None
                for j in range(pn // 128):
                    ic = p0_ // 128 + j
                    for half in range(2):
                        a = t.matmul(s.ps_B[:, half * 512:(half + 1) * 512],
                                     s.hmq[:, j * 128:(j + 1) * 128],
                                     s.wd_q[:, ic * 1024 + half * 512:
                                            ic * 1024 + (half + 1) * 512],
                                     start=(idx == 0 and j == 0), stop=True,
                                     skip_group_check=True)
                s.sPE.inc(a)
            s.te(_)

        for idx in range(nP + 1):
            if idx < nP:
                emit_gateup(idx)
                emit_tanh(idx)
            if idx >= 1:
                emit_hmidT(idx - 1)
            if idx < nP:
                emit_siluquant(idx)
            if idx >= 1:
                emit_down(idx - 1)

        def _(v):
            v.wait_ge(s.sPE.h, s.sPE.v)
            s.sV.inc(v.tensor_tensor(s.h2buf[:], s.ps_B[:], s.gbb[:], ALU.mult))
        s.ve(_)
        r2v = s.sV.v

        def _(p):
            p.wait_ge(s.sV.h, r2v)
            s.sP.inc(p.tensor_tensor(s.h2buf[:], s.h2buf[:], s.x1buf[:], ALU.add))
        s.po(_)

        def _(q):
            q.wait_ge(s.sP.h, s.sP.v)
            q.dma_start(s.out_d[row0:row0 + 128, :], s.h2buf[:]).then_inc(s.sOUT.h, 16)
            s.sOUT.v += 16
        s.qe(_)

    def seq_setup(self, seq):
        s = self
        with s.nc.Block() as blk:
            s.blk = blk
            p0rowb = s.bfA[0:1, :]
            for slot_, dst in ((0, s.cbb), (1, s.sbb), (2, s.gab), (3, s.gbb)):
                def _(q, slot_=slot_):
                    q.wait_ge(s.sV.h, s.sV.v)
                    q.wait_ge(s.sPE.h, s.sPE.v)
                    q.dma_start(p0rowb,
                                s.rows_d[seq:seq + 1, slot_ * D:(slot_ + 1) * D]
                                ).then_inc(s.sLD.h, 16)
                    s.sLD.v += 16
                s.qe(_)

                def _(t):
                    t.wait_ge(s.sLD.h, s.sLD.v)
                    t.matmul(s.ps_B[:, 0:512], s.onesb[:], p0rowb[:, 0:512],
                             start=True, stop=True, skip_group_check=True)
                    a = t.matmul(s.ps_B[:, 512:1024], s.onesb[:], p0rowb[:, 512:1024],
                                 start=True, stop=True, skip_group_check=True)
                    s.sPE.inc(a)
                s.te(_)

                def _(v, dst=dst):
                    v.wait_ge(s.sPE.h, s.sPE.v)
                    s.sV.inc(v.tensor_copy(dst[:], s.ps_B[:]))
                s.ve(_)

    def build(self):
        self.declare_io()
        with self.es:
            self.declare_persistent()
            self.phase0()
            self.phase1_mod()
            self.phase2_weights()
            self.declare_main_bufs()

            with self.nc.Block() as blk:
                self.blk = blk

                def _(v):
                    v.memset(
                        self.v_all[:].rearrange("p (s c) -> p s c", c=65)[:, :, 64:65],
                        1.0)
                    v.memset(self.stat[:], 0.0)
                self.ve(_)

                def _(q):
                    q.dma_start(self.xb[0][:],
                                self.x_in[0:128, :]).then_inc(self.sX.h, 16)
                    self.sX.v += 16
                self.qe(_)
                self.x_ld[0] = self.sX.v

            for seq in range(BPC):
                self.seq_setup(seq)
                with self.nc.Block() as blk:
                    self.blk = blk
                    for qt in range(NT):
                        self.emit_tile(seq, qt)

            with self.nc.Block() as blk:
                self.blk = blk

                def _(q):
                    q.wait_ge(self.sOUT.h, self.sOUT.v)
                self.qe(_)
        self.nc.compile()
        return self.nc


_NC_CACHE = {}


def _get_nc(debug_stop=None):
    if debug_stop not in _NC_CACHE:
        _NC_CACHE[debug_stop] = KB(debug_stop).build()
    return _NC_CACHE[debug_stop]


def kernel(L0_attn_x, L0_attn_segment_pos, L0_attn_Wq, L0_attn_Wk, L0_attn_Wv,
           L0_attn_Wf, L0_attn_bf, L0_ffn_w_gate, L0_ffn_w_up, L0_ffn_w_down,
           L0_c, L0_wmod, _debug_stop=None):
    nc = _get_nc(_debug_stop)

    assert np.asarray(L0_attn_segment_pos).shape[-1] == T
    x = np.ascontiguousarray(np.asarray(L0_attn_x, dtype=np.float32))
    ident = np.eye(128, dtype=ml_dtypes.bfloat16)
    jj, ii = np.meshgrid(np.arange(128), np.arange(128), indexing="ij")
    mnew = (jj <= ii).astype(ml_dtypes.bfloat16)
    mold = (jj >= ii).astype(ml_dtypes.bfloat16)
    shared = dict(
        wq=np.ascontiguousarray(np.asarray(L0_attn_Wq, dtype=np.float32)),
        wk=np.ascontiguousarray(np.asarray(L0_attn_Wk, dtype=np.float32)),
        wv=np.ascontiguousarray(np.asarray(L0_attn_Wv, dtype=np.float32)),
        wf=np.ascontiguousarray(np.asarray(L0_attn_Wf, dtype=np.float32)),
        bfv=np.ascontiguousarray(np.asarray(L0_attn_bf, dtype=np.float32)).reshape(1, D),
        wg=np.ascontiguousarray(np.asarray(L0_ffn_w_gate, dtype=np.float32)),
        wu=np.ascontiguousarray(np.asarray(L0_ffn_w_up, dtype=np.float32)),
        wd=np.ascontiguousarray(np.asarray(L0_ffn_w_down, dtype=np.float32)),
        wmod=np.ascontiguousarray(np.asarray(L0_wmod, dtype=np.float32)),
        ident=ident, mnew=np.ascontiguousarray(mnew), mold=np.ascontiguousarray(mold),
        identf=np.eye(BPC, dtype=np.float32),
    )
    c_full = np.asarray(L0_c, dtype=np.float32)
    in_maps = []
    for core in range(NCORES):
        m = dict(shared)
        m["x"] = np.ascontiguousarray(
            x[core * BPC:(core + 1) * BPC].reshape(BPC * T, D))
        m["c"] = np.ascontiguousarray(c_full[core * BPC:(core + 1) * BPC])
        in_maps.append(m)

    res = run_bass_kernel_spmd(nc, in_maps, core_ids=list(range(NCORES)))
    out = np.concatenate(
        [r["out"].reshape(BPC, T, D) for r in res.results], axis=0)
    return out.astype(np.float32)


# version 11
# speedup vs baseline: 1.0338x; 1.0338x over previous
import numpy as np
import ml_dtypes
from contextlib import ExitStack

import concourse.bacc as bacc
import concourse.mybir as mybir
from concourse.bass_utils import run_bass_kernel_spmd
from concourse import dve_ops
from concourse.dve_op import DveOp
from concourse.dve_spec import Spec, Src0, Src1, C0, C1, C2, maxx, minn
from concourse.dve_spec import lower as dve_lower
from concourse.dve_table_gen import dve_ver_for

F32 = mybir.dt.float32
BF16 = mybir.dt.bfloat16
AF = mybir.ActivationFunctionType
ALU = mybir.AluOpType
AX = mybir.AxisListType

B, T, D = 16, 2048, 1024
NCORES, BPC = 8, 2
NT = T // 128
NH, HD = 16, 64
II = 2816
NKC = D // 128
NIC = II // 128
EPS = 1e-5
PIECES = [(0, 512), (512, 512), (1024, 512), (1536, 512), (2048, 512), (2560, 256)]

C_RND1 = float(1.5 * 2.0**23)
C_RND2 = float(1.5 * 2.0**24)


def _rnd(x, g):
    return np.round(x / g) * g


def _ref_t1(in0, in1, s0, s1, imm2):
    return _rnd(np.clip(in0, -s0, s0), 1.0).astype(in0.dtype)


def _ref_t12(in0, in1, s0, s1, imm2):
    m4 = np.clip(in0, -s0, s0)
    return (_rnd(np.clip(in0 - m4, -s0, s0), 2.0) + in1).astype(in0.dtype)


def _ref_fin(in0, in1, s0, s1, imm2):
    t3 = ((in0 > s0).astype(in0.dtype) - (in0 < -s0).astype(in0.dtype)) * imm2
    return (t3 + in1).astype(in0.dtype)


_VER = dve_ver_for("TRN2")


def _mk_op(name, spec, ref):
    sha = dve_lower(spec, ver=_VER, perf_en=False, name=name).sha(_VER)
    return DveOp(name, spec, subdim=False, uops_sha={_VER: sha}, reference=ref)


_c4 = minn(maxx(Src0, C2), C0)
OP_T1 = _mk_op("ANT_FP4_T1", Spec(body=(_c4 + C1) - C1), _ref_t1)
_d2 = minn(maxx(Src0 - _c4, C2), C0)
OP_T12 = _mk_op("ANT_FP4_T12", Spec(body=((_d2 + C1) - C1) + Src1), _ref_t12)
_g3 = (Src0 > C0) - (Src0 < C1)
OP_FIN = _mk_op("ANT_FP4_FIN", Spec(body=_g3 * C2 + Src1), _ref_fin)


def _register_ops():
    victims = ["DVEHASH32L_SEED", "DVEHASH32L_BODY", "DVEHASH32L_DRAIN"]
    for op, victim in zip((OP_T1, OP_T12, OP_FIN), victims):
        if op.name in dve_ops._SUB_OPCODE_FOR_NAME:
            continue
        row = dve_ops._SUB_OPCODE_FOR_NAME.pop(victim)
        idx = next(i for i, o in enumerate(dve_ops.OPS) if o.name == victim)
        dve_ops.OPS[idx] = op
        dve_ops._SUB_OPCODE_FOR_NAME[op.name] = row
        dve_ops.CUSTOM_DVE_SPECS.pop(victim, None)
        dve_ops.CUSTOM_DVE_REFS.pop(victim, None)
        dve_ops.CUSTOM_DVE_SPECS[op.name] = op.spec
        dve_ops.CUSTOM_DVE_REFS[op.name] = op.ref


_register_ops()


_OUT_KW = ("out", "accum_out")


def _ap_rng(x):
    t = getattr(x, "tensor", None)
    name = getattr(t, "name", None)
    if name is None:
        return None
    try:
        off = x.offset
        reach = 1
        for step, count in list(x.ap)[1:]:
            reach += (count - 1) * step
        return (name, off, off + reach)
    except Exception:
        return (name, 0, 1 << 60)


def _overlaps(a, bs):
    for b in bs:
        if a[0] == b[0] and a[1] < b[2] and b[1] < a[2]:
            return True
    return False


class DrainWrap:

    _OPS = frozenset((
        "tensor_tensor", "tensor_scalar", "tensor_copy", "tensor_reduce",
        "reduce_max", "reduce_sum", "reciprocal", "memset", "_custom_dve",
        "scalar_tensor_tensor", "activation", "select", "copy",
    ))

    def __init__(self, eng, state):
        object.__setattr__(self, "_e", eng)
        object.__setattr__(self, "_st", state)

    def __getattr__(self, name):
        a = getattr(self._e, name)
        if name not in self._OPS:
            return a

        def f(*args, **kw):
            st = self._st
            rngs = [r for r in (_ap_rng(x) for x in args) if r] +                    [r for r in (_ap_rng(v) for v in kw.values()) if r]
            if name == "_custom_dve":
                outs = [_ap_rng(kw[k]) for k in _OUT_KW if kw.get(k) is not None]
            else:
                outs = [_ap_rng(args[0])]
                if kw.get("accum_out") is not None:
                    outs.append(_ap_rng(kw["accum_out"]))
            outs = [o for o in outs if o]
            ins = [r for r in rngs if r not in outs]
            if (any(_overlaps(o, st["w"]) or _overlaps(o, st["r"]) for o in outs)
                    or any(_overlaps(i, st["w"]) for i in ins)):
                self._e.drain()
                st["w"].clear()
                st["r"].clear()
            st["w"] += outs
            st["r"] += ins
            return a(*args, **kw)
        return f


class Ctr:
    def __init__(self, handle, step=1):
        self.h, self.v, self.step = handle, 0, step

    def inc(self, inst):
        inst.then_inc(self.h, self.step)
        self.v += self.step


def pv_off(h):
    b = h // 7
    return b * 512 + (h - 7 * b) * 65


class KB:

    def __init__(self, debug_stop=None):
        self.debug_stop = debug_stop
        self.nc = bacc.Bacc("TRN2", target_bir_lowering=False)
        self.es = ExitStack()
        self.blk = None
        self._haz = {e: {"w": [], "r": []} for e in ("v", "s", "p")}

    def te(self, f):
        self.blk.tensor(f)

    def ve(self, f):
        self.blk.vector(lambda e: f(DrainWrap(e, self._haz["v"])))

    def se(self, f):
        self.blk.scalar(lambda e: f(DrainWrap(e, self._haz["s"])))

    def po(self, f):
        self.blk.gpsimd(lambda e: f(DrainWrap(e, self._haz["p"])))

    def qe(self, f):
        self.blk.sync(f)

    def sbuf(self, name, shape, dt=F32, stack=None):
        return (stack or self.es).enter_context(self.nc.sbuf_tensor(name, shape, dt))

    def psum(self, name, shape, dt=F32):
        return self.es.enter_context(self.nc.psum_tensor(name, shape, dt))

    def sem(self, name, step=1):
        return Ctr(self.es.enter_context(self.nc.semaphore(name)), step)

    def declare_io(self):
        nc = self.nc
        self.x_in = nc.dram_tensor("x", [BPC * T, D], F32, kind="ExternalInput").ap()
        self.wq_in = nc.dram_tensor("wq", [D, D], F32, kind="ExternalInput").ap()
        self.wk_in = nc.dram_tensor("wk", [HD, D], F32, kind="ExternalInput").ap()
        self.wv_in = nc.dram_tensor("wv", [HD, D], F32, kind="ExternalInput").ap()
        self.wf_in = nc.dram_tensor("wf", [D, D], F32, kind="ExternalInput").ap()
        self.bf_in = nc.dram_tensor("bfv", [1, D], F32, kind="ExternalInput").ap()
        self.wg_in = nc.dram_tensor("wg", [II, D], F32, kind="ExternalInput").ap()
        self.wu_in = nc.dram_tensor("wu", [II, D], F32, kind="ExternalInput").ap()
        self.wd_in = nc.dram_tensor("wd", [D, II], F32, kind="ExternalInput").ap()
        self.c_in = nc.dram_tensor("c", [BPC, D], F32, kind="ExternalInput").ap()
        self.wmod_in = nc.dram_tensor("wmod", [D, 6 * D], F32, kind="ExternalInput").ap()
        self.ident_in = nc.dram_tensor("ident", [128, 128], BF16, kind="ExternalInput").ap()
        self.mnew_in = nc.dram_tensor("mnew", [128, 128], BF16, kind="ExternalInput").ap()
        self.mold_in = nc.dram_tensor("mold", [128, 128], BF16, kind="ExternalInput").ap()
        self.identf_in = nc.dram_tensor("identf", [BPC, BPC], F32, kind="ExternalInput").ap()
        self.out_d = nc.dram_tensor("out", [BPC * T, D], F32, kind="ExternalOutput").ap()

    def declare_persistent(self):
        sbuf, psum, sem = self.sbuf, self.psum, self.sem
        self.wq_T = sbuf("wq_T", [128, 64 * 128], BF16)
        self.kv_T = sbuf("kv_T", [128, 8 * 128], BF16)
        self.kv2_T = sbuf("kv2_T", [128, 8 * 128], BF16)
        self.wf_T = sbuf("wf_T", [128, 8 * 1024], BF16)
        self.wg_q = sbuf("wg_q", [128, 8 * II], BF16)
        self.wu_q = sbuf("wu_q", [128, 8 * II], BF16)
        self.wd_q = sbuf("wd_q", [128, NIC * 1024], BF16)
        self.ident = sbuf("ident_s", [128, 128], BF16)
        self.mnew = sbuf("mnew_s", [128, 128], BF16)
        self.mold = sbuf("mold_s", [128, 128], BF16)
        self.onesb = self.mnew[0:1, :]
        self.identf = sbuf("identf_s", [BPC, BPC], F32)
        self.bfrow = sbuf("bfrow", [1, D], BF16)
        self.consts = sbuf("consts", [128, 4], F32)
        self.ca1c = sbuf("ca1c", [128, NKC * BPC], F32)
        self.sac = sbuf("sac", [128, NKC * BPC], F32)
        self.rows_d = self.nc.dram_tensor("rows_d", [BPC, 4 * D], BF16).ap()
        self.cbb = sbuf("cbb", [128, D], BF16)
        self.sbb = sbuf("sbb", [128, D], BF16)
        self.gab = sbuf("gab", [128, D], BF16)
        self.gbb = sbuf("gbb", [128, D], BF16)
        self.ps_A = psum("ps_A", [128, 1024], F32)
        self.ps_B = psum("ps_B", [128, 1024], F32)
        self.ps_pv = psum("ps_pv", [128, 1280], F32)
        self.ps_T = psum("ps_T", [128, 512], BF16)
        self.sLD = sem("sLD", 16)
        self.sL2 = sem("sL2", 16)
        self.sL3 = sem("sL3", 16)
        self.sL4 = sem("sL4", 16)
        self.sX = sem("sX", 16)
        self.sX2 = sem("sX2", 16)
        self.sOUT = sem("sOUT", 16)
        self.sPE = sem("sPE")
        self.sV = sem("sV")
        self.sS = sem("sS")
        self.sP = sem("sP")

    def declare_main_bufs(self):
        sbuf = self.sbuf
        self.xbuf = sbuf("xbuf", [128, D], F32)
        self.xbuf2 = sbuf("xbuf2", [128, D], F32)
        self.xb = [self.xbuf, self.xbuf2]
        self.x_ld = [0, 0]
        self.res1_cnt = {}
        self.x1buf = sbuf("x1buf", [128, D], F32)
        self.h2buf = sbuf("h2buf", [128, D], F32)
        self.bfA = sbuf("bfA", [128, D], BF16)
        self.bfB = sbuf("bfB", [128, D], BF16)
        self.bfC = sbuf("bfC", [128, D], BF16)
        self.bfD = sbuf("bfD", [128, D], BF16)
        self.bfE = sbuf("bfE", [128, D], BF16)
        self.probsT = None
        self.kk = sbuf("kk", [128, 4 * 128], BF16)
        self.v_all = sbuf("v_all", [128, 4 * 65], BF16)
        self.stat = sbuf("stat", [128, 8], F32)
        self.bmm = sbuf("bmm", [128, 64], F32)
        self.qr2 = sbuf("qr2", [128, 64], F32)
        self.qsh = sbuf("qsh", [128, 64], BF16)
        self.sums = self.bmm[:, 48:64]
        self.rsum = self.qr2[:, 48:64]
        self.probsT2 = self.bfD
        self.probsT = self.bfB
        self.tf_a = self.h2buf[:, 512:1024]
        self.tf_b = self.h2buf[:, 0:512]
        self.hmT = self.bfD[:, 0:512]
        self.hmq = self.bfD[:, 512:1024]

    def phase0(self):
        s = self
        with s.nc.Block() as blk:
            s.blk = blk

            def _(q):
                q.dma_start(s.ident[:], s.ident_in).then_inc(s.sLD.h, 16)
                q.dma_start(s.mnew[:], s.mnew_in).then_inc(s.sLD.h, 16)
                q.dma_start(s.mold[:], s.mold_in).then_inc(s.sLD.h, 16)
                q.dma_start(s.identf[:], s.identf_in).then_inc(s.sLD.h, 16)
                s.sLD.v += 64
            s.qe(_)

            def _(sc):
                sc.preload_act_table(AF.Exp)
            s.se(_)

            def _(v):
                v.memset(s.consts[:, 0:1], -30.0)
                v.memset(s.consts[:, 1:2], -0.5)
            s.ve(_)

    def phase1_mod(self):
        s = self
        with ExitStack() as ph:
            c_sb = s.sbuf("c_sb", [BPC, D], F32, ph)
            sct = s.sbuf("sct", [BPC, D], F32, ph)
            sc_bf = s.sbuf("sc_bf", [BPC, D], BF16, ph)
            rowstg = s.sbuf("rowstg", [BPC, D], BF16, ph)
            scT = s.sbuf("scT", [128, NKC * BPC], F32, ph)
            wmod_f = s.sbuf("wmod_f", [128, 1024], F32, ph)
            wmod_g = s.sbuf("wmod_g", [128, 1024], F32, ph)
            wmod_h = s.sbuf("wmod_h", [128, 1024], F32, ph)
            wmod_i = s.sbuf("wmod_i", [128, 1024], F32, ph)
            tmpr = wmod_f[0:BPC, :]

            with s.nc.Block() as blk:
                s.blk = blk

                def _(q):
                    q.dma_start(c_sb[:], s.c_in).then_inc(s.sLD.h, 16)
                    q.dma_start(sct[0:1, :], s.bf_in).then_inc(s.sLD.h, 16)
                    s.sLD.v += 32
                s.qe(_)

                def _(v):
                    v.wait_ge(s.sLD.h, s.sLD.v)
                    s.sV.inc(v.tensor_copy(s.bfrow[:], sct[0:1, :]))
                s.ve(_)

                def _(sc):
                    sc.wait_ge(s.sV.h, s.sV.v)
                    s.sS.inc(sc.activation(sct[:], c_sb[:], AF.Tanh, scale=0.5))
                s.se(_)

                def _(v):
                    v.wait_ge(s.sS.h, s.sS.v)
                    v.tensor_scalar(sct[:], sct[:], 0.5, 0.5, ALU.mult, ALU.add)
                    s.sV.inc(v.tensor_tensor(sct[:], sct[:], c_sb[:], ALU.mult))
                s.ve(_)

                def _(t):
                    t.wait_ge(s.sV.h, s.sV.v)
                    a = None
                    for kc in range(NKC):
                        a = t.transpose(s.ps_A[:, kc * BPC:(kc + 1) * BPC],
                                        sct[:, kc * 128:(kc + 1) * 128],
                                        s.identf[:])
                    s.sPE.inc(a)
                s.te(_)

                def _(v):
                    v.wait_ge(s.sPE.h, s.sPE.v)
                    s.sV.inc(v.tensor_copy(scT[:], s.ps_A[:, 0:NKC * BPC]))
                s.ve(_)

                WNB = 4
                wbufs = [wmod_f, wmod_g, wmod_h, wmod_i]
                wlds = [s.sLD, s.sL2, s.sL3, s.sL4]
                wqes = [s.qe, s.po, s.qe, s.po]
                wm_ld = [0] * WNB
                wm_pe = [0] * WNB
                wi = 0
                for pair in range(3):
                    for sub in range(2):
                        vec = pair * 2 + sub
                        ps = s.ps_A if sub == 0 else s.ps_B
                        for kc in range(NKC):
                            buf, ld = wbufs[wi % WNB], wlds[wi % WNB]
                            sv_now, ss_now = s.sV.v, s.sS.v

                            def _(q, vec=vec, kc=kc, wi=wi, ld=ld, buf=buf,
                                  sv_now=sv_now, ss_now=ss_now):
                                q.wait_ge(s.sPE.h, wm_pe[wi % WNB])
                                q.wait_ge(s.sV.h, sv_now)
                                q.wait_ge(s.sS.h, ss_now)
                                q.dma_start(buf[:],
                                            s.wmod_in[kc * 128:(kc + 1) * 128,
                                                      vec * D:(vec + 1) * D]
                                            ).then_inc(ld.h, 16)
                                ld.v += 16
                            wqes[wi % WNB](_)
                            wm_ld[wi % WNB] = ld.v

                            def _(t, ps=ps, kc=kc, buf=buf, wi=wi, ld=ld,
                                  sv_now=sv_now):
                                t.wait_ge(ld.h, wm_ld[wi % WNB])
                                t.wait_ge(s.sV.h, sv_now)
                                a = None
                                for piece in range(2):
                                    a = t.matmul(ps[0:BPC, piece * 512:(piece + 1) * 512],
                                                 scT[:, kc * BPC:(kc + 1) * BPC],
                                                 buf[:, piece * 512:(piece + 1) * 512],
                                                 start=(kc == 0), stop=(kc == NKC - 1),
                                                 skip_group_check=True)
                                s.sPE.inc(a)
                            s.te(_)
                            wm_pe[wi % WNB] = s.sPE.v
                            wi += 1

                    def _(v):
                        v.wait_ge(s.sPE.h, s.sPE.v)
                        v.tensor_copy(c_sb[:], s.ps_A[0:BPC, :])
                        s.sV.inc(v.tensor_copy(sct[:], s.ps_B[0:BPC, :]))
                    s.ve(_)

                    if pair == 0:
                        for rowbuf, dst, add1 in ((c_sb, s.sac, 0.0), (sct, s.ca1c, 1.0)):
                            for kc in range(NKC):
                                def _(t, rowbuf=rowbuf, kc=kc):
                                    t.wait_ge(s.sV.h, s.sV.v)
                                    s.sPE.inc(t.transpose(
                                        s.ps_B[:, 512:512 + BPC],
                                        rowbuf[:, kc * 128:(kc + 1) * 128],
                                        s.identf[:]))
                                s.te(_)

                                def _(v, dst=dst, add1=add1, kc=kc):
                                    v.wait_ge(s.sPE.h, s.sPE.v)
                                    a = None
                                    for sq in range(BPC):
                                        a = v.tensor_scalar(
                                            dst[:, kc * BPC + sq: kc * BPC + sq + 1],
                                            s.ps_B[:, 512 + sq:512 + sq + 1],
                                            add1, None, ALU.add)
                                    s.sV.inc(a)
                                s.ve(_)
                    elif pair == 1:
                        def _(sc):
                            sc.wait_ge(s.sV.h, s.sV.v)
                            s.sS.inc(sc.activation(tmpr, c_sb[:], AF.Sign))
                        s.se(_)

                        def _(v):
                            v.wait_ge(s.sS.h, s.sS.v)
                            v.tensor_tensor(c_sb[:], c_sb[:], tmpr, ALU.mult)
                            v.tensor_scalar(c_sb[:], c_sb[:], 0.25, None, ALU.max)
                            v.tensor_tensor(sc_bf[:], c_sb[:], tmpr, ALU.mult)
                            s.sV.inc(v.tensor_copy(rowstg[:], sct[:]))
                        s.ve(_)

                        def _(q):
                            q.wait_ge(s.sV.h, s.sV.v)
                            q.dma_start(s.rows_d[:, 2 * D:3 * D], sc_bf[:]).then_inc(s.sX.h, 16)
                            q.dma_start(s.rows_d[:, 1 * D:2 * D], rowstg[:]).then_inc(s.sX.h, 16)
                            s.sX.v += 32
                            q.wait_ge(s.sX.h, s.sX.v)
                        s.qe(_)
                    else:
                        def _(sc):
                            sc.wait_ge(s.sV.h, s.sV.v)
                            s.sS.inc(sc.activation(tmpr, sct[:], AF.Sign))
                        s.se(_)

                        def _(v):
                            v.wait_ge(s.sS.h, s.sS.v)
                            v.wait_ge(s.sX.h, s.sX.v)
                            v.tensor_scalar(sc_bf[:], c_sb[:], 1.0, None, ALU.add)
                            v.tensor_tensor(sct[:], sct[:], tmpr, ALU.mult)
                            v.tensor_scalar(sct[:], sct[:], 0.25, None, ALU.max)
                            s.sV.inc(v.tensor_tensor(rowstg[:], sct[:], tmpr, ALU.mult))
                        s.ve(_)

                        def _(q):
                            q.wait_ge(s.sV.h, s.sV.v)
                            q.dma_start(s.rows_d[:, 0 * D:1 * D], sc_bf[:]).then_inc(s.sX.h, 16)
                            q.dma_start(s.rows_d[:, 3 * D:4 * D], rowstg[:]).then_inc(s.sX.h, 16)
                            s.sX.v += 32
                            q.wait_ge(s.sX.h, s.sX.v)
                        s.qe(_)

    def emit_quant(self, v, src_f32, F, out_bf, w_scr, t1_bf, t12_bf,
                   bm, r2, sh, fold=1.0):
        nb = F // 16
        src = src_f32[:, 0:F]
        v.tensor_reduce(bm[:, 0:nb], src.rearrange("p (n k) -> p n k", k=16),
                        AX.X, ALU.max, apply_absolute_value=True)
        v.tensor_scalar(bm[:, 0:nb], bm[:, 0:nb], 1.0 / 12.0, 5e-13, ALU.mult, ALU.max)
        v.tensor_scalar(sh[:, 0:nb], bm[:, 0:nb], fold, None, ALU.mult)
        v.reciprocal(r2[:, 0:nb], bm[:, 0:nb])
        rb = r2[:, 0:nb].unsqueeze(2).broadcast_to([128, nb, 16])
        v.tensor_tensor(w_scr[:, 0:F].rearrange("p (n k) -> p n k", k=16),
                        src.rearrange("p (n k) -> p n k", k=16), rb, ALU.mult)
        v._custom_dve(OP_T1, out=t1_bf[:, 0:F], in0=w_scr[:, 0:F],
                      s0=4.0, s1=C_RND1, imm2=-4.0)
        v._custom_dve(OP_T12, out=t12_bf[:, 0:F], in0=w_scr[:, 0:F],
                      in1=t1_bf[:, 0:F], s0=4.0, s1=C_RND2, imm2=-4.0)
        v._custom_dve(OP_FIN, out=t1_bf[:, 0:F], in0=w_scr[:, 0:F],
                      in1=t12_bf[:, 0:F], s0=10.0, s1=-10.0, imm2=4.0)
        shb = sh[:, 0:nb].unsqueeze(2).broadcast_to([128, nb, 16])
        return v.tensor_tensor(out_bf[:, 0:F].rearrange("p (n k) -> p n k", k=16),
                               t1_bf[:, 0:F].rearrange("p (n k) -> p n k", k=16),
                               shb, ALU.mult)

    def emit_quant2(self, src_f32, F, out_bf, w_scr, t1_bf, t12_bf,
                    bm, r2, sh, fold=1.0, pre=None):
        s = self
        nb = F // 16

        def _(v):
            if pre is not None:
                pre(v)
            srcv = src_f32[:, 0:F]
            v.tensor_reduce(bm[:, 0:nb], srcv.rearrange("p (n k) -> p n k", k=16),
                            AX.X, ALU.max, apply_absolute_value=True)
            v.tensor_scalar(bm[:, 0:nb], bm[:, 0:nb], 1.0 / 12.0, 5e-13,
                            ALU.mult, ALU.max)
            v.tensor_scalar(sh[:, 0:nb], bm[:, 0:nb], fold, None, ALU.mult)
            s.sV.inc(v.reciprocal(r2[:, 0:nb], bm[:, 0:nb]))
        s.ve(_)

        pe_now = s.sPE.v

        def _(p):
            p.wait_ge(s.sV.h, s.sV.v)
            p.wait_ge(s.sPE.h, pe_now)
            rb = r2[:, 0:nb].unsqueeze(2).broadcast_to([128, nb, 16])
            s.sP.inc(p.tensor_tensor(
                w_scr[:, 0:F].rearrange("p (n k) -> p n k", k=16),
                src_f32[:, 0:F].rearrange("p (n k) -> p n k", k=16),
                rb, ALU.mult))
        s.po(_)

        def _(v):
            v.wait_ge(s.sP.h, s.sP.v)
            v._custom_dve(OP_T1, out=t1_bf[:, 0:F], in0=w_scr[:, 0:F],
                          s0=4.0, s1=C_RND1, imm2=-4.0)
            v._custom_dve(OP_T12, out=t12_bf[:, 0:F], in0=w_scr[:, 0:F],
                          in1=t1_bf[:, 0:F], s0=4.0, s1=C_RND2, imm2=-4.0)
            s.sV.inc(v._custom_dve(OP_FIN, out=t1_bf[:, 0:F], in0=w_scr[:, 0:F],
                                   in1=t12_bf[:, 0:F], s0=10.0, s1=-10.0, imm2=4.0))
        s.ve(_)

        def _(p):
            p.wait_ge(s.sV.h, s.sV.v)
            p.wait_ge(s.sPE.h, pe_now)
            shb = sh[:, 0:nb].unsqueeze(2).broadcast_to([128, nb, 16])
            s.sP.inc(p.tensor_tensor(
                out_bf[:, 0:F].rearrange("p (n k) -> p n k", k=16),
                t1_bf[:, 0:F].rearrange("p (n k) -> p n k", k=16),
                shb, ALU.mult))
        s.po(_)
        return s.sP.v

    def transpose_blocks(self, src_bf, n_blocks, dst, dst_offs, evac="v",
                         wait_p=None):
        s = self
        for g0 in range(0, n_blocks, 4):
            gn = min(4, n_blocks - g0)

            def _(t, g0=g0, gn=gn):
                t.wait_ge(s.sV.h, s.sV.v)
                if wait_p is not None:
                    t.wait_ge(s.sP.h, wait_p)
                if evac == "s":
                    t.wait_ge(s.sS.h, s.sS.v)
                a = None
                for j in range(gn):
                    a = t.transpose(s.ps_T[:, j * 128:(j + 1) * 128],
                                    src_bf[:, (g0 + j) * 128:(g0 + j + 1) * 128],
                                    s.ident[:])
                s.sPE.inc(a)
            s.te(_)

            def _(e, g0=g0, gn=gn):
                e.wait_ge(s.sPE.h, s.sPE.v)
                cp = e.tensor_copy if evac == "v" else e.copy
                a = None
                for j in range(gn):
                    a = cp(dst[:, dst_offs[g0 + j]:dst_offs[g0 + j] + 128],
                           s.ps_T[:, j * 128:(j + 1) * 128])
                (s.sV if evac == "v" else s.sS).inc(a)
            (s.ve if evac == "v" else s.se)(_)

    def phase2_weights(self):
        s = self
        with ExitStack() as ph:
            WHALF = II // 2
            stg = s.sbuf("stg", [128, WHALF], F32, ph)
            stg2 = s.sbuf("stg2", [128, WHALF], F32, ph)
            stg3 = s.sbuf("stg3", [128, WHALF], F32, ph)
            stq = s.sbuf("stq", [128, WHALF], BF16, ph)
            qt1 = s.sbuf("qt1", [128, WHALF], BF16, ph)
            qt12 = s.sbuf("qt12", [128, WHALF], BF16, ph)
            bmw = s.sbuf("bmw", [128, WHALF // 16], F32, ph)
            qr2w = s.sbuf("qr2w", [128, WHALF // 16], F32, ph)
            qshw = s.sbuf("qshw", [128, WHALF // 16], BF16, ph)

            with s.nc.Block() as blk:
                s.blk = blk
                NB = 3
                bufs = [stg, stg2, stg3]
                lds = [s.sLD, s.sL2, s.sL3]
                qes = [s.qe, s.po, s.qe]
                st_state = {"i": 0, "ld": [0] * NB, "c": [None] * NB}

                def load_rows(src_ap, ncols):
                    i = st_state["i"]
                    buf, ld = bufs[i % NB], lds[i % NB]
                    gate = st_state["c"][i % NB]
                    if gate is None:
                        gate = s.sV.v

                    def _(q, ld=ld, gate=gate):
                        q.wait_ge(s.sV.h, gate)
                        q.dma_start(buf[:, 0:ncols], src_ap).then_inc(ld.h, 16)
                        ld.v += 16
                    qes[i % NB](_)
                    st_state["ld"][i % NB] = ld.v
                    st_state["i"] += 1
                    return buf

                def cast_stage(buf, ncols):
                    i = st_state["i"] - 1
                    ld = lds[i % NB]
                    thr = st_state["ld"][i % NB]
                    pe_now = s.sPE.v

                    def _(v, ld=ld, thr=thr, pe_now=pe_now):
                        v.wait_ge(ld.h, thr)
                        v.wait_ge(s.sPE.h, pe_now)
                        s.sV.inc(v.tensor_copy(stq[:, 0:ncols], buf[:, 0:ncols]))
                    s.ve(_)
                    st_state["c"][i % NB] = s.sV.v

                def quant_stage(buf, ncols, fold):
                    i = st_state["i"] - 1
                    ld = lds[i % NB]
                    thr = st_state["ld"][i % NB]
                    sp_now = s.sP.v

                    def _pre(v, ld=ld, thr=thr, sp_now=sp_now):
                        v.wait_ge(ld.h, thr)
                        v.wait_ge(s.sP.h, sp_now)
                    qw = s.emit_quant2(buf, ncols, stq, buf, qt1, qt12,
                                       bmw, qr2w, qshw, fold, pre=_pre)
                    st_state["c"][i % NB] = s.sV.v
                    return qw

                for oc in range(NKC):
                    b = load_rows(s.wq_in[oc * 128:(oc + 1) * 128, :], D)
                    cast_stage(b, D)
                    s.transpose_blocks(stq, NKC, s.wq_T,
                                       [(oc * 8 + kc) * 128 for kc in range(NKC)],
                                       evac="s")

                kvi = st_state["i"]
                kvbuf, kvld = bufs[kvi % NB], lds[kvi % NB]
                kvgate = st_state["c"][kvi % NB]
                if kvgate is None:
                    kvgate = s.sV.v

                def _(q, kvld=kvld, kvgate=kvgate):
                    q.wait_ge(s.sV.h, kvgate)
                    q.dma_start(kvbuf[0:HD, 0:D], s.wk_in).then_inc(kvld.h, 16)
                    q.dma_start(kvbuf[64:128, 0:D], s.wv_in).then_inc(kvld.h, 16)
                    kvld.v += 32
                qes[kvi % NB](_)
                st_state["ld"][kvi % NB] = kvld.v
                st_state["i"] += 1
                cast_stage(kvbuf, D)
                for kc in range(NKC):
                    def _(t, kc=kc):
                        t.wait_ge(s.sV.h, s.sV.v)
                        t.wait_ge(s.sS.h, s.sS.v)
                        s.sPE.inc(t.transpose(s.ps_T[:, 0:128],
                                              stq[:, kc * 128:(kc + 1) * 128],
                                              s.ident[:]))
                    s.te(_)

                    def _(v, kc=kc):
                        v.wait_ge(s.sPE.h, s.sPE.v)
                        v.tensor_copy(s.kv_T[:, kc * 128:(kc + 1) * 128], s.ps_T[:, 0:128])
                        v.tensor_copy(s.kv2_T[:, kc * 128:kc * 128 + 64],
                                      s.ps_T[:, 64:128])
                        s.sV.inc(v.tensor_copy(s.kv2_T[:, kc * 128 + 64:(kc + 1) * 128],
                                               s.ps_T[:, 0:64]))
                    s.ve(_)

                for oc in range(NKC):
                    b = load_rows(s.wf_in[oc * 128:(oc + 1) * 128, :], D)
                    cast_stage(b, D)
                    s.transpose_blocks(stq, NKC, s.wf_T,
                                       [kc * 1024 + oc * 128 for kc in range(NKC)],
                                       evac="s")

                for w_src, w_dst, fold in ((s.wg_in, s.wg_q, 1.0), (s.wu_in, s.wu_q, 0.5)):
                    for ic in range(NIC):
                        b = load_rows(w_src[ic * 128:(ic + 1) * 128, :], D)
                        qw = quant_stage(b, D, fold)
                        s.transpose_blocks(stq, NKC, w_dst,
                                           [kc * II + ic * 128 for kc in range(NKC)],
                                           evac="s", wait_p=qw)

                for oc in range(NKC):
                    for hf in range(2):
                        b = load_rows(s.wd_in[oc * 128:(oc + 1) * 128,
                                              hf * WHALF:(hf + 1) * WHALF], WHALF)
                        qw = quant_stage(b, WHALF, 1.0)
                        s.transpose_blocks(stq, WHALF // 128, s.wd_q,
                                           [(hf * 11 + ic) * 1024 + oc * 128
                                            for ic in range(WHALF // 128)],
                                           evac="s", wait_p=qw)

    def emit_ln(self, src, dst, wait_out=False):
        s = self

        def _(v):
            s.sV.inc(v.tensor_reduce(s.stat[:, 0:1], src[:], AX.X, ALU.add))
        s.ve(_)

        def _(sc):
            sc.wait_ge(s.sV.h, s.sV.v)
            s.sS.inc(sc.activation(s.bfE[:], src[:], AF.Square,
                                   accum_out=s.stat[:, 3:4]))
        s.se(_)

        def _(v):
            v.wait_ge(s.sS.h, s.sS.v)
            v.tensor_scalar(s.stat[:, 1:2], s.stat[:, 0:1], 1.0 / D, None, ALU.mult)
            v.tensor_scalar(s.stat[:, 2:3], s.stat[:, 1:2], s.stat[:, 1:2], -EPS,
                            ALU.mult, ALU.add)
            s.sV.inc(v.tensor_scalar(s.stat[:, 4:5], s.stat[:, 3:4], 1.0 / D,
                                     s.stat[:, 2:3], ALU.mult, ALU.subtract))
        s.ve(_)

        def _(p):
            p.wait_ge(s.sV.h, s.sV.v)
            s.sP.inc(p.tensor_tensor(s.stat[:, 5:6], s.stat[:, 4:5],
                                     s.consts[:, 1:2], ALU.pow))
        s.po(_)

        def _(v):
            v.wait_ge(s.sP.h, s.sP.v)
            if wait_out:
                v.wait_ge(s.sOUT.h, s.sOUT.v)
            s.sV.inc(v.tensor_scalar(dst[:], src[:], s.stat[:, 1:2], s.stat[:, 5:6],
                                     ALU.subtract, ALU.mult))
        s.ve(_)


    def _dbg_dump(self, src_ap, row0):
        s = self

        def _(v):
            s.sV.inc(v.tensor_copy(s.h2buf[:], src_ap))
        s.ve(_)

        def _(q):
            q.wait_ge(s.sV.h, s.sV.v)
            q.wait_ge(s.sOUT.h, s.sOUT.v)
            q.dma_start(s.out_d[row0:row0 + 128, :], s.h2buf[:]).then_inc(s.sOUT.h, 16)
            s.sOUT.v += 16
        s.qe(_)

    def emit_tile(self, seq, qt):
        s = self
        row0 = seq * T + qt * 128
        ti = seq * NT + qt
        kts = [k for k in (qt - 2, qt - 1, qt) if k >= 0]
        slot = qt % 4
        s.xbuf = s.xb[ti % 2]

        if ti + 1 < BPC * NT:
            nti = ti + 1
            nrow = (nti // NT) * T + (nti % NT) * 128
            gate = s.res1_cnt.get(ti - 1)
            sem2 = s.sX if nti % 2 == 0 else s.sX2

            def _(q, nti=nti, nrow=nrow, gate=gate, sem2=sem2):
                if gate is None:
                    q.wait_ge(s.sV.h, s.sV.v)
                else:
                    q.wait_ge(s.sP.h, gate)
                q.dma_start(s.xb[nti % 2][:],
                            s.x_in[nrow:nrow + 128, :]).then_inc(sem2.h, 16)
                sem2.v += 16
            s.qe(_)
            s.x_ld[nti % 2] = sem2.v

        xsem = s.sX if ti % 2 == 0 else s.sX2
        xw = s.x_ld[ti % 2]

        def _(v):
            v.wait_ge(xsem.h, xw)
        s.ve(_)

        def _(sc):
            sc.wait_ge(xsem.h, xw)
        s.se(_)

        if s.debug_stop == "noop":
            def _(v):
                s.sV.inc(v.tensor_copy(s.h2buf[:], s.xbuf[:]))
            s.ve(_)

            def _(q):
                q.wait_ge(s.sV.h, s.sV.v)
                q.wait_ge(s.sOUT.h, s.sOUT.v)
                q.dma_start(s.out_d[row0:row0 + 128, :], s.h2buf[:]).then_inc(s.sOUT.h, 16)
                s.sOUT.v += 16
            s.qe(_)
            return

        self.emit_ln(s.xbuf, s.bfA)

        s.transpose_blocks(s.bfA, NKC, s.bfB, [kc * 128 for kc in range(NKC)],
                           evac="s")

        def _(v):
            v.wait_ge(s.sS.h, s.sS.v)
            a = None
            for kc in range(NKC):
                a = v.tensor_scalar(s.bfB[:, kc * 128:(kc + 1) * 128],
                                    s.bfB[:, kc * 128:(kc + 1) * 128],
                                    s.ca1c[:, kc * BPC + seq:kc * BPC + seq + 1],
                                    s.sac[:, kc * BPC + seq:kc * BPC + seq + 1],
                                    ALU.mult, ALU.add)
            s.sV.inc(a)
        s.ve(_)

        if s.debug_stop == "ln1":
            self._dbg_dump(s.bfB[:], row0)
            return

        def _(t):
            t.wait_ge(s.sV.h, s.sV.v)
            a = None
            for mc in range(8):
                for kc in range(NKC):
                    a = t.matmul(s.ps_A[:, mc * 128:(mc + 1) * 128],
                                 s.wq_T[:, (mc * 8 + kc) * 128:(mc * 8 + kc + 1) * 128],
                                 s.bfB[:, kc * 128:(kc + 1) * 128],
                                 start=(kc == 0 and mc % 4 == 0), stop=True,
                                 skip_group_check=True)
            for kc in range(NKC):
                a = t.matmul(s.ps_B[:, 0:128],
                             s.kv_T[:, kc * 128:(kc + 1) * 128],
                             s.bfB[:, kc * 128:(kc + 1) * 128],
                             start=(kc == 0), stop=True, skip_group_check=True)
            for kc in range(NKC):
                a = t.matmul(s.ps_B[:, 128:256],
                             s.kv2_T[:, kc * 128:(kc + 1) * 128],
                             s.bfB[:, kc * 128:(kc + 1) * 128],
                             start=False, stop=True, skip_group_check=True)
            s.sPE.inc(a)
        s.te(_)

        def _(e):
            e.wait_ge(s.sPE.h, s.sPE.v)
            e.copy(s.bfC[:], s.ps_A[:])
            e.copy(s.kk[0:64, slot * 128:(slot + 1) * 128], s.ps_B[0:64, 0:128])
            e.copy(s.kk[64:128, slot * 128:(slot + 1) * 128],
                   s.ps_B[64:128, 128:256])
            s.sS.inc(e.copy(s.bfA[64:128, 0:128], s.ps_B[64:128, 0:128]))
        s.se(_)

        def _(t):
            t.wait_ge(s.sS.h, s.sS.v)
            s.sPE.inc(t.transpose(s.ps_T[:, 0:64], s.bfA[64:128, 0:128],
                                  s.ident[64:128, 64:128]))
        s.te(_)

        def _(e):
            e.wait_ge(s.sPE.h, s.sPE.v)
            s.sS.inc(e.copy(s.v_all[:, slot * 65:slot * 65 + 64],
                            s.ps_T[:, 0:64]))
        s.se(_)

        if s.debug_stop == "qkv":
            self._dbg_dump(s.bfC[:], row0)
            return

        groups = []
        for kt in kts:
            mask = s.mnew if kt == qt else (s.mold if kt == qt - 2 else None)
            for grp in range(2):
                groups.append((kt, grp, mask))
        exp_w, msk_w = {}, {}

        def emit_scores(gi):
            kt, grp, mask = groups[gi]
            kslot = kt % 4
            ps = s.ps_A if gi % 2 == 0 else s.ps_B
            pbuf = s.probsT[:, 0:1024] if gi % 2 == 0 else s.probsT2[:, 0:1024]

            def _(t):
                if gi == 0:
                    t.wait_ge(s.sV.h, s.sV.v)
                    t.wait_ge(s.sS.h, s.sS.v)
                if gi >= 2:
                    t.wait_ge(s.sS.h, exp_w[gi - 2])
                a = None
                for h8 in range(8):
                    h = grp * 8 + h8
                    base = (h % 2) * 64
                    col = (h8 % 2) * 512 + (h8 // 2) * 128
                    a = t.matmul(ps[:, col:col + 128],
                                 s.kk[base:base + 64,
                                      kslot * 128:(kslot + 1) * 128],
                                 s.bfC[base:base + 64,
                                       (h // 2) * 128:(h // 2 + 1) * 128],
                                 start=(h8 in (0, 1)), stop=True,
                                 skip_group_check=True)
                s.sPE.inc(a)
            s.te(_)

            def _(sc):
                sc.wait_ge(s.sPE.h, s.sPE.v)
                s.sS.inc(sc.activation(pbuf, ps[:], AF.Exp,
                                       scale=0.125, bias=s.consts[:, 0:1]))
            s.se(_)
            exp_w[gi] = s.sS.v

            if mask is not None:
                def _(v):
                    v.wait_ge(s.sS.h, s.sS.v)
                    mb = mask[:].unsqueeze(1).broadcast_to([128, 8, 128])
                    pvw = pbuf.rearrange("p (h c) -> p h c", c=128)
                    s.sV.inc(v.tensor_tensor(pvw, pvw, mb, ALU.mult))
                s.ve(_)
                msk_w[gi] = s.sV.v

        def emit_pv(gi):
            kt, grp, mask = groups[gi]
            kslot = kt % 4
            pbuf = s.probsT[:, 0:1024] if gi % 2 == 0 else s.probsT2[:, 0:1024]

            def _(t):
                if mask is not None:
                    t.wait_ge(s.sV.h, msk_w[gi])
                else:
                    t.wait_ge(s.sS.h, exp_w[gi])
                a = None
                for h8 in range(8):
                    h = grp * 8 + h8
                    col = (h8 % 2) * 512 + (h8 // 2) * 128
                    a = t.matmul(s.ps_pv[:, pv_off(h):pv_off(h) + 65],
                                 pbuf[:, col:col + 128],
                                 s.v_all[:, kslot * 65:(kslot + 1) * 65],
                                 start=(gi in (0, 1) and h in (0, 7, 14)),
                                 stop=True, skip_group_check=True)
                s.sPE.inc(a)
            s.te(_)

        for gi in range(len(groups)):
            emit_scores(gi)
            if gi >= 1:
                emit_pv(gi - 1)
        emit_pv(len(groups) - 1)

        def _(v):
            v.wait_ge(s.sPE.h, s.sPE.v)
            for b_ in range(3):
                n = (7, 7, 2)[b_]
                src = s.ps_pv[:, b_ * 512 + 64: b_ * 512 + 64 + n * 65]
                v.tensor_copy(s.sums[:, b_ * 7:b_ * 7 + n].unsqueeze(2),
                              src.rearrange("p (h c) -> p h c", c=65)[:, :, 0:1])
            v.reciprocal(s.rsum[:], s.sums[:])
            a = None
            for b_ in range(3):
                n = (7, 7, 2)[b_]
                pvv = s.ps_pv[:, b_ * 512: b_ * 512 + n * 65].rearrange(
                    "p (h c) -> p h c", c=65)[:, :, 0:64]
                rb = s.rsum[:, b_ * 7:b_ * 7 + n].unsqueeze(2).broadcast_to([128, n, 64])
                dst = s.bfA[:, b_ * 7 * 64: b_ * 7 * 64 + n * 64].rearrange(
                    "p (h c) -> p h c", c=64)
                a = v.tensor_tensor(dst, pvv, rb, ALU.mult)
            s.sV.inc(a)
        s.ve(_)

        if s.debug_stop in ("smax", "smk"):
            def _(v):
                v.wait_ge(s.sS.h, s.sS.v)
                pass
            s.ve(_)
            self._dbg_dump(s.probsT[:].rearrange("p (a b) -> p (a b)") if False else s.probsT[:], row0)
            return

        if s.debug_stop == "pv":
            self._dbg_dump(s.bfA[:], row0)
            return

        s.transpose_blocks(s.bfA, NKC, s.bfB, [kc * 128 for kc in range(NKC)],
                           evac="s")

        def _(t):
            t.wait_ge(s.sV.h, s.sV.v)
            t.wait_ge(s.sS.h, s.sS.v)
            a = None
            for kc in range(NKC):
                for half in range(2):
                    a = t.matmul(s.ps_B[:, half * 512:(half + 1) * 512],
                                 s.bfB[:, kc * 128:(kc + 1) * 128],
                                 s.wf_T[:, kc * 1024 + half * 512:
                                        kc * 1024 + (half + 1) * 512],
                                 start=(kc == 0), stop=True, skip_group_check=True)
            for half in range(2):
                a = t.matmul(s.ps_B[:, half * 512:(half + 1) * 512],
                             s.onesb[:], s.bfrow[:, half * 512:(half + 1) * 512],
                             start=False, stop=True, skip_group_check=True)
            s.sPE.inc(a)
        s.te(_)

        def _(v):
            v.wait_ge(s.sPE.h, s.sPE.v)
            v.wait_ge(s.sOUT.h, s.sOUT.v)
            s.sV.inc(v.tensor_tensor(s.h2buf[:], s.ps_B[:], s.gab[:], ALU.mult))
        s.ve(_)
        r1v = s.sV.v

        def _(p):
            p.wait_ge(s.sV.h, r1v)
            s.sP.inc(p.tensor_tensor(s.x1buf[:], s.h2buf[:], s.xbuf[:], ALU.add))
        s.po(_)
        s.res1_cnt[ti] = s.sP.v
        res1_p = s.sP.v

        if s.debug_stop == "attn":
            def _(q):
                q.wait_ge(s.sV.h, s.sV.v)
                q.wait_ge(s.sP.h, s.sP.v)
                q.dma_start(s.out_d[row0:row0 + 128, :], s.x1buf[:]).then_inc(s.sOUT.h, 16)
                s.sOUT.v += 16
            s.qe(_)

            def _(v):
                v.wait_ge(s.sOUT.h, s.sOUT.v)
            s.ve(_)
            return

        def _(v):
            v.wait_ge(s.sP.h, res1_p)
            s.sV.inc(v.tensor_reduce(s.stat[:, 0:1], s.x1buf[:], AX.X, ALU.add))
        s.ve(_)

        def _(sc):
            sc.wait_ge(s.sP.h, res1_p)
            s.sS.inc(sc.activation(s.bfE[:], s.x1buf[:], AF.Square,
                                   accum_out=s.stat[:, 3:4]))
        s.se(_)

        def _(v):
            v.wait_ge(s.sS.h, s.sS.v)
            v.tensor_scalar(s.stat[:, 1:2], s.stat[:, 0:1], 1.0 / D, None, ALU.mult)
            v.tensor_scalar(s.stat[:, 2:3], s.stat[:, 1:2], s.stat[:, 1:2], -EPS,
                            ALU.mult, ALU.add)
            s.sV.inc(v.tensor_scalar(s.stat[:, 4:5], s.stat[:, 3:4], 1.0 / D,
                                     s.stat[:, 2:3], ALU.mult, ALU.subtract))
        s.ve(_)

        def _(p):
            p.wait_ge(s.sV.h, s.sV.v)
            s.sP.inc(p.tensor_tensor(s.stat[:, 5:6], s.stat[:, 4:5],
                                     s.consts[:, 1:2], ALU.pow))
        s.po(_)

        def _(v):
            v.wait_ge(s.sP.h, s.sP.v)
            v.wait_ge(s.sOUT.h, s.sOUT.v)
            v.scalar_tensor_tensor(s.h2buf[:], s.x1buf[:], s.stat[:, 1:2],
                                   s.cbb[:], ALU.subtract, ALU.mult)
            s.sV.inc(v.scalar_tensor_tensor(s.h2buf[:], s.h2buf[:], s.stat[:, 5:6],
                                            s.sbb[:], ALU.mult, ALU.add))
        s.ve(_)

        h2q_w = s.emit_quant2(s.h2buf, D, s.bfE, s.h2buf, s.bfD, s.bfE,
                              s.bmm, s.qr2, s.qsh, 1.0)

        s.transpose_blocks(s.bfE, NKC, s.bfC, [kc * 128 for kc in range(NKC)],
                           evac="s", wait_p=h2q_w)

        if s.debug_stop == "h2q":
            self._dbg_dump(s.bfC[:], row0)
            return

        pieces = PIECES
        if s.debug_stop == "gate1":
            pieces = PIECES[:1]
        nP = len(pieces)
        ps_free = {}
        quant_w = {}
        evac_w = {}
        gu_w = {}
        up_w = {}
        hT_w = {}

        def gu_ps(idx):
            return s.ps_A if idx % 2 == 0 else s.ps_pv

        def emit_gateup(idx):
            p0_, pn = pieces[idx]

            def _(t):
                if idx < 2:
                    t.wait_ge(s.sV.h, s.sV.v)
                    t.wait_ge(s.sS.h, s.sS.v)
                else:
                    t.wait_ge(s.sV.h, ps_free[idx - 2])
                a = None
                ps = gu_ps(idx)
                for kc in range(NKC):
                    a = t.matmul(ps[:, 0:pn],
                                 s.bfC[:, kc * 128:(kc + 1) * 128],
                                 s.wg_q[:, kc * II + p0_: kc * II + p0_ + pn],
                                 start=(kc == 0), stop=True, skip_group_check=True)
                s.sPE.inc(a)
            s.te(_)
            gu_w[idx] = s.sPE.v

            def _(t):
                a = None
                ps = gu_ps(idx)
                for kc in range(NKC):
                    a = t.matmul(ps[:, 512:512 + pn],
                                 s.bfC[:, kc * 128:(kc + 1) * 128],
                                 s.wu_q[:, kc * II + p0_: kc * II + p0_ + pn],
                                 start=(kc == 0), stop=True, skip_group_check=True)
                s.sPE.inc(a)
            s.te(_)
            up_w[idx] = s.sPE.v

        tanh_w = {}

        def emit_tanh(idx):
            p0_, pn = pieces[idx]

            def _(sc):
                sc.wait_ge(s.sPE.h, gu_w[idx])
                if idx >= 1:
                    sc.wait_ge(s.sP.h, quant_w[idx - 1])
                s.sS.inc(sc.activation(s.tf_a[:, 0:pn], gu_ps(idx)[:, 0:pn],
                                       AF.Tanh, scale=0.5))
            s.se(_)
            tanh_w[idx] = s.sS.v

        def emit_siluquant(idx):
            p0_, pn = pieces[idx]

            def _(v):
                v.wait_ge(s.sS.h, tanh_w[idx])
                v.wait_ge(s.sPE.h, up_w[idx])
                ps = gu_ps(idx)
                v.scalar_tensor_tensor(s.tf_b[:, 0:pn], s.tf_a[:, 0:pn], 1.0,
                                       ps[:, 0:pn], ALU.add, ALU.mult)
                s.sV.inc(v.tensor_tensor(s.tf_b[:, 0:pn], s.tf_b[:, 0:pn],
                                         ps[:, 512:512 + pn], ALU.mult))
            s.ve(_)
            ps_free[idx] = s.sV.v

            quant_w[idx] = s.emit_quant2(s.tf_b, pn, s.hmT, s.tf_a,
                                         s.bfE[:, 512:1024], s.bfE[:, 0:512],
                                         s.bmm, s.qr2, s.qsh, 1.0)

        def emit_hmidT(idx):
            p0_, pn = pieces[idx]
            nsub = pn // 128

            def _(t):
                t.wait_ge(s.sP.h, quant_w[idx])
                t.wait_ge(s.sS.h, s.sS.v)
                a = None
                for j in range(nsub):
                    a = t.transpose(s.ps_T[:, j * 128:(j + 1) * 128],
                                    s.hmT[:, j * 128:(j + 1) * 128],
                                    s.ident[:])
                s.sPE.inc(a)
            s.te(_)
            hT_w[idx] = s.sPE.v

            def _(e):
                e.wait_ge(s.sPE.h, hT_w[idx])
                a = None
                for j in range(nsub):
                    a = e.copy(s.hmq[:, j * 128:(j + 1) * 128],
                               s.ps_T[:, j * 128:(j + 1) * 128])
                s.sS.inc(a)
            s.se(_)
            evac_w[idx] = s.sS.v

        def emit_down(idx):
            p0_, pn = pieces[idx]

            def _(t):
                t.wait_ge(s.sS.h, evac_w[idx])
                a = None
                for j in range(pn // 128):
                    ic = p0_ // 128 + j
                    for half in range(2):
                        a = t.matmul(s.ps_B[:, half * 512:(half + 1) * 512],
                                     s.hmq[:, j * 128:(j + 1) * 128],
                                     s.wd_q[:, ic * 1024 + half * 512:
                                            ic * 1024 + (half + 1) * 512],
                                     start=(idx == 0 and j == 0), stop=True,
                                     skip_group_check=True)
                s.sPE.inc(a)
            s.te(_)

        for idx in range(nP + 1):
            if idx < nP:
                emit_gateup(idx)
                emit_tanh(idx)
            if idx >= 1:
                emit_hmidT(idx - 1)
            if idx < nP:
                emit_siluquant(idx)
            if idx >= 1:
                emit_down(idx - 1)

        def _(v):
            v.wait_ge(s.sPE.h, s.sPE.v)
            s.sV.inc(v.tensor_tensor(s.h2buf[:], s.ps_B[:], s.gbb[:], ALU.mult))
        s.ve(_)
        r2v = s.sV.v

        def _(p):
            p.wait_ge(s.sV.h, r2v)
            s.sP.inc(p.tensor_tensor(s.h2buf[:], s.h2buf[:], s.x1buf[:], ALU.add))
        s.po(_)

        def _(q):
            q.wait_ge(s.sP.h, s.sP.v)
            q.dma_start(s.out_d[row0:row0 + 128, :], s.h2buf[:]).then_inc(s.sOUT.h, 16)
            s.sOUT.v += 16
        s.qe(_)

    def seq_setup(self, seq):
        s = self
        with s.nc.Block() as blk:
            s.blk = blk
            p0rowb = s.bfA[0:1, :]
            for slot_, dst in ((0, s.cbb), (1, s.sbb), (2, s.gab), (3, s.gbb)):
                def _(q, slot_=slot_):
                    q.wait_ge(s.sV.h, s.sV.v)
                    q.wait_ge(s.sPE.h, s.sPE.v)
                    q.dma_start(p0rowb,
                                s.rows_d[seq:seq + 1, slot_ * D:(slot_ + 1) * D]
                                ).then_inc(s.sLD.h, 16)
                    s.sLD.v += 16
                s.qe(_)

                def _(t):
                    t.wait_ge(s.sLD.h, s.sLD.v)
                    t.matmul(s.ps_B[:, 0:512], s.onesb[:], p0rowb[:, 0:512],
                             start=True, stop=True, skip_group_check=True)
                    a = t.matmul(s.ps_B[:, 512:1024], s.onesb[:], p0rowb[:, 512:1024],
                                 start=True, stop=True, skip_group_check=True)
                    s.sPE.inc(a)
                s.te(_)

                def _(v, dst=dst):
                    v.wait_ge(s.sPE.h, s.sPE.v)
                    s.sV.inc(v.tensor_copy(dst[:], s.ps_B[:]))
                s.ve(_)

    def build(self):
        self.declare_io()
        with self.es:
            self.declare_persistent()
            self.phase0()
            self.phase1_mod()
            self.phase2_weights()
            self.declare_main_bufs()

            with self.nc.Block() as blk:
                self.blk = blk

                def _(v):
                    v.memset(
                        self.v_all[:].rearrange("p (s c) -> p s c", c=65)[:, :, 64:65],
                        1.0)
                    v.memset(self.stat[:], 0.0)
                self.ve(_)

                def _(q):
                    q.dma_start(self.xb[0][:],
                                self.x_in[0:128, :]).then_inc(self.sX.h, 16)
                    self.sX.v += 16
                self.qe(_)
                self.x_ld[0] = self.sX.v

            for seq in range(BPC):
                self.seq_setup(seq)
                with self.nc.Block() as blk:
                    self.blk = blk
                    for qt in range(NT):
                        self.emit_tile(seq, qt)

            with self.nc.Block() as blk:
                self.blk = blk

                def _(q):
                    q.wait_ge(self.sOUT.h, self.sOUT.v)
                self.qe(_)
        self.nc.compile()
        return self.nc


_NC_CACHE = {}


def _get_nc(debug_stop=None):
    if debug_stop not in _NC_CACHE:
        _NC_CACHE[debug_stop] = KB(debug_stop).build()
    return _NC_CACHE[debug_stop]


def kernel(L0_attn_x, L0_attn_segment_pos, L0_attn_Wq, L0_attn_Wk, L0_attn_Wv,
           L0_attn_Wf, L0_attn_bf, L0_ffn_w_gate, L0_ffn_w_up, L0_ffn_w_down,
           L0_c, L0_wmod, _debug_stop=None):
    nc = _get_nc(_debug_stop)

    assert np.asarray(L0_attn_segment_pos).shape[-1] == T
    x = np.ascontiguousarray(np.asarray(L0_attn_x, dtype=np.float32))
    ident = np.eye(128, dtype=ml_dtypes.bfloat16)
    jj, ii = np.meshgrid(np.arange(128), np.arange(128), indexing="ij")
    mnew = (jj <= ii).astype(ml_dtypes.bfloat16)
    mold = (jj >= ii).astype(ml_dtypes.bfloat16)
    shared = dict(
        wq=np.ascontiguousarray(np.asarray(L0_attn_Wq, dtype=np.float32)),
        wk=np.ascontiguousarray(np.asarray(L0_attn_Wk, dtype=np.float32)),
        wv=np.ascontiguousarray(np.asarray(L0_attn_Wv, dtype=np.float32)),
        wf=np.ascontiguousarray(np.asarray(L0_attn_Wf, dtype=np.float32)),
        bfv=np.ascontiguousarray(np.asarray(L0_attn_bf, dtype=np.float32)).reshape(1, D),
        wg=np.ascontiguousarray(np.asarray(L0_ffn_w_gate, dtype=np.float32)),
        wu=np.ascontiguousarray(np.asarray(L0_ffn_w_up, dtype=np.float32)),
        wd=np.ascontiguousarray(np.asarray(L0_ffn_w_down, dtype=np.float32)),
        wmod=np.ascontiguousarray(np.asarray(L0_wmod, dtype=np.float32)),
        ident=ident, mnew=np.ascontiguousarray(mnew), mold=np.ascontiguousarray(mold),
        identf=np.eye(BPC, dtype=np.float32),
    )
    c_full = np.asarray(L0_c, dtype=np.float32)
    in_maps = []
    for core in range(NCORES):
        m = dict(shared)
        m["x"] = np.ascontiguousarray(
            x[core * BPC:(core + 1) * BPC].reshape(BPC * T, D))
        m["c"] = np.ascontiguousarray(c_full[core * BPC:(core + 1) * BPC])
        in_maps.append(m)

    res = run_bass_kernel_spmd(nc, in_maps, core_ids=list(range(NCORES)))
    out = np.concatenate(
        [r["out"].reshape(BPC, T, D) for r in res.results], axis=0)
    return out.astype(np.float32)


# version 12
# speedup vs baseline: 1.0348x; 1.0010x over previous
import numpy as np
import ml_dtypes
from contextlib import ExitStack

import concourse.bacc as bacc
import concourse.mybir as mybir
from concourse.bass_utils import run_bass_kernel_spmd
from concourse import dve_ops
from concourse.dve_op import DveOp
from concourse.dve_spec import Spec, Src0, Src1, C0, C1, C2, maxx, minn
from concourse.dve_spec import lower as dve_lower
from concourse.dve_table_gen import dve_ver_for

F32 = mybir.dt.float32
BF16 = mybir.dt.bfloat16
AF = mybir.ActivationFunctionType
ALU = mybir.AluOpType
AX = mybir.AxisListType

B, T, D = 16, 2048, 1024
NCORES, BPC = 8, 2
NT = T // 128
NH, HD = 16, 64
II = 2816
NKC = D // 128
NIC = II // 128
EPS = 1e-5
PIECES = [(0, 512), (512, 512), (1024, 512), (1536, 512), (2048, 512), (2560, 256)]

C_RND1 = float(1.5 * 2.0**23)
C_RND2 = float(1.5 * 2.0**24)


def _rnd(x, g):
    return np.round(x / g) * g


def _ref_t1(in0, in1, s0, s1, imm2):
    return _rnd(np.clip(in0, -s0, s0), 1.0).astype(in0.dtype)


def _ref_t12(in0, in1, s0, s1, imm2):
    m4 = np.clip(in0, -s0, s0)
    return (_rnd(np.clip(in0 - m4, -s0, s0), 2.0) + in1).astype(in0.dtype)


def _ref_fin(in0, in1, s0, s1, imm2):
    t3 = ((in0 > s0).astype(in0.dtype) - (in0 < -s0).astype(in0.dtype)) * imm2
    return (t3 + in1).astype(in0.dtype)


_VER = dve_ver_for("TRN2")


def _mk_op(name, spec, ref):
    sha = dve_lower(spec, ver=_VER, perf_en=False, name=name).sha(_VER)
    return DveOp(name, spec, subdim=False, uops_sha={_VER: sha}, reference=ref)


_c4 = minn(maxx(Src0, C2), C0)
OP_T1 = _mk_op("ANT_FP4_T1", Spec(body=(_c4 + C1) - C1), _ref_t1)
_d2 = minn(maxx(Src0 - _c4, C2), C0)
OP_T12 = _mk_op("ANT_FP4_T12", Spec(body=((_d2 + C1) - C1) + Src1), _ref_t12)
_g3 = (Src0 > C0) - (Src0 < C1)
OP_FIN = _mk_op("ANT_FP4_FIN", Spec(body=_g3 * C2 + Src1), _ref_fin)


def _register_ops():
    victims = ["DVEHASH32L_SEED", "DVEHASH32L_BODY", "DVEHASH32L_DRAIN"]
    for op, victim in zip((OP_T1, OP_T12, OP_FIN), victims):
        if op.name in dve_ops._SUB_OPCODE_FOR_NAME:
            continue
        row = dve_ops._SUB_OPCODE_FOR_NAME.pop(victim)
        idx = next(i for i, o in enumerate(dve_ops.OPS) if o.name == victim)
        dve_ops.OPS[idx] = op
        dve_ops._SUB_OPCODE_FOR_NAME[op.name] = row
        dve_ops.CUSTOM_DVE_SPECS.pop(victim, None)
        dve_ops.CUSTOM_DVE_REFS.pop(victim, None)
        dve_ops.CUSTOM_DVE_SPECS[op.name] = op.spec
        dve_ops.CUSTOM_DVE_REFS[op.name] = op.ref


_register_ops()


_OUT_KW = ("out", "accum_out")


def _ap_rng(x):
    t = getattr(x, "tensor", None)
    name = getattr(t, "name", None)
    if name is None:
        return None
    try:
        off = x.offset
        reach = 1
        for step, count in list(x.ap)[1:]:
            reach += (count - 1) * step
        return (name, off, off + reach)
    except Exception:
        return (name, 0, 1 << 60)


def _overlaps(a, bs):
    for b in bs:
        if a[0] == b[0] and a[1] < b[2] and b[1] < a[2]:
            return True
    return False


class DrainWrap:

    _OPS = frozenset((
        "tensor_tensor", "tensor_scalar", "tensor_copy", "tensor_reduce",
        "reduce_max", "reduce_sum", "reciprocal", "memset", "_custom_dve",
        "scalar_tensor_tensor", "activation", "select", "copy",
    ))

    def __init__(self, eng, state):
        object.__setattr__(self, "_e", eng)
        object.__setattr__(self, "_st", state)

    def __getattr__(self, name):
        a = getattr(self._e, name)
        if name not in self._OPS:
            return a

        def f(*args, **kw):
            st = self._st
            rngs = [r for r in (_ap_rng(x) for x in args) if r] +                    [r for r in (_ap_rng(v) for v in kw.values()) if r]
            if name == "_custom_dve":
                outs = [_ap_rng(kw[k]) for k in _OUT_KW if kw.get(k) is not None]
            else:
                outs = [_ap_rng(args[0])]
                if kw.get("accum_out") is not None:
                    outs.append(_ap_rng(kw["accum_out"]))
            outs = [o for o in outs if o]
            ins = [r for r in rngs if r not in outs]
            if (any(_overlaps(o, st["w"]) or _overlaps(o, st["r"]) for o in outs)
                    or any(_overlaps(i, st["w"]) for i in ins)):
                self._e.drain()
                st["w"].clear()
                st["r"].clear()
            st["w"] += outs
            st["r"] += ins
            return a(*args, **kw)
        return f


class Ctr:
    def __init__(self, handle, step=1):
        self.h, self.v, self.step = handle, 0, step

    def inc(self, inst):
        inst.then_inc(self.h, self.step)
        self.v += self.step


def pv_off(h):
    b = h // 7
    return b * 512 + (h - 7 * b) * 65


class KB:

    def __init__(self, debug_stop=None):
        self.debug_stop = debug_stop
        self.nc = bacc.Bacc("TRN2", target_bir_lowering=False)
        self.es = ExitStack()
        self.blk = None
        self._haz = {e: {"w": [], "r": []} for e in ("v", "s", "p")}

    def te(self, f):
        self.blk.tensor(f)

    def ve(self, f):
        self.blk.vector(lambda e: f(DrainWrap(e, self._haz["v"])))

    def se(self, f):
        self.blk.scalar(lambda e: f(DrainWrap(e, self._haz["s"])))

    def po(self, f):
        self.blk.gpsimd(lambda e: f(DrainWrap(e, self._haz["p"])))

    def qe(self, f):
        self.blk.sync(f)

    def sbuf(self, name, shape, dt=F32, stack=None):
        return (stack or self.es).enter_context(self.nc.sbuf_tensor(name, shape, dt))

    def psum(self, name, shape, dt=F32):
        return self.es.enter_context(self.nc.psum_tensor(name, shape, dt))

    def sem(self, name, step=1):
        return Ctr(self.es.enter_context(self.nc.semaphore(name)), step)

    def declare_io(self):
        nc = self.nc
        self.x_in = nc.dram_tensor("x", [BPC * T, D], F32, kind="ExternalInput").ap()
        self.wq_in = nc.dram_tensor("wq", [D, D], F32, kind="ExternalInput").ap()
        self.wk_in = nc.dram_tensor("wk", [HD, D], F32, kind="ExternalInput").ap()
        self.wv_in = nc.dram_tensor("wv", [HD, D], F32, kind="ExternalInput").ap()
        self.wf_in = nc.dram_tensor("wf", [D, D], F32, kind="ExternalInput").ap()
        self.bf_in = nc.dram_tensor("bfv", [1, D], F32, kind="ExternalInput").ap()
        self.wg_in = nc.dram_tensor("wg", [II, D], F32, kind="ExternalInput").ap()
        self.wu_in = nc.dram_tensor("wu", [II, D], F32, kind="ExternalInput").ap()
        self.wd_in = nc.dram_tensor("wd", [D, II], F32, kind="ExternalInput").ap()
        self.c_in = nc.dram_tensor("c", [BPC, D], F32, kind="ExternalInput").ap()
        self.wmod_in = nc.dram_tensor("wmod", [D, 6 * D], F32, kind="ExternalInput").ap()
        self.ident_in = nc.dram_tensor("ident", [128, 128], BF16, kind="ExternalInput").ap()
        self.mnew_in = nc.dram_tensor("mnew", [128, 128], BF16, kind="ExternalInput").ap()
        self.mold_in = nc.dram_tensor("mold", [128, 128], BF16, kind="ExternalInput").ap()
        self.identf_in = nc.dram_tensor("identf", [BPC, BPC], F32, kind="ExternalInput").ap()
        self.out_d = nc.dram_tensor("out", [BPC * T, D], F32, kind="ExternalOutput").ap()

    def declare_persistent(self):
        sbuf, psum, sem = self.sbuf, self.psum, self.sem
        self.wq_T = sbuf("wq_T", [128, 64 * 128], BF16)
        self.kv_T = sbuf("kv_T", [128, 8 * 128], BF16)
        self.kv2_T = sbuf("kv2_T", [128, 8 * 128], BF16)
        self.wf_T = sbuf("wf_T", [128, 8 * 1024], BF16)
        self.wg_q = sbuf("wg_q", [128, 8 * II], BF16)
        self.wu_q = sbuf("wu_q", [128, 8 * II], BF16)
        self.wd_q = sbuf("wd_q", [128, NIC * 1024], BF16)
        self.ident = sbuf("ident_s", [128, 128], BF16)
        self.mnew = sbuf("mnew_s", [128, 128], BF16)
        self.mold = sbuf("mold_s", [128, 128], BF16)
        self.onesb = self.mnew[0:1, :]
        self.identf = sbuf("identf_s", [BPC, BPC], F32)
        self.bfrow = sbuf("bfrow", [1, D], BF16)
        self.consts = sbuf("consts", [128, 4], F32)
        self.ca1c = sbuf("ca1c", [128, NKC * BPC], F32)
        self.sac = sbuf("sac", [128, NKC * BPC], F32)
        self.rows_d = self.nc.dram_tensor("rows_d", [BPC, 4 * D], BF16).ap()
        self.cbb = sbuf("cbb", [128, D], BF16)
        self.sbb = sbuf("sbb", [128, D], BF16)
        self.gab = sbuf("gab", [128, D], BF16)
        self.gbb = sbuf("gbb", [128, D], BF16)
        self.ps_A = psum("ps_A", [128, 1024], F32)
        self.ps_B = psum("ps_B", [128, 1024], F32)
        self.ps_pv = psum("ps_pv", [128, 1280], F32)
        self.ps_T = psum("ps_T", [128, 512], BF16)
        self.sLD = sem("sLD", 16)
        self.sL2 = sem("sL2", 16)
        self.sL3 = sem("sL3", 16)
        self.sL4 = sem("sL4", 16)
        self.sX = sem("sX", 16)
        self.sX2 = sem("sX2", 16)
        self.sOUT = sem("sOUT", 16)
        self.sPE = sem("sPE")
        self.sV = sem("sV")
        self.sS = sem("sS")
        self.sP = sem("sP")

    def declare_main_bufs(self):
        sbuf = self.sbuf
        self.xbuf = sbuf("xbuf", [128, D], F32)
        self.xbuf2 = sbuf("xbuf2", [128, D], F32)
        self.xb = [self.xbuf, self.xbuf2]
        self.x_ld = [0, 0]
        self.res1_cnt = {}
        self.x1buf = sbuf("x1buf", [128, D], F32)
        self.h2buf = sbuf("h2buf", [128, D], F32)
        self.bfA = sbuf("bfA", [128, D], BF16)
        self.bfB = sbuf("bfB", [128, D], BF16)
        self.bfC = sbuf("bfC", [128, D], BF16)
        self.bfD = sbuf("bfD", [128, D], BF16)
        self.bfE = sbuf("bfE", [128, D], BF16)
        self.probsT = None
        self.kk = sbuf("kk", [128, 4 * 128], BF16)
        self.v_all = sbuf("v_all", [128, 4 * 65], BF16)
        self.stat = sbuf("stat", [128, 16], F32)
        self.bmm = sbuf("bmm", [128, 64], F32)
        self.qr2 = sbuf("qr2", [128, 64], F32)
        self.qsh = sbuf("qsh", [128, 64], BF16)
        self.sums = self.bmm[:, 48:64]
        self.rsum = self.qr2[:, 48:64]
        self.probsT2 = self.bfD
        self.probsT = self.bfB
        self.tf_a = self.h2buf[:, 512:1024]
        self.tf_b = self.h2buf[:, 0:512]
        self.hmT = self.bfD[:, 0:512]
        self.hmq = self.bfD[:, 512:1024]

    def phase0(self):
        s = self
        with s.nc.Block() as blk:
            s.blk = blk

            def _(q):
                q.dma_start(s.ident[:], s.ident_in).then_inc(s.sLD.h, 16)
                q.dma_start(s.mnew[:], s.mnew_in).then_inc(s.sLD.h, 16)
                q.dma_start(s.mold[:], s.mold_in).then_inc(s.sLD.h, 16)
                q.dma_start(s.identf[:], s.identf_in).then_inc(s.sLD.h, 16)
                s.sLD.v += 64
            s.qe(_)

            def _(sc):
                sc.preload_act_table(AF.Exp)
            s.se(_)

            def _(v):
                v.memset(s.consts[:, 0:1], -30.0)
                v.memset(s.consts[:, 1:2], -0.5)
            s.ve(_)

    def phase1_mod(self):
        s = self
        with ExitStack() as ph:
            c_sb = s.sbuf("c_sb", [BPC, D], F32, ph)
            sct = s.sbuf("sct", [BPC, D], F32, ph)
            sc_bf = s.sbuf("sc_bf", [BPC, D], BF16, ph)
            rowstg = s.sbuf("rowstg", [BPC, D], BF16, ph)
            scT = s.sbuf("scT", [128, NKC * BPC], F32, ph)
            wmod_f = s.sbuf("wmod_f", [128, 1024], F32, ph)
            wmod_g = s.sbuf("wmod_g", [128, 1024], F32, ph)
            wmod_h = s.sbuf("wmod_h", [128, 1024], F32, ph)
            wmod_i = s.sbuf("wmod_i", [128, 1024], F32, ph)
            tmpr = wmod_f[0:BPC, :]

            with s.nc.Block() as blk:
                s.blk = blk

                def _(q):
                    q.dma_start(c_sb[:], s.c_in).then_inc(s.sLD.h, 16)
                    q.dma_start(sct[0:1, :], s.bf_in).then_inc(s.sLD.h, 16)
                    s.sLD.v += 32
                s.qe(_)

                def _(v):
                    v.wait_ge(s.sLD.h, s.sLD.v)
                    s.sV.inc(v.tensor_copy(s.bfrow[:], sct[0:1, :]))
                s.ve(_)

                def _(sc):
                    sc.wait_ge(s.sV.h, s.sV.v)
                    s.sS.inc(sc.activation(sct[:], c_sb[:], AF.Tanh, scale=0.5))
                s.se(_)

                def _(v):
                    v.wait_ge(s.sS.h, s.sS.v)
                    v.tensor_scalar(sct[:], sct[:], 0.5, 0.5, ALU.mult, ALU.add)
                    s.sV.inc(v.tensor_tensor(sct[:], sct[:], c_sb[:], ALU.mult))
                s.ve(_)

                def _(t):
                    t.wait_ge(s.sV.h, s.sV.v)
                    a = None
                    for kc in range(NKC):
                        a = t.transpose(s.ps_A[:, kc * BPC:(kc + 1) * BPC],
                                        sct[:, kc * 128:(kc + 1) * 128],
                                        s.identf[:])
                    s.sPE.inc(a)
                s.te(_)

                def _(v):
                    v.wait_ge(s.sPE.h, s.sPE.v)
                    s.sV.inc(v.tensor_copy(scT[:], s.ps_A[:, 0:NKC * BPC]))
                s.ve(_)

                WNB = 4
                wbufs = [wmod_f, wmod_g, wmod_h, wmod_i]
                wlds = [s.sLD, s.sL2, s.sL3, s.sL4]
                wqes = [s.qe, s.po, s.qe, s.po]
                wm_ld = [0] * WNB
                wm_pe = [0] * WNB
                wi = 0
                for pair in range(3):
                    for sub in range(2):
                        vec = pair * 2 + sub
                        ps = s.ps_A if sub == 0 else s.ps_B
                        for kc in range(NKC):
                            buf, ld = wbufs[wi % WNB], wlds[wi % WNB]
                            sv_now, ss_now = s.sV.v, s.sS.v

                            def _(q, vec=vec, kc=kc, wi=wi, ld=ld, buf=buf,
                                  sv_now=sv_now, ss_now=ss_now):
                                q.wait_ge(s.sPE.h, wm_pe[wi % WNB])
                                q.wait_ge(s.sV.h, sv_now)
                                q.wait_ge(s.sS.h, ss_now)
                                q.dma_start(buf[:],
                                            s.wmod_in[kc * 128:(kc + 1) * 128,
                                                      vec * D:(vec + 1) * D]
                                            ).then_inc(ld.h, 16)
                                ld.v += 16
                            wqes[wi % WNB](_)
                            wm_ld[wi % WNB] = ld.v

                            def _(t, ps=ps, kc=kc, buf=buf, wi=wi, ld=ld,
                                  sv_now=sv_now):
                                t.wait_ge(ld.h, wm_ld[wi % WNB])
                                t.wait_ge(s.sV.h, sv_now)
                                a = None
                                for piece in range(2):
                                    a = t.matmul(ps[0:BPC, piece * 512:(piece + 1) * 512],
                                                 scT[:, kc * BPC:(kc + 1) * BPC],
                                                 buf[:, piece * 512:(piece + 1) * 512],
                                                 start=(kc == 0), stop=(kc == NKC - 1),
                                                 skip_group_check=True)
                                s.sPE.inc(a)
                            s.te(_)
                            wm_pe[wi % WNB] = s.sPE.v
                            wi += 1

                    def _(v):
                        v.wait_ge(s.sPE.h, s.sPE.v)
                        v.tensor_copy(c_sb[:], s.ps_A[0:BPC, :])
                        s.sV.inc(v.tensor_copy(sct[:], s.ps_B[0:BPC, :]))
                    s.ve(_)

                    if pair == 0:
                        for rowbuf, dst, add1 in ((c_sb, s.sac, 0.0), (sct, s.ca1c, 1.0)):
                            for kc in range(NKC):
                                def _(t, rowbuf=rowbuf, kc=kc):
                                    t.wait_ge(s.sV.h, s.sV.v)
                                    s.sPE.inc(t.transpose(
                                        s.ps_B[:, 512:512 + BPC],
                                        rowbuf[:, kc * 128:(kc + 1) * 128],
                                        s.identf[:]))
                                s.te(_)

                                def _(v, dst=dst, add1=add1, kc=kc):
                                    v.wait_ge(s.sPE.h, s.sPE.v)
                                    a = None
                                    for sq in range(BPC):
                                        a = v.tensor_scalar(
                                            dst[:, kc * BPC + sq: kc * BPC + sq + 1],
                                            s.ps_B[:, 512 + sq:512 + sq + 1],
                                            add1, None, ALU.add)
                                    s.sV.inc(a)
                                s.ve(_)
                    elif pair == 1:
                        def _(sc):
                            sc.wait_ge(s.sV.h, s.sV.v)
                            s.sS.inc(sc.activation(tmpr, c_sb[:], AF.Sign))
                        s.se(_)

                        def _(v):
                            v.wait_ge(s.sS.h, s.sS.v)
                            v.tensor_tensor(c_sb[:], c_sb[:], tmpr, ALU.mult)
                            v.tensor_scalar(c_sb[:], c_sb[:], 0.25, None, ALU.max)
                            v.tensor_tensor(sc_bf[:], c_sb[:], tmpr, ALU.mult)
                            s.sV.inc(v.tensor_copy(rowstg[:], sct[:]))
                        s.ve(_)

                        def _(q):
                            q.wait_ge(s.sV.h, s.sV.v)
                            q.dma_start(s.rows_d[:, 2 * D:3 * D], sc_bf[:]).then_inc(s.sX.h, 16)
                            q.dma_start(s.rows_d[:, 1 * D:2 * D], rowstg[:]).then_inc(s.sX.h, 16)
                            s.sX.v += 32
                            q.wait_ge(s.sX.h, s.sX.v)
                        s.qe(_)
                    else:
                        def _(sc):
                            sc.wait_ge(s.sV.h, s.sV.v)
                            s.sS.inc(sc.activation(tmpr, sct[:], AF.Sign))
                        s.se(_)

                        def _(v):
                            v.wait_ge(s.sS.h, s.sS.v)
                            v.wait_ge(s.sX.h, s.sX.v)
                            v.tensor_scalar(sc_bf[:], c_sb[:], 1.0, None, ALU.add)
                            v.tensor_tensor(sct[:], sct[:], tmpr, ALU.mult)
                            v.tensor_scalar(sct[:], sct[:], 0.25, None, ALU.max)
                            s.sV.inc(v.tensor_tensor(rowstg[:], sct[:], tmpr, ALU.mult))
                        s.ve(_)

                        def _(q):
                            q.wait_ge(s.sV.h, s.sV.v)
                            q.dma_start(s.rows_d[:, 0 * D:1 * D], sc_bf[:]).then_inc(s.sX.h, 16)
                            q.dma_start(s.rows_d[:, 3 * D:4 * D], rowstg[:]).then_inc(s.sX.h, 16)
                            s.sX.v += 32
                            q.wait_ge(s.sX.h, s.sX.v)
                        s.qe(_)

    def emit_quant(self, v, src_f32, F, out_bf, w_scr, t1_bf, t12_bf,
                   bm, r2, sh, fold=1.0):
        nb = F // 16
        src = src_f32[:, 0:F]
        v.tensor_reduce(bm[:, 0:nb], src.rearrange("p (n k) -> p n k", k=16),
                        AX.X, ALU.max, apply_absolute_value=True)
        v.tensor_scalar(bm[:, 0:nb], bm[:, 0:nb], 1.0 / 12.0, 5e-13, ALU.mult, ALU.max)
        v.tensor_scalar(sh[:, 0:nb], bm[:, 0:nb], fold, None, ALU.mult)
        v.reciprocal(r2[:, 0:nb], bm[:, 0:nb])
        rb = r2[:, 0:nb].unsqueeze(2).broadcast_to([128, nb, 16])
        v.tensor_tensor(w_scr[:, 0:F].rearrange("p (n k) -> p n k", k=16),
                        src.rearrange("p (n k) -> p n k", k=16), rb, ALU.mult)
        v._custom_dve(OP_T1, out=t1_bf[:, 0:F], in0=w_scr[:, 0:F],
                      s0=4.0, s1=C_RND1, imm2=-4.0)
        v._custom_dve(OP_T12, out=t12_bf[:, 0:F], in0=w_scr[:, 0:F],
                      in1=t1_bf[:, 0:F], s0=4.0, s1=C_RND2, imm2=-4.0)
        v._custom_dve(OP_FIN, out=t1_bf[:, 0:F], in0=w_scr[:, 0:F],
                      in1=t12_bf[:, 0:F], s0=10.0, s1=-10.0, imm2=4.0)
        shb = sh[:, 0:nb].unsqueeze(2).broadcast_to([128, nb, 16])
        return v.tensor_tensor(out_bf[:, 0:F].rearrange("p (n k) -> p n k", k=16),
                               t1_bf[:, 0:F].rearrange("p (n k) -> p n k", k=16),
                               shb, ALU.mult)

    def emit_quant2(self, src_f32, F, out_bf, w_scr, t1_bf, t12_bf,
                    bm, r2, sh, fold=1.0, pre=None):
        s = self
        nb = F // 16

        def _(v):
            if pre is not None:
                pre(v)
            srcv = src_f32[:, 0:F]
            v.tensor_reduce(bm[:, 0:nb], srcv.rearrange("p (n k) -> p n k", k=16),
                            AX.X, ALU.max, apply_absolute_value=True)
            v.tensor_scalar(bm[:, 0:nb], bm[:, 0:nb], 1.0 / 12.0, 5e-13,
                            ALU.mult, ALU.max)
            v.tensor_scalar(sh[:, 0:nb], bm[:, 0:nb], fold, None, ALU.mult)
            s.sV.inc(v.reciprocal(r2[:, 0:nb], bm[:, 0:nb]))
        s.ve(_)

        pe_now = s.sPE.v

        def _(p):
            p.wait_ge(s.sV.h, s.sV.v)
            p.wait_ge(s.sPE.h, pe_now)
            rb = r2[:, 0:nb].unsqueeze(2).broadcast_to([128, nb, 16])
            s.sP.inc(p.tensor_tensor(
                w_scr[:, 0:F].rearrange("p (n k) -> p n k", k=16),
                src_f32[:, 0:F].rearrange("p (n k) -> p n k", k=16),
                rb, ALU.mult))
        s.po(_)

        def _(v):
            v.wait_ge(s.sP.h, s.sP.v)
            v._custom_dve(OP_T1, out=t1_bf[:, 0:F], in0=w_scr[:, 0:F],
                          s0=4.0, s1=C_RND1, imm2=-4.0)
            v._custom_dve(OP_T12, out=t12_bf[:, 0:F], in0=w_scr[:, 0:F],
                          in1=t1_bf[:, 0:F], s0=4.0, s1=C_RND2, imm2=-4.0)
            s.sV.inc(v._custom_dve(OP_FIN, out=t1_bf[:, 0:F], in0=w_scr[:, 0:F],
                                   in1=t12_bf[:, 0:F], s0=10.0, s1=-10.0, imm2=4.0))
        s.ve(_)

        def _(p):
            p.wait_ge(s.sV.h, s.sV.v)
            p.wait_ge(s.sPE.h, pe_now)
            shb = sh[:, 0:nb].unsqueeze(2).broadcast_to([128, nb, 16])
            s.sP.inc(p.tensor_tensor(
                out_bf[:, 0:F].rearrange("p (n k) -> p n k", k=16),
                t1_bf[:, 0:F].rearrange("p (n k) -> p n k", k=16),
                shb, ALU.mult))
        s.po(_)
        return s.sP.v

    def transpose_blocks(self, src_bf, n_blocks, dst, dst_offs, evac="v",
                         wait_p=None):
        s = self
        for g0 in range(0, n_blocks, 4):
            gn = min(4, n_blocks - g0)

            def _(t, g0=g0, gn=gn):
                t.wait_ge(s.sV.h, s.sV.v)
                if wait_p is not None:
                    t.wait_ge(s.sP.h, wait_p)
                if evac == "s":
                    t.wait_ge(s.sS.h, s.sS.v)
                a = None
                for j in range(gn):
                    a = t.transpose(s.ps_T[:, j * 128:(j + 1) * 128],
                                    src_bf[:, (g0 + j) * 128:(g0 + j + 1) * 128],
                                    s.ident[:])
                s.sPE.inc(a)
            s.te(_)

            def _(e, g0=g0, gn=gn):
                e.wait_ge(s.sPE.h, s.sPE.v)
                cp = e.tensor_copy if evac == "v" else e.copy
                a = None
                for j in range(gn):
                    a = cp(dst[:, dst_offs[g0 + j]:dst_offs[g0 + j] + 128],
                           s.ps_T[:, j * 128:(j + 1) * 128])
                (s.sV if evac == "v" else s.sS).inc(a)
            (s.ve if evac == "v" else s.se)(_)

    def phase2_weights(self):
        s = self
        with ExitStack() as ph:
            WHALF = II // 2
            stg = s.sbuf("stg", [128, WHALF], F32, ph)
            stg2 = s.sbuf("stg2", [128, WHALF], F32, ph)
            stg3 = s.sbuf("stg3", [128, WHALF], F32, ph)
            stq = s.sbuf("stq", [128, WHALF], BF16, ph)
            qt1 = s.sbuf("qt1", [128, WHALF], BF16, ph)
            qt12 = s.sbuf("qt12", [128, WHALF], BF16, ph)
            bmw = s.sbuf("bmw", [128, WHALF // 16], F32, ph)
            qr2w = s.sbuf("qr2w", [128, WHALF // 16], F32, ph)
            qshw = s.sbuf("qshw", [128, WHALF // 16], BF16, ph)

            with s.nc.Block() as blk:
                s.blk = blk
                NB = 3
                bufs = [stg, stg2, stg3]
                lds = [s.sLD, s.sL2, s.sL3]
                qes = [s.qe, s.po, s.qe]
                st_state = {"i": 0, "ld": [0] * NB, "c": [None] * NB}

                def load_rows(src_ap, ncols):
                    i = st_state["i"]
                    buf, ld = bufs[i % NB], lds[i % NB]
                    gate = st_state["c"][i % NB]
                    if gate is None:
                        gate = s.sV.v

                    def _(q, ld=ld, gate=gate):
                        q.wait_ge(s.sV.h, gate)
                        q.dma_start(buf[:, 0:ncols], src_ap).then_inc(ld.h, 16)
                        ld.v += 16
                    qes[i % NB](_)
                    st_state["ld"][i % NB] = ld.v
                    st_state["i"] += 1
                    return buf

                def cast_stage(buf, ncols):
                    i = st_state["i"] - 1
                    ld = lds[i % NB]
                    thr = st_state["ld"][i % NB]
                    pe_now = s.sPE.v

                    def _(v, ld=ld, thr=thr, pe_now=pe_now):
                        v.wait_ge(ld.h, thr)
                        v.wait_ge(s.sPE.h, pe_now)
                        s.sV.inc(v.tensor_copy(stq[:, 0:ncols], buf[:, 0:ncols]))
                    s.ve(_)
                    st_state["c"][i % NB] = s.sV.v

                def quant_stage(buf, ncols, fold):
                    i = st_state["i"] - 1
                    ld = lds[i % NB]
                    thr = st_state["ld"][i % NB]
                    sp_now = s.sP.v

                    def _pre(v, ld=ld, thr=thr, sp_now=sp_now):
                        v.wait_ge(ld.h, thr)
                        v.wait_ge(s.sP.h, sp_now)
                    qw = s.emit_quant2(buf, ncols, stq, buf, qt1, qt12,
                                       bmw, qr2w, qshw, fold, pre=_pre)
                    st_state["c"][i % NB] = s.sV.v
                    return qw

                for oc in range(NKC):
                    b = load_rows(s.wq_in[oc * 128:(oc + 1) * 128, :], D)
                    cast_stage(b, D)
                    s.transpose_blocks(stq, NKC, s.wq_T,
                                       [(oc * 8 + kc) * 128 for kc in range(NKC)],
                                       evac="s")

                kvi = st_state["i"]
                kvbuf, kvld = bufs[kvi % NB], lds[kvi % NB]
                kvgate = st_state["c"][kvi % NB]
                if kvgate is None:
                    kvgate = s.sV.v

                def _(q, kvld=kvld, kvgate=kvgate):
                    q.wait_ge(s.sV.h, kvgate)
                    q.dma_start(kvbuf[0:HD, 0:D], s.wk_in).then_inc(kvld.h, 16)
                    q.dma_start(kvbuf[64:128, 0:D], s.wv_in).then_inc(kvld.h, 16)
                    kvld.v += 32
                qes[kvi % NB](_)
                st_state["ld"][kvi % NB] = kvld.v
                st_state["i"] += 1
                cast_stage(kvbuf, D)
                for kc in range(NKC):
                    def _(t, kc=kc):
                        t.wait_ge(s.sV.h, s.sV.v)
                        t.wait_ge(s.sS.h, s.sS.v)
                        s.sPE.inc(t.transpose(s.ps_T[:, 0:128],
                                              stq[:, kc * 128:(kc + 1) * 128],
                                              s.ident[:]))
                    s.te(_)

                    def _(v, kc=kc):
                        v.wait_ge(s.sPE.h, s.sPE.v)
                        v.tensor_copy(s.kv_T[:, kc * 128:(kc + 1) * 128], s.ps_T[:, 0:128])
                        v.tensor_copy(s.kv2_T[:, kc * 128:kc * 128 + 64],
                                      s.ps_T[:, 64:128])
                        s.sV.inc(v.tensor_copy(s.kv2_T[:, kc * 128 + 64:(kc + 1) * 128],
                                               s.ps_T[:, 0:64]))
                    s.ve(_)

                for oc in range(NKC):
                    b = load_rows(s.wf_in[oc * 128:(oc + 1) * 128, :], D)
                    cast_stage(b, D)
                    s.transpose_blocks(stq, NKC, s.wf_T,
                                       [kc * 1024 + oc * 128 for kc in range(NKC)],
                                       evac="s")

                for w_src, w_dst, fold in ((s.wg_in, s.wg_q, 1.0), (s.wu_in, s.wu_q, 0.5)):
                    for ic in range(NIC):
                        b = load_rows(w_src[ic * 128:(ic + 1) * 128, :], D)
                        qw = quant_stage(b, D, fold)
                        s.transpose_blocks(stq, NKC, w_dst,
                                           [kc * II + ic * 128 for kc in range(NKC)],
                                           evac="s", wait_p=qw)

                for oc in range(NKC):
                    for hf in range(2):
                        b = load_rows(s.wd_in[oc * 128:(oc + 1) * 128,
                                              hf * WHALF:(hf + 1) * WHALF], WHALF)
                        qw = quant_stage(b, WHALF, 1.0)
                        s.transpose_blocks(stq, WHALF // 128, s.wd_q,
                                           [(hf * 11 + ic) * 1024 + oc * 128
                                            for ic in range(WHALF // 128)],
                                           evac="s", wait_p=qw)

    def emit_ln(self, src, dst, wait_out=False):
        s = self

        def _(v):
            s.sV.inc(v.tensor_reduce(s.stat[:, 0:1], src[:], AX.X, ALU.add))
        s.ve(_)

        def _(sc):
            sc.wait_ge(s.sV.h, s.sV.v)
            s.sS.inc(sc.activation(s.bfE[:], src[:], AF.Square,
                                   accum_out=s.stat[:, 3:4]))
        s.se(_)

        def _(v):
            v.wait_ge(s.sS.h, s.sS.v)
            v.tensor_scalar(s.stat[:, 1:2], s.stat[:, 0:1], 1.0 / D, None, ALU.mult)
            v.tensor_scalar(s.stat[:, 2:3], s.stat[:, 1:2], s.stat[:, 1:2], -EPS,
                            ALU.mult, ALU.add)
            s.sV.inc(v.tensor_scalar(s.stat[:, 4:5], s.stat[:, 3:4], 1.0 / D,
                                     s.stat[:, 2:3], ALU.mult, ALU.subtract))
        s.ve(_)

        def _(p):
            p.wait_ge(s.sV.h, s.sV.v)
            s.sP.inc(p.tensor_tensor(s.stat[:, 5:6], s.stat[:, 4:5],
                                     s.consts[:, 1:2], ALU.pow))
        s.po(_)

        def _(v):
            v.wait_ge(s.sP.h, s.sP.v)
            if wait_out:
                v.wait_ge(s.sOUT.h, s.sOUT.v)
            s.sV.inc(v.tensor_scalar(dst[:], src[:], s.stat[:, 1:2], s.stat[:, 5:6],
                                     ALU.subtract, ALU.mult))
        s.ve(_)


    def emit_head(self, nti):
        s = self
        xb = s.xb[nti % 2]
        xsem = s.sX if nti % 2 == 0 else s.sX2
        xw = s.x_ld[nti % 2]
        seq = nti // NT
        sp_now = s.sP.v

        def _(v):
            v.wait_ge(xsem.h, xw)
            s.sV.inc(v.tensor_reduce(s.stat[:, 8:9], xb[:], AX.X, ALU.add))
        s.ve(_)

        def _(sc):
            sc.wait_ge(xsem.h, xw)
            sc.wait_ge(s.sP.h, sp_now)
            s.sS.inc(sc.activation(s.bfE[:], xb[:], AF.Square,
                                   accum_out=s.stat[:, 11:12]))
        s.se(_)

        def _(v):
            v.wait_ge(s.sS.h, s.sS.v)
            v.tensor_scalar(s.stat[:, 9:10], s.stat[:, 8:9], 1.0 / D, None,
                            ALU.mult)
            v.tensor_scalar(s.stat[:, 10:11], s.stat[:, 9:10], s.stat[:, 9:10],
                            -EPS, ALU.mult, ALU.add)
            s.sV.inc(v.tensor_scalar(s.stat[:, 12:13], s.stat[:, 11:12], 1.0 / D,
                                     s.stat[:, 10:11], ALU.mult, ALU.subtract))
        s.ve(_)

        def _(p):
            p.wait_ge(s.sV.h, s.sV.v)
            s.sP.inc(p.tensor_tensor(s.stat[:, 13:14], s.stat[:, 12:13],
                                     s.consts[:, 1:2], ALU.pow))
        s.po(_)
        pw = s.sP.v

        def _(v):
            v.wait_ge(s.sP.h, pw)
            v.wait_ge(s.sPE.h, s.sPE.v)
            s.sV.inc(v.tensor_scalar(s.bfA[:], xb[:], s.stat[:, 9:10],
                                     s.stat[:, 13:14], ALU.subtract, ALU.mult))
        s.ve(_)

        s.transpose_blocks(s.bfA, NKC, s.bfB, [kc * 128 for kc in range(NKC)],
                           evac="s")

        def _(v):
            v.wait_ge(s.sS.h, s.sS.v)
            a = None
            for kc in range(NKC):
                a = v.tensor_scalar(s.bfB[:, kc * 128:(kc + 1) * 128],
                                    s.bfB[:, kc * 128:(kc + 1) * 128],
                                    s.ca1c[:, kc * BPC + seq:kc * BPC + seq + 1],
                                    s.sac[:, kc * BPC + seq:kc * BPC + seq + 1],
                                    ALU.mult, ALU.add)
            s.sV.inc(a)
        s.ve(_)

    def _dbg_dump(self, src_ap, row0):
        s = self

        def _(v):
            s.sV.inc(v.tensor_copy(s.h2buf[:], src_ap))
        s.ve(_)

        def _(q):
            q.wait_ge(s.sV.h, s.sV.v)
            q.wait_ge(s.sOUT.h, s.sOUT.v)
            q.dma_start(s.out_d[row0:row0 + 128, :], s.h2buf[:]).then_inc(s.sOUT.h, 16)
            s.sOUT.v += 16
        s.qe(_)

    def emit_tile(self, seq, qt):
        s = self
        row0 = seq * T + qt * 128
        ti = seq * NT + qt
        kts = [k for k in (qt - 2, qt - 1, qt) if k >= 0]
        slot = qt % 4
        s.xbuf = s.xb[ti % 2]

        if ti + 1 < BPC * NT:
            nti = ti + 1
            nrow = (nti // NT) * T + (nti % NT) * 128
            gate = s.res1_cnt.get(ti - 1)
            sem2 = s.sX if nti % 2 == 0 else s.sX2

            def _(q, nti=nti, nrow=nrow, gate=gate, sem2=sem2):
                if gate is None:
                    q.wait_ge(s.sV.h, s.sV.v)
                else:
                    q.wait_ge(s.sP.h, gate)
                q.dma_start(s.xb[nti % 2][:],
                            s.x_in[nrow:nrow + 128, :]).then_inc(sem2.h, 16)
                sem2.v += 16
            s.qe(_)
            s.x_ld[nti % 2] = sem2.v

        xsem = s.sX if ti % 2 == 0 else s.sX2
        xw = s.x_ld[ti % 2]

        def _(v):
            v.wait_ge(xsem.h, xw)
        s.ve(_)

        def _(sc):
            sc.wait_ge(xsem.h, xw)
        s.se(_)

        if ti == 0 and s.debug_stop != "noop":
            s.emit_head(0)

        if s.debug_stop == "noop":
            def _(v):
                s.sV.inc(v.tensor_copy(s.h2buf[:], s.xbuf[:]))
            s.ve(_)

            def _(q):
                q.wait_ge(s.sV.h, s.sV.v)
                q.wait_ge(s.sOUT.h, s.sOUT.v)
                q.dma_start(s.out_d[row0:row0 + 128, :], s.h2buf[:]).then_inc(s.sOUT.h, 16)
                s.sOUT.v += 16
            s.qe(_)
            return

        if s.debug_stop == "ln1":
            self._dbg_dump(s.bfB[:], row0)
            return

        def _(t):
            t.wait_ge(s.sV.h, s.sV.v)
            a = None
            for mc in range(8):
                for kc in range(NKC):
                    a = t.matmul(s.ps_A[:, mc * 128:(mc + 1) * 128],
                                 s.wq_T[:, (mc * 8 + kc) * 128:(mc * 8 + kc + 1) * 128],
                                 s.bfB[:, kc * 128:(kc + 1) * 128],
                                 start=(kc == 0 and mc % 4 == 0), stop=True,
                                 skip_group_check=True)
            for kc in range(NKC):
                a = t.matmul(s.ps_B[:, 0:128],
                             s.kv_T[:, kc * 128:(kc + 1) * 128],
                             s.bfB[:, kc * 128:(kc + 1) * 128],
                             start=(kc == 0), stop=True, skip_group_check=True)
            for kc in range(NKC):
                a = t.matmul(s.ps_B[:, 128:256],
                             s.kv2_T[:, kc * 128:(kc + 1) * 128],
                             s.bfB[:, kc * 128:(kc + 1) * 128],
                             start=False, stop=True, skip_group_check=True)
            s.sPE.inc(a)
        s.te(_)

        def _(e):
            e.wait_ge(s.sPE.h, s.sPE.v)
            e.copy(s.bfC[:], s.ps_A[:])
            e.copy(s.kk[0:64, slot * 128:(slot + 1) * 128], s.ps_B[0:64, 0:128])
            e.copy(s.kk[64:128, slot * 128:(slot + 1) * 128],
                   s.ps_B[64:128, 128:256])
            s.sS.inc(e.copy(s.bfA[64:128, 0:128], s.ps_B[64:128, 0:128]))
        s.se(_)

        def _(t):
            t.wait_ge(s.sS.h, s.sS.v)
            s.sPE.inc(t.transpose(s.ps_T[:, 0:64], s.bfA[64:128, 0:128],
                                  s.ident[64:128, 64:128]))
        s.te(_)

        def _(e):
            e.wait_ge(s.sPE.h, s.sPE.v)
            s.sS.inc(e.copy(s.v_all[:, slot * 65:slot * 65 + 64],
                            s.ps_T[:, 0:64]))
        s.se(_)

        if s.debug_stop == "qkv":
            self._dbg_dump(s.bfC[:], row0)
            return

        groups = []
        for kt in kts:
            mask = s.mnew if kt == qt else (s.mold if kt == qt - 2 else None)
            for grp in range(2):
                groups.append((kt, grp, mask))
        exp_w, msk_w = {}, {}

        def emit_scores(gi):
            kt, grp, mask = groups[gi]
            kslot = kt % 4
            ps = s.ps_A if gi % 2 == 0 else s.ps_B
            pbuf = s.probsT[:, 0:1024] if gi % 2 == 0 else s.probsT2[:, 0:1024]

            def _(t):
                if gi == 0:
                    t.wait_ge(s.sV.h, s.sV.v)
                    t.wait_ge(s.sS.h, s.sS.v)
                if gi >= 2:
                    t.wait_ge(s.sS.h, exp_w[gi - 2])
                a = None
                for h8 in range(8):
                    h = grp * 8 + h8
                    base = (h % 2) * 64
                    col = (h8 % 2) * 512 + (h8 // 2) * 128
                    a = t.matmul(ps[:, col:col + 128],
                                 s.kk[base:base + 64,
                                      kslot * 128:(kslot + 1) * 128],
                                 s.bfC[base:base + 64,
                                       (h // 2) * 128:(h // 2 + 1) * 128],
                                 start=(h8 in (0, 1)), stop=True,
                                 skip_group_check=True)
                s.sPE.inc(a)
            s.te(_)

            def _(sc):
                sc.wait_ge(s.sPE.h, s.sPE.v)
                s.sS.inc(sc.activation(pbuf, ps[:], AF.Exp,
                                       scale=0.125, bias=s.consts[:, 0:1]))
            s.se(_)
            exp_w[gi] = s.sS.v

            if mask is not None:
                def _(v):
                    v.wait_ge(s.sS.h, s.sS.v)
                    mb = mask[:].unsqueeze(1).broadcast_to([128, 8, 128])
                    pvw = pbuf.rearrange("p (h c) -> p h c", c=128)
                    s.sV.inc(v.tensor_tensor(pvw, pvw, mb, ALU.mult))
                s.ve(_)
                msk_w[gi] = s.sV.v

        def emit_pv(gi):
            kt, grp, mask = groups[gi]
            kslot = kt % 4
            pbuf = s.probsT[:, 0:1024] if gi % 2 == 0 else s.probsT2[:, 0:1024]

            def _(t):
                if mask is not None:
                    t.wait_ge(s.sV.h, msk_w[gi])
                else:
                    t.wait_ge(s.sS.h, exp_w[gi])
                a = None
                for h8 in range(8):
                    h = grp * 8 + h8
                    col = (h8 % 2) * 512 + (h8 // 2) * 128
                    a = t.matmul(s.ps_pv[:, pv_off(h):pv_off(h) + 65],
                                 pbuf[:, col:col + 128],
                                 s.v_all[:, kslot * 65:(kslot + 1) * 65],
                                 start=(gi in (0, 1) and h in (0, 7, 14)),
                                 stop=True, skip_group_check=True)
                s.sPE.inc(a)
            s.te(_)

        for gi in range(len(groups)):
            emit_scores(gi)
            if gi >= 1:
                emit_pv(gi - 1)
        emit_pv(len(groups) - 1)

        def _(v):
            v.wait_ge(s.sPE.h, s.sPE.v)
            for b_ in range(3):
                n = (7, 7, 2)[b_]
                src = s.ps_pv[:, b_ * 512 + 64: b_ * 512 + 64 + n * 65]
                v.tensor_copy(s.sums[:, b_ * 7:b_ * 7 + n].unsqueeze(2),
                              src.rearrange("p (h c) -> p h c", c=65)[:, :, 0:1])
            v.reciprocal(s.rsum[:], s.sums[:])
            a = None
            for b_ in range(3):
                n = (7, 7, 2)[b_]
                pvv = s.ps_pv[:, b_ * 512: b_ * 512 + n * 65].rearrange(
                    "p (h c) -> p h c", c=65)[:, :, 0:64]
                rb = s.rsum[:, b_ * 7:b_ * 7 + n].unsqueeze(2).broadcast_to([128, n, 64])
                dst = s.bfA[:, b_ * 7 * 64: b_ * 7 * 64 + n * 64].rearrange(
                    "p (h c) -> p h c", c=64)
                a = v.tensor_tensor(dst, pvv, rb, ALU.mult)
            s.sV.inc(a)
        s.ve(_)

        if s.debug_stop in ("smax", "smk"):
            def _(v):
                v.wait_ge(s.sS.h, s.sS.v)
                pass
            s.ve(_)
            self._dbg_dump(s.probsT[:].rearrange("p (a b) -> p (a b)") if False else s.probsT[:], row0)
            return

        if s.debug_stop == "pv":
            self._dbg_dump(s.bfA[:], row0)
            return

        s.transpose_blocks(s.bfA, NKC, s.bfB, [kc * 128 for kc in range(NKC)],
                           evac="s")

        def _(t):
            t.wait_ge(s.sV.h, s.sV.v)
            t.wait_ge(s.sS.h, s.sS.v)
            a = None
            for kc in range(NKC):
                for half in range(2):
                    a = t.matmul(s.ps_B[:, half * 512:(half + 1) * 512],
                                 s.bfB[:, kc * 128:(kc + 1) * 128],
                                 s.wf_T[:, kc * 1024 + half * 512:
                                        kc * 1024 + (half + 1) * 512],
                                 start=(kc == 0), stop=True, skip_group_check=True)
            for half in range(2):
                a = t.matmul(s.ps_B[:, half * 512:(half + 1) * 512],
                             s.onesb[:], s.bfrow[:, half * 512:(half + 1) * 512],
                             start=False, stop=True, skip_group_check=True)
            s.sPE.inc(a)
        s.te(_)

        def _(v):
            v.wait_ge(s.sPE.h, s.sPE.v)
            v.wait_ge(s.sOUT.h, s.sOUT.v)
            s.sV.inc(v.tensor_tensor(s.h2buf[:], s.ps_B[:], s.gab[:], ALU.mult))
        s.ve(_)
        r1v = s.sV.v

        def _(p):
            p.wait_ge(s.sV.h, r1v)
            s.sP.inc(p.tensor_tensor(s.x1buf[:], s.h2buf[:], s.xbuf[:], ALU.add))
        s.po(_)
        s.res1_cnt[ti] = s.sP.v
        res1_p = s.sP.v

        if s.debug_stop == "attn":
            def _(q):
                q.wait_ge(s.sV.h, s.sV.v)
                q.wait_ge(s.sP.h, s.sP.v)
                q.dma_start(s.out_d[row0:row0 + 128, :], s.x1buf[:]).then_inc(s.sOUT.h, 16)
                s.sOUT.v += 16
            s.qe(_)

            def _(v):
                v.wait_ge(s.sOUT.h, s.sOUT.v)
            s.ve(_)
            return

        if ti + 1 < BPC * NT:
            s.emit_head(ti + 1)

        def _(v):
            v.wait_ge(s.sP.h, res1_p)
            s.sV.inc(v.tensor_reduce(s.stat[:, 0:1], s.x1buf[:], AX.X, ALU.add))
        s.ve(_)

        def _(sc):
            sc.wait_ge(s.sP.h, res1_p)
            s.sS.inc(sc.activation(s.bfE[:], s.x1buf[:], AF.Square,
                                   accum_out=s.stat[:, 3:4]))
        s.se(_)

        def _(v):
            v.wait_ge(s.sS.h, s.sS.v)
            v.tensor_scalar(s.stat[:, 1:2], s.stat[:, 0:1], 1.0 / D, None, ALU.mult)
            v.tensor_scalar(s.stat[:, 2:3], s.stat[:, 1:2], s.stat[:, 1:2], -EPS,
                            ALU.mult, ALU.add)
            s.sV.inc(v.tensor_scalar(s.stat[:, 4:5], s.stat[:, 3:4], 1.0 / D,
                                     s.stat[:, 2:3], ALU.mult, ALU.subtract))
        s.ve(_)

        def _(p):
            p.wait_ge(s.sV.h, s.sV.v)
            s.sP.inc(p.tensor_tensor(s.stat[:, 5:6], s.stat[:, 4:5],
                                     s.consts[:, 1:2], ALU.pow))
        s.po(_)

        def _(v):
            v.wait_ge(s.sP.h, s.sP.v)
            v.wait_ge(s.sOUT.h, s.sOUT.v)
            v.scalar_tensor_tensor(s.h2buf[:], s.x1buf[:], s.stat[:, 1:2],
                                   s.cbb[:], ALU.subtract, ALU.mult)
            s.sV.inc(v.scalar_tensor_tensor(s.h2buf[:], s.h2buf[:], s.stat[:, 5:6],
                                            s.sbb[:], ALU.mult, ALU.add))
        s.ve(_)

        h2q_w = s.emit_quant2(s.h2buf, D, s.bfE, s.h2buf, s.bfD, s.bfE,
                              s.bmm, s.qr2, s.qsh, 1.0)

        s.transpose_blocks(s.bfE, NKC, s.bfC, [kc * 128 for kc in range(NKC)],
                           evac="s", wait_p=h2q_w)

        if s.debug_stop == "h2q":
            self._dbg_dump(s.bfC[:], row0)
            return

        pieces = PIECES
        if s.debug_stop == "gate1":
            pieces = PIECES[:1]
        nP = len(pieces)
        ps_free = {}
        quant_w = {}
        evac_w = {}
        gu_w = {}
        up_w = {}
        hT_w = {}

        def gu_ps(idx):
            return s.ps_A if idx % 2 == 0 else s.ps_pv

        def emit_gateup(idx):
            p0_, pn = pieces[idx]

            def _(t):
                if idx < 2:
                    t.wait_ge(s.sV.h, s.sV.v)
                    t.wait_ge(s.sS.h, s.sS.v)
                else:
                    t.wait_ge(s.sV.h, ps_free[idx - 2])
                a = None
                ps = gu_ps(idx)
                for kc in range(NKC):
                    a = t.matmul(ps[:, 0:pn],
                                 s.bfC[:, kc * 128:(kc + 1) * 128],
                                 s.wg_q[:, kc * II + p0_: kc * II + p0_ + pn],
                                 start=(kc == 0), stop=True, skip_group_check=True)
                s.sPE.inc(a)
            s.te(_)
            gu_w[idx] = s.sPE.v

            def _(t):
                a = None
                ps = gu_ps(idx)
                for kc in range(NKC):
                    a = t.matmul(ps[:, 512:512 + pn],
                                 s.bfC[:, kc * 128:(kc + 1) * 128],
                                 s.wu_q[:, kc * II + p0_: kc * II + p0_ + pn],
                                 start=(kc == 0), stop=True, skip_group_check=True)
                s.sPE.inc(a)
            s.te(_)
            up_w[idx] = s.sPE.v

        tanh_w = {}

        def emit_tanh(idx):
            p0_, pn = pieces[idx]

            def _(sc):
                sc.wait_ge(s.sPE.h, gu_w[idx])
                if idx >= 1:
                    sc.wait_ge(s.sP.h, quant_w[idx - 1])
                s.sS.inc(sc.activation(s.tf_a[:, 0:pn], gu_ps(idx)[:, 0:pn],
                                       AF.Tanh, scale=0.5))
            s.se(_)
            tanh_w[idx] = s.sS.v

        def emit_siluquant(idx):
            p0_, pn = pieces[idx]

            def _(v):
                v.wait_ge(s.sS.h, tanh_w[idx])
                v.wait_ge(s.sPE.h, up_w[idx])
                ps = gu_ps(idx)
                v.scalar_tensor_tensor(s.tf_b[:, 0:pn], s.tf_a[:, 0:pn], 1.0,
                                       ps[:, 0:pn], ALU.add, ALU.mult)
                s.sV.inc(v.tensor_tensor(s.tf_b[:, 0:pn], s.tf_b[:, 0:pn],
                                         ps[:, 512:512 + pn], ALU.mult))
            s.ve(_)
            ps_free[idx] = s.sV.v

            quant_w[idx] = s.emit_quant2(s.tf_b, pn, s.hmT, s.tf_a,
                                         s.bfE[:, 512:1024], s.bfE[:, 0:512],
                                         s.bmm, s.qr2, s.qsh, 1.0)

        def emit_hmidT(idx):
            p0_, pn = pieces[idx]
            nsub = pn // 128

            def _(t):
                t.wait_ge(s.sP.h, quant_w[idx])
                t.wait_ge(s.sS.h, s.sS.v)
                a = None
                for j in range(nsub):
                    a = t.transpose(s.ps_T[:, j * 128:(j + 1) * 128],
                                    s.hmT[:, j * 128:(j + 1) * 128],
                                    s.ident[:])
                s.sPE.inc(a)
            s.te(_)
            hT_w[idx] = s.sPE.v

            def _(e):
                e.wait_ge(s.sPE.h, hT_w[idx])
                a = None
                for j in range(nsub):
                    a = e.copy(s.hmq[:, j * 128:(j + 1) * 128],
                               s.ps_T[:, j * 128:(j + 1) * 128])
                s.sS.inc(a)
            s.se(_)
            evac_w[idx] = s.sS.v

        def emit_down(idx):
            p0_, pn = pieces[idx]

            def _(t):
                t.wait_ge(s.sS.h, evac_w[idx])
                a = None
                for j in range(pn // 128):
                    ic = p0_ // 128 + j
                    for half in range(2):
                        a = t.matmul(s.ps_B[:, half * 512:(half + 1) * 512],
                                     s.hmq[:, j * 128:(j + 1) * 128],
                                     s.wd_q[:, ic * 1024 + half * 512:
                                            ic * 1024 + (half + 1) * 512],
                                     start=(idx == 0 and j == 0), stop=True,
                                     skip_group_check=True)
                s.sPE.inc(a)
            s.te(_)

        for idx in range(nP + 1):
            if idx < nP:
                emit_gateup(idx)
                emit_tanh(idx)
            if idx >= 1:
                emit_hmidT(idx - 1)
            if idx < nP:
                emit_siluquant(idx)
            if idx >= 1:
                emit_down(idx - 1)

        def _(v):
            v.wait_ge(s.sPE.h, s.sPE.v)
            s.sV.inc(v.tensor_tensor(s.h2buf[:], s.ps_B[:], s.gbb[:], ALU.mult))
        s.ve(_)
        r2v = s.sV.v

        def _(p):
            p.wait_ge(s.sV.h, r2v)
            s.sP.inc(p.tensor_tensor(s.h2buf[:], s.h2buf[:], s.x1buf[:], ALU.add))
        s.po(_)

        def _(q):
            q.wait_ge(s.sP.h, s.sP.v)
            q.dma_start(s.out_d[row0:row0 + 128, :], s.h2buf[:]).then_inc(s.sOUT.h, 16)
            s.sOUT.v += 16
        s.qe(_)

    def seq_setup(self, seq):
        s = self
        with s.nc.Block() as blk:
            s.blk = blk
            p0rowb = s.bfD[0:1, :]
            for slot_, dst in ((0, s.cbb), (1, s.sbb), (2, s.gab), (3, s.gbb)):
                def _(q, slot_=slot_):
                    q.wait_ge(s.sV.h, s.sV.v)
                    q.wait_ge(s.sPE.h, s.sPE.v)
                    q.dma_start(p0rowb,
                                s.rows_d[seq:seq + 1, slot_ * D:(slot_ + 1) * D]
                                ).then_inc(s.sLD.h, 16)
                    s.sLD.v += 16
                s.qe(_)

                def _(t):
                    t.wait_ge(s.sLD.h, s.sLD.v)
                    t.matmul(s.ps_B[:, 0:512], s.onesb[:], p0rowb[:, 0:512],
                             start=True, stop=True, skip_group_check=True)
                    a = t.matmul(s.ps_B[:, 512:1024], s.onesb[:], p0rowb[:, 512:1024],
                                 start=True, stop=True, skip_group_check=True)
                    s.sPE.inc(a)
                s.te(_)

                def _(v, dst=dst):
                    v.wait_ge(s.sPE.h, s.sPE.v)
                    s.sV.inc(v.tensor_copy(dst[:], s.ps_B[:]))
                s.ve(_)

    def build(self):
        self.declare_io()
        with self.es:
            self.declare_persistent()
            self.phase0()
            self.phase1_mod()
            self.phase2_weights()
            self.declare_main_bufs()

            with self.nc.Block() as blk:
                self.blk = blk

                def _(v):
                    v.memset(
                        self.v_all[:].rearrange("p (s c) -> p s c", c=65)[:, :, 64:65],
                        1.0)
                    v.memset(self.stat[:], 0.0)
                self.ve(_)

                def _(q):
                    q.dma_start(self.xb[0][:],
                                self.x_in[0:128, :]).then_inc(self.sX.h, 16)
                    self.sX.v += 16
                self.qe(_)
                self.x_ld[0] = self.sX.v

            for seq in range(BPC):
                self.seq_setup(seq)
                with self.nc.Block() as blk:
                    self.blk = blk
                    for qt in range(NT):
                        self.emit_tile(seq, qt)

            with self.nc.Block() as blk:
                self.blk = blk

                def _(q):
                    q.wait_ge(self.sOUT.h, self.sOUT.v)
                self.qe(_)
        self.nc.compile()
        return self.nc


_NC_CACHE = {}


def _get_nc(debug_stop=None):
    if debug_stop not in _NC_CACHE:
        _NC_CACHE[debug_stop] = KB(debug_stop).build()
    return _NC_CACHE[debug_stop]


def kernel(L0_attn_x, L0_attn_segment_pos, L0_attn_Wq, L0_attn_Wk, L0_attn_Wv,
           L0_attn_Wf, L0_attn_bf, L0_ffn_w_gate, L0_ffn_w_up, L0_ffn_w_down,
           L0_c, L0_wmod, _debug_stop=None):
    nc = _get_nc(_debug_stop)

    assert np.asarray(L0_attn_segment_pos).shape[-1] == T
    x = np.ascontiguousarray(np.asarray(L0_attn_x, dtype=np.float32))
    ident = np.eye(128, dtype=ml_dtypes.bfloat16)
    jj, ii = np.meshgrid(np.arange(128), np.arange(128), indexing="ij")
    mnew = (jj <= ii).astype(ml_dtypes.bfloat16)
    mold = (jj >= ii).astype(ml_dtypes.bfloat16)
    shared = dict(
        wq=np.ascontiguousarray(np.asarray(L0_attn_Wq, dtype=np.float32)),
        wk=np.ascontiguousarray(np.asarray(L0_attn_Wk, dtype=np.float32)),
        wv=np.ascontiguousarray(np.asarray(L0_attn_Wv, dtype=np.float32)),
        wf=np.ascontiguousarray(np.asarray(L0_attn_Wf, dtype=np.float32)),
        bfv=np.ascontiguousarray(np.asarray(L0_attn_bf, dtype=np.float32)).reshape(1, D),
        wg=np.ascontiguousarray(np.asarray(L0_ffn_w_gate, dtype=np.float32)),
        wu=np.ascontiguousarray(np.asarray(L0_ffn_w_up, dtype=np.float32)),
        wd=np.ascontiguousarray(np.asarray(L0_ffn_w_down, dtype=np.float32)),
        wmod=np.ascontiguousarray(np.asarray(L0_wmod, dtype=np.float32)),
        ident=ident, mnew=np.ascontiguousarray(mnew), mold=np.ascontiguousarray(mold),
        identf=np.eye(BPC, dtype=np.float32),
    )
    c_full = np.asarray(L0_c, dtype=np.float32)
    in_maps = []
    for core in range(NCORES):
        m = dict(shared)
        m["x"] = np.ascontiguousarray(
            x[core * BPC:(core + 1) * BPC].reshape(BPC * T, D))
        m["c"] = np.ascontiguousarray(c_full[core * BPC:(core + 1) * BPC])
        in_maps.append(m)

    res = run_bass_kernel_spmd(nc, in_maps, core_ids=list(range(NCORES)))
    out = np.concatenate(
        [r["out"].reshape(BPC, T, D) for r in res.results], axis=0)
    return out.astype(np.float32)


# version 13
# speedup vs baseline: 1.0509x; 1.0155x over previous
import numpy as np
import ml_dtypes
from contextlib import ExitStack

import concourse.bacc as bacc
import concourse.mybir as mybir
from concourse.bass_utils import run_bass_kernel_spmd
from concourse import dve_ops
from concourse.dve_op import DveOp
from concourse.dve_spec import Spec, Src0, Src1, C0, C1, C2, maxx, minn
from concourse.dve_spec import lower as dve_lower
from concourse.dve_table_gen import dve_ver_for

F32 = mybir.dt.float32
BF16 = mybir.dt.bfloat16
AF = mybir.ActivationFunctionType
ALU = mybir.AluOpType
AX = mybir.AxisListType

B, T, D = 16, 2048, 1024
NCORES, BPC = 8, 2
NT = T // 128
NH, HD = 16, 64
II = 2816
NKC = D // 128
NIC = II // 128
EPS = 1e-5
PIECES = [(0, 512), (512, 512), (1024, 512), (1536, 512), (2048, 512), (2560, 256)]

C_RND1 = float(1.5 * 2.0**23)
C_RND2 = float(1.5 * 2.0**24)


def _rnd(x, g):
    return np.round(x / g) * g


def _ref_t1(in0, in1, s0, s1, imm2):
    return _rnd(np.clip(in0, -s0, s0), 1.0).astype(in0.dtype)


def _ref_t12(in0, in1, s0, s1, imm2):
    m4 = np.clip(in0, -s0, s0)
    return (_rnd(np.clip(in0 - m4, -s0, s0), 2.0) + in1).astype(in0.dtype)


def _ref_fin(in0, in1, s0, s1, imm2):
    t3 = ((in0 > s0).astype(in0.dtype) - (in0 < -s0).astype(in0.dtype)) * imm2
    return (t3 + in1).astype(in0.dtype)


_VER = dve_ver_for("TRN2")


def _mk_op(name, spec, ref):
    sha = dve_lower(spec, ver=_VER, perf_en=False, name=name).sha(_VER)
    return DveOp(name, spec, subdim=False, uops_sha={_VER: sha}, reference=ref)


_c4 = minn(maxx(Src0, C2), C0)
OP_T1 = _mk_op("ANT_FP4_T1", Spec(body=(_c4 + C1) - C1), _ref_t1)
_d2 = minn(maxx(Src0 - _c4, C2), C0)
OP_T12 = _mk_op("ANT_FP4_T12", Spec(body=((_d2 + C1) - C1) + Src1), _ref_t12)
_g3 = (Src0 > C0) - (Src0 < C1)
OP_FIN = _mk_op("ANT_FP4_FIN", Spec(body=_g3 * C2 + Src1), _ref_fin)


def _register_ops():
    victims = ["DVEHASH32L_SEED", "DVEHASH32L_BODY", "DVEHASH32L_DRAIN"]
    for op, victim in zip((OP_T1, OP_T12, OP_FIN), victims):
        if op.name in dve_ops._SUB_OPCODE_FOR_NAME:
            continue
        row = dve_ops._SUB_OPCODE_FOR_NAME.pop(victim)
        idx = next(i for i, o in enumerate(dve_ops.OPS) if o.name == victim)
        dve_ops.OPS[idx] = op
        dve_ops._SUB_OPCODE_FOR_NAME[op.name] = row
        dve_ops.CUSTOM_DVE_SPECS.pop(victim, None)
        dve_ops.CUSTOM_DVE_REFS.pop(victim, None)
        dve_ops.CUSTOM_DVE_SPECS[op.name] = op.spec
        dve_ops.CUSTOM_DVE_REFS[op.name] = op.ref


_register_ops()


_OUT_KW = ("out", "accum_out")


def _ap_rng(x):
    t = getattr(x, "tensor", None)
    name = getattr(t, "name", None)
    if name is None:
        return None
    try:
        off = x.offset
        reach = 1
        for step, count in list(x.ap)[1:]:
            reach += (count - 1) * step
        return (name, off, off + reach)
    except Exception:
        return (name, 0, 1 << 60)


def _overlaps(a, bs):
    for b in bs:
        if a[0] == b[0] and a[1] < b[2] and b[1] < a[2]:
            return True
    return False


class DrainWrap:

    _OPS = frozenset((
        "tensor_tensor", "tensor_scalar", "tensor_copy", "tensor_reduce",
        "reduce_max", "reduce_sum", "reciprocal", "memset", "_custom_dve",
        "scalar_tensor_tensor", "activation", "select", "copy",
    ))

    def __init__(self, eng, state):
        object.__setattr__(self, "_e", eng)
        object.__setattr__(self, "_st", state)

    def __getattr__(self, name):
        a = getattr(self._e, name)
        if name not in self._OPS:
            return a

        def f(*args, **kw):
            st = self._st
            rngs = [r for r in (_ap_rng(x) for x in args) if r] +                    [r for r in (_ap_rng(v) for v in kw.values()) if r]
            if name == "_custom_dve":
                outs = [_ap_rng(kw[k]) for k in _OUT_KW if kw.get(k) is not None]
            else:
                outs = [_ap_rng(args[0])]
                if kw.get("accum_out") is not None:
                    outs.append(_ap_rng(kw["accum_out"]))
            outs = [o for o in outs if o]
            ins = [r for r in rngs if r not in outs]
            if (any(_overlaps(o, st["w"]) or _overlaps(o, st["r"]) for o in outs)
                    or any(_overlaps(i, st["w"]) for i in ins)):
                self._e.drain()
                st["w"].clear()
                st["r"].clear()
            st["w"] += outs
            st["r"] += ins
            return a(*args, **kw)
        return f


class Ctr:
    def __init__(self, handle, step=1):
        self.h, self.v, self.step = handle, 0, step

    def inc(self, inst):
        inst.then_inc(self.h, self.step)
        self.v += self.step


def pv_off(h):
    b = h // 7
    return b * 512 + (h - 7 * b) * 65


class KB:

    def __init__(self, debug_stop=None):
        self.debug_stop = debug_stop
        self.nc = bacc.Bacc("TRN2", target_bir_lowering=False)
        self.es = ExitStack()
        self.blk = None
        self._haz = {e: {"w": [], "r": []} for e in ("v", "s", "p")}

    def te(self, f):
        self.blk.tensor(f)

    def ve(self, f):
        self.blk.vector(lambda e: f(DrainWrap(e, self._haz["v"])))

    def se(self, f):
        self.blk.scalar(lambda e: f(DrainWrap(e, self._haz["s"])))

    def po(self, f):
        self.blk.gpsimd(lambda e: f(DrainWrap(e, self._haz["p"])))

    def qe(self, f):
        self.blk.sync(f)

    def sbuf(self, name, shape, dt=F32, stack=None):
        return (stack or self.es).enter_context(self.nc.sbuf_tensor(name, shape, dt))

    def psum(self, name, shape, dt=F32):
        return self.es.enter_context(self.nc.psum_tensor(name, shape, dt))

    def sem(self, name, step=1):
        return Ctr(self.es.enter_context(self.nc.semaphore(name)), step)

    def declare_io(self):
        nc = self.nc
        self.x_in = nc.dram_tensor("x", [BPC * T, D], F32, kind="ExternalInput").ap()
        self.wq_in = nc.dram_tensor("wq", [D, D], F32, kind="ExternalInput").ap()
        self.wk_in = nc.dram_tensor("wk", [HD, D], F32, kind="ExternalInput").ap()
        self.wv_in = nc.dram_tensor("wv", [HD, D], F32, kind="ExternalInput").ap()
        self.wf_in = nc.dram_tensor("wf", [D, D], F32, kind="ExternalInput").ap()
        self.bf_in = nc.dram_tensor("bfv", [1, D], F32, kind="ExternalInput").ap()
        self.wg_in = nc.dram_tensor("wg", [II, D], F32, kind="ExternalInput").ap()
        self.wu_in = nc.dram_tensor("wu", [II, D], F32, kind="ExternalInput").ap()
        self.wd_in = nc.dram_tensor("wd", [D, II], F32, kind="ExternalInput").ap()
        self.c_in = nc.dram_tensor("c", [BPC, D], F32, kind="ExternalInput").ap()
        self.wmod_in = nc.dram_tensor("wmod", [D, 6 * D], F32, kind="ExternalInput").ap()
        self.ident_in = nc.dram_tensor("ident", [128, 128], BF16, kind="ExternalInput").ap()
        self.mnew_in = nc.dram_tensor("mnew", [128, 128], BF16, kind="ExternalInput").ap()
        self.mold_in = nc.dram_tensor("mold", [128, 128], BF16, kind="ExternalInput").ap()
        self.identf_in = nc.dram_tensor("identf", [BPC, BPC], F32, kind="ExternalInput").ap()
        self.out_d = nc.dram_tensor("out", [BPC * T, D], F32, kind="ExternalOutput").ap()

    def declare_persistent(self):
        sbuf, psum, sem = self.sbuf, self.psum, self.sem
        self.wq_T = sbuf("wq_T", [128, 64 * 128], BF16)
        self.kv_T = sbuf("kv_T", [128, 8 * 128], BF16)
        self.kv2_T = sbuf("kv2_T", [128, 8 * 128], BF16)
        self.wf_T = sbuf("wf_T", [128, 8 * 1024], BF16)
        self.wg_q = sbuf("wg_q", [128, 8 * II], BF16)
        self.wu_q = sbuf("wu_q", [128, 8 * II], BF16)
        self.wd_q = sbuf("wd_q", [128, NIC * 1024], BF16)
        self.ident = sbuf("ident_s", [128, 128], BF16)
        self.mnew = sbuf("mnew_s", [128, 128], BF16)
        self.mold = sbuf("mold_s", [128, 128], BF16)
        self.onesb = self.mnew[0:1, :]
        self.identf = sbuf("identf_s", [BPC, BPC], F32)
        self.bfrow = sbuf("bfrow", [1, D], BF16)
        self.consts = sbuf("consts", [128, 4], F32)
        self.ca1c = sbuf("ca1c", [128, NKC * BPC], F32)
        self.sac = sbuf("sac", [128, NKC * BPC], F32)
        self.rows_d = self.nc.dram_tensor("rows_d", [BPC, 4 * D], BF16).ap()
        self.cbb = sbuf("cbb", [128, D], BF16)
        self.sbb = sbuf("sbb", [128, D], BF16)
        self.gab = sbuf("gab", [128, D], BF16)
        self.gbb = sbuf("gbb", [128, D], BF16)
        self.ps_A = psum("ps_A", [128, 1024], F32)
        self.ps_B = psum("ps_B", [128, 1024], F32)
        self.ps_pv = psum("ps_pv", [128, 1280], F32)
        self.ps_T = psum("ps_T", [128, 512], BF16)
        self.sLD = sem("sLD", 16)
        self.sL2 = sem("sL2", 16)
        self.sL3 = sem("sL3", 16)
        self.sL4 = sem("sL4", 16)
        self.sX = sem("sX", 16)
        self.sX2 = sem("sX2", 16)
        self.sOUT = sem("sOUT", 16)
        self.sPE = sem("sPE")
        self.sV = sem("sV")
        self.sS = sem("sS")
        self.sP = sem("sP")

    def declare_main_bufs(self):
        sbuf = self.sbuf
        self.xbuf = sbuf("xbuf", [128, D], F32)
        self.xbuf2 = sbuf("xbuf2", [128, D], F32)
        self.xb = [self.xbuf, self.xbuf2]
        self.x_ld = [0, 0]
        self.res1_cnt = {}
        self.head_mod = {}
        self.prev_psfree = 0
        self.r2v_prev = 0
        self.x1buf = sbuf("x1buf", [128, D], F32)
        self.h2buf = sbuf("h2buf", [128, D], F32)
        self.bfA = sbuf("bfA", [128, D], BF16)
        self.bfB = sbuf("bfB", [128, D], BF16)
        self.bfC = sbuf("bfC", [128, D], BF16)
        self.bfD = sbuf("bfD", [128, D], BF16)
        self.bfE = sbuf("bfE", [128, D], BF16)
        self.probsT = None
        self.kk = sbuf("kk", [128, 4 * 128], BF16)
        self.v_all = sbuf("v_all", [128, 4 * 65], BF16)
        self.stat = sbuf("stat", [128, 16], F32)
        self.bmm = sbuf("bmm", [128, 64], F32)
        self.qr2 = sbuf("qr2", [128, 64], F32)
        self.qsh = sbuf("qsh", [128, 64], BF16)
        self.sums = self.bmm[:, 48:64]
        self.rsum = self.qr2[:, 48:64]
        self.probsT2 = self.bfD
        self.probsT = self.bfB
        self.tf_a = self.h2buf[:, 512:1024]
        self.tf_b = self.h2buf[:, 0:512]
        self.hmT = self.bfD[:, 0:512]
        self.hmq = self.bfD[:, 512:1024]

    def phase0(self):
        s = self
        with s.nc.Block() as blk:
            s.blk = blk

            def _(q):
                q.dma_start(s.ident[:], s.ident_in).then_inc(s.sLD.h, 16)
                q.dma_start(s.mnew[:], s.mnew_in).then_inc(s.sLD.h, 16)
                q.dma_start(s.mold[:], s.mold_in).then_inc(s.sLD.h, 16)
                q.dma_start(s.identf[:], s.identf_in).then_inc(s.sLD.h, 16)
                s.sLD.v += 64
            s.qe(_)

            def _(sc):
                sc.preload_act_table(AF.Exp)
            s.se(_)

            def _(v):
                v.memset(s.consts[:, 0:1], -30.0)
                v.memset(s.consts[:, 1:2], -0.5)
            s.ve(_)

    def phase1_mod(self):
        s = self
        with ExitStack() as ph:
            c_sb = s.sbuf("c_sb", [BPC, D], F32, ph)
            sct = s.sbuf("sct", [BPC, D], F32, ph)
            sc_bf = s.sbuf("sc_bf", [BPC, D], BF16, ph)
            rowstg = s.sbuf("rowstg", [BPC, D], BF16, ph)
            scT = s.sbuf("scT", [128, NKC * BPC], F32, ph)
            wmod_f = s.sbuf("wmod_f", [128, 1024], F32, ph)
            wmod_g = s.sbuf("wmod_g", [128, 1024], F32, ph)
            wmod_h = s.sbuf("wmod_h", [128, 1024], F32, ph)
            wmod_i = s.sbuf("wmod_i", [128, 1024], F32, ph)
            tmpr = wmod_f[0:BPC, :]

            with s.nc.Block() as blk:
                s.blk = blk

                def _(q):
                    q.dma_start(c_sb[:], s.c_in).then_inc(s.sLD.h, 16)
                    q.dma_start(sct[0:1, :], s.bf_in).then_inc(s.sLD.h, 16)
                    s.sLD.v += 32
                s.qe(_)

                def _(v):
                    v.wait_ge(s.sLD.h, s.sLD.v)
                    s.sV.inc(v.tensor_copy(s.bfrow[:], sct[0:1, :]))
                s.ve(_)

                def _(sc):
                    sc.wait_ge(s.sV.h, s.sV.v)
                    s.sS.inc(sc.activation(sct[:], c_sb[:], AF.Tanh, scale=0.5))
                s.se(_)

                def _(v):
                    v.wait_ge(s.sS.h, s.sS.v)
                    v.tensor_scalar(sct[:], sct[:], 0.5, 0.5, ALU.mult, ALU.add)
                    s.sV.inc(v.tensor_tensor(sct[:], sct[:], c_sb[:], ALU.mult))
                s.ve(_)

                def _(t):
                    t.wait_ge(s.sV.h, s.sV.v)
                    a = None
                    for kc in range(NKC):
                        a = t.transpose(s.ps_A[:, kc * BPC:(kc + 1) * BPC],
                                        sct[:, kc * 128:(kc + 1) * 128],
                                        s.identf[:])
                    s.sPE.inc(a)
                s.te(_)

                def _(v):
                    v.wait_ge(s.sPE.h, s.sPE.v)
                    s.sV.inc(v.tensor_copy(scT[:], s.ps_A[:, 0:NKC * BPC]))
                s.ve(_)

                WNB = 4
                wbufs = [wmod_f, wmod_g, wmod_h, wmod_i]
                wlds = [s.sLD, s.sL2, s.sL3, s.sL4]
                wqes = [s.qe, s.po, s.qe, s.po]
                wm_ld = [0] * WNB
                wm_pe = [0] * WNB
                wi = 0
                for pair in range(3):
                    for sub in range(2):
                        vec = pair * 2 + sub
                        ps = s.ps_A if sub == 0 else s.ps_B
                        for kc in range(NKC):
                            buf, ld = wbufs[wi % WNB], wlds[wi % WNB]
                            sv_now, ss_now = s.sV.v, s.sS.v

                            def _(q, vec=vec, kc=kc, wi=wi, ld=ld, buf=buf,
                                  sv_now=sv_now, ss_now=ss_now):
                                q.wait_ge(s.sPE.h, wm_pe[wi % WNB])
                                q.wait_ge(s.sV.h, sv_now)
                                q.wait_ge(s.sS.h, ss_now)
                                q.dma_start(buf[:],
                                            s.wmod_in[kc * 128:(kc + 1) * 128,
                                                      vec * D:(vec + 1) * D]
                                            ).then_inc(ld.h, 16)
                                ld.v += 16
                            wqes[wi % WNB](_)
                            wm_ld[wi % WNB] = ld.v

                            def _(t, ps=ps, kc=kc, buf=buf, wi=wi, ld=ld,
                                  sv_now=sv_now):
                                t.wait_ge(ld.h, wm_ld[wi % WNB])
                                t.wait_ge(s.sV.h, sv_now)
                                a = None
                                for piece in range(2):
                                    a = t.matmul(ps[0:BPC, piece * 512:(piece + 1) * 512],
                                                 scT[:, kc * BPC:(kc + 1) * BPC],
                                                 buf[:, piece * 512:(piece + 1) * 512],
                                                 start=(kc == 0), stop=(kc == NKC - 1),
                                                 skip_group_check=True)
                                s.sPE.inc(a)
                            s.te(_)
                            wm_pe[wi % WNB] = s.sPE.v
                            wi += 1

                    def _(v):
                        v.wait_ge(s.sPE.h, s.sPE.v)
                        v.tensor_copy(c_sb[:], s.ps_A[0:BPC, :])
                        s.sV.inc(v.tensor_copy(sct[:], s.ps_B[0:BPC, :]))
                    s.ve(_)

                    if pair == 0:
                        for rowbuf, dst, add1 in ((c_sb, s.sac, 0.0), (sct, s.ca1c, 1.0)):
                            for kc in range(NKC):
                                def _(t, rowbuf=rowbuf, kc=kc):
                                    t.wait_ge(s.sV.h, s.sV.v)
                                    s.sPE.inc(t.transpose(
                                        s.ps_B[:, 512:512 + BPC],
                                        rowbuf[:, kc * 128:(kc + 1) * 128],
                                        s.identf[:]))
                                s.te(_)

                                def _(v, dst=dst, add1=add1, kc=kc):
                                    v.wait_ge(s.sPE.h, s.sPE.v)
                                    a = None
                                    for sq in range(BPC):
                                        a = v.tensor_scalar(
                                            dst[:, kc * BPC + sq: kc * BPC + sq + 1],
                                            s.ps_B[:, 512 + sq:512 + sq + 1],
                                            add1, None, ALU.add)
                                    s.sV.inc(a)
                                s.ve(_)
                    elif pair == 1:
                        def _(sc):
                            sc.wait_ge(s.sV.h, s.sV.v)
                            s.sS.inc(sc.activation(tmpr, c_sb[:], AF.Sign))
                        s.se(_)

                        def _(v):
                            v.wait_ge(s.sS.h, s.sS.v)
                            v.tensor_tensor(c_sb[:], c_sb[:], tmpr, ALU.mult)
                            v.tensor_scalar(c_sb[:], c_sb[:], 0.25, None, ALU.max)
                            v.tensor_tensor(sc_bf[:], c_sb[:], tmpr, ALU.mult)
                            s.sV.inc(v.tensor_copy(rowstg[:], sct[:]))
                        s.ve(_)

                        def _(q):
                            q.wait_ge(s.sV.h, s.sV.v)
                            q.dma_start(s.rows_d[:, 2 * D:3 * D], sc_bf[:]).then_inc(s.sX.h, 16)
                            q.dma_start(s.rows_d[:, 1 * D:2 * D], rowstg[:]).then_inc(s.sX.h, 16)
                            s.sX.v += 32
                            q.wait_ge(s.sX.h, s.sX.v)
                        s.qe(_)
                    else:
                        def _(sc):
                            sc.wait_ge(s.sV.h, s.sV.v)
                            s.sS.inc(sc.activation(tmpr, sct[:], AF.Sign))
                        s.se(_)

                        def _(v):
                            v.wait_ge(s.sS.h, s.sS.v)
                            v.wait_ge(s.sX.h, s.sX.v)
                            v.tensor_scalar(sc_bf[:], c_sb[:], 1.0, None, ALU.add)
                            v.tensor_tensor(sct[:], sct[:], tmpr, ALU.mult)
                            v.tensor_scalar(sct[:], sct[:], 0.25, None, ALU.max)
                            s.sV.inc(v.tensor_tensor(rowstg[:], sct[:], tmpr, ALU.mult))
                        s.ve(_)

                        def _(q):
                            q.wait_ge(s.sV.h, s.sV.v)
                            q.dma_start(s.rows_d[:, 0 * D:1 * D], sc_bf[:]).then_inc(s.sX.h, 16)
                            q.dma_start(s.rows_d[:, 3 * D:4 * D], rowstg[:]).then_inc(s.sX.h, 16)
                            s.sX.v += 32
                            q.wait_ge(s.sX.h, s.sX.v)
                        s.qe(_)

    def emit_quant(self, v, src_f32, F, out_bf, w_scr, t1_bf, t12_bf,
                   bm, r2, sh, fold=1.0):
        nb = F // 16
        src = src_f32[:, 0:F]
        v.tensor_reduce(bm[:, 0:nb], src.rearrange("p (n k) -> p n k", k=16),
                        AX.X, ALU.max, apply_absolute_value=True)
        v.tensor_scalar(bm[:, 0:nb], bm[:, 0:nb], 1.0 / 12.0, 5e-13, ALU.mult, ALU.max)
        v.tensor_scalar(sh[:, 0:nb], bm[:, 0:nb], fold, None, ALU.mult)
        v.reciprocal(r2[:, 0:nb], bm[:, 0:nb])
        rb = r2[:, 0:nb].unsqueeze(2).broadcast_to([128, nb, 16])
        v.tensor_tensor(w_scr[:, 0:F].rearrange("p (n k) -> p n k", k=16),
                        src.rearrange("p (n k) -> p n k", k=16), rb, ALU.mult)
        v._custom_dve(OP_T1, out=t1_bf[:, 0:F], in0=w_scr[:, 0:F],
                      s0=4.0, s1=C_RND1, imm2=-4.0)
        v._custom_dve(OP_T12, out=t12_bf[:, 0:F], in0=w_scr[:, 0:F],
                      in1=t1_bf[:, 0:F], s0=4.0, s1=C_RND2, imm2=-4.0)
        v._custom_dve(OP_FIN, out=t1_bf[:, 0:F], in0=w_scr[:, 0:F],
                      in1=t12_bf[:, 0:F], s0=10.0, s1=-10.0, imm2=4.0)
        shb = sh[:, 0:nb].unsqueeze(2).broadcast_to([128, nb, 16])
        return v.tensor_tensor(out_bf[:, 0:F].rearrange("p (n k) -> p n k", k=16),
                               t1_bf[:, 0:F].rearrange("p (n k) -> p n k", k=16),
                               shb, ALU.mult)

    def emit_quant2(self, src_f32, F, out_bf, w_scr, t1_bf, t12_bf,
                    bm, r2, sh, fold=1.0, pre=None):
        s = self
        nb = F // 16

        def _(v):
            if pre is not None:
                pre(v)
            srcv = src_f32[:, 0:F]
            v.tensor_reduce(bm[:, 0:nb], srcv.rearrange("p (n k) -> p n k", k=16),
                            AX.X, ALU.max, apply_absolute_value=True)
            v.tensor_scalar(bm[:, 0:nb], bm[:, 0:nb], 1.0 / 12.0, 5e-13,
                            ALU.mult, ALU.max)
            v.tensor_scalar(sh[:, 0:nb], bm[:, 0:nb], fold, None, ALU.mult)
            s.sV.inc(v.reciprocal(r2[:, 0:nb], bm[:, 0:nb]))
        s.ve(_)

        pe_now = s.sPE.v

        def _(p):
            p.wait_ge(s.sV.h, s.sV.v)
            p.wait_ge(s.sPE.h, pe_now)
            rb = r2[:, 0:nb].unsqueeze(2).broadcast_to([128, nb, 16])
            s.sP.inc(p.tensor_tensor(
                w_scr[:, 0:F].rearrange("p (n k) -> p n k", k=16),
                src_f32[:, 0:F].rearrange("p (n k) -> p n k", k=16),
                rb, ALU.mult))
        s.po(_)

        def _(v):
            v.wait_ge(s.sP.h, s.sP.v)
            v._custom_dve(OP_T1, out=t1_bf[:, 0:F], in0=w_scr[:, 0:F],
                          s0=4.0, s1=C_RND1, imm2=-4.0)
            v._custom_dve(OP_T12, out=t12_bf[:, 0:F], in0=w_scr[:, 0:F],
                          in1=t1_bf[:, 0:F], s0=4.0, s1=C_RND2, imm2=-4.0)
            s.sV.inc(v._custom_dve(OP_FIN, out=t1_bf[:, 0:F], in0=w_scr[:, 0:F],
                                   in1=t12_bf[:, 0:F], s0=10.0, s1=-10.0, imm2=4.0))
        s.ve(_)

        def _(p):
            p.wait_ge(s.sV.h, s.sV.v)
            p.wait_ge(s.sPE.h, pe_now)
            shb = sh[:, 0:nb].unsqueeze(2).broadcast_to([128, nb, 16])
            s.sP.inc(p.tensor_tensor(
                out_bf[:, 0:F].rearrange("p (n k) -> p n k", k=16),
                t1_bf[:, 0:F].rearrange("p (n k) -> p n k", k=16),
                shb, ALU.mult))
        s.po(_)
        return s.sP.v

    def transpose_blocks(self, src_bf, n_blocks, dst, dst_offs, evac="v",
                         wait_p=None):
        s = self
        for g0 in range(0, n_blocks, 4):
            gn = min(4, n_blocks - g0)

            def _(t, g0=g0, gn=gn):
                t.wait_ge(s.sV.h, s.sV.v)
                if wait_p is not None:
                    t.wait_ge(s.sP.h, wait_p)
                if evac == "s":
                    t.wait_ge(s.sS.h, s.sS.v)
                a = None
                for j in range(gn):
                    a = t.transpose(s.ps_T[:, j * 128:(j + 1) * 128],
                                    src_bf[:, (g0 + j) * 128:(g0 + j + 1) * 128],
                                    s.ident[:])
                s.sPE.inc(a)
            s.te(_)

            def _(e, g0=g0, gn=gn):
                e.wait_ge(s.sPE.h, s.sPE.v)
                cp = e.tensor_copy if evac == "v" else e.copy
                a = None
                for j in range(gn):
                    a = cp(dst[:, dst_offs[g0 + j]:dst_offs[g0 + j] + 128],
                           s.ps_T[:, j * 128:(j + 1) * 128])
                (s.sV if evac == "v" else s.sS).inc(a)
            (s.ve if evac == "v" else s.se)(_)

    def phase2_weights(self):
        s = self
        with ExitStack() as ph:
            WHALF = II // 2
            stg = s.sbuf("stg", [128, WHALF], F32, ph)
            stg2 = s.sbuf("stg2", [128, WHALF], F32, ph)
            stg3 = s.sbuf("stg3", [128, WHALF], F32, ph)
            stq = s.sbuf("stq", [128, WHALF], BF16, ph)
            qt1 = s.sbuf("qt1", [128, WHALF], BF16, ph)
            qt12 = s.sbuf("qt12", [128, WHALF], BF16, ph)
            bmw = s.sbuf("bmw", [128, WHALF // 16], F32, ph)
            qr2w = s.sbuf("qr2w", [128, WHALF // 16], F32, ph)
            qshw = s.sbuf("qshw", [128, WHALF // 16], BF16, ph)

            with s.nc.Block() as blk:
                s.blk = blk
                NB = 3
                bufs = [stg, stg2, stg3]
                lds = [s.sLD, s.sL2, s.sL3]
                qes = [s.qe, s.po, s.qe]
                st_state = {"i": 0, "ld": [0] * NB, "c": [None] * NB}

                def load_rows(src_ap, ncols):
                    i = st_state["i"]
                    buf, ld = bufs[i % NB], lds[i % NB]
                    gate = st_state["c"][i % NB]
                    if gate is None:
                        gate = s.sV.v

                    def _(q, ld=ld, gate=gate):
                        q.wait_ge(s.sV.h, gate)
                        q.dma_start(buf[:, 0:ncols], src_ap).then_inc(ld.h, 16)
                        ld.v += 16
                    qes[i % NB](_)
                    st_state["ld"][i % NB] = ld.v
                    st_state["i"] += 1
                    return buf

                def cast_stage(buf, ncols):
                    i = st_state["i"] - 1
                    ld = lds[i % NB]
                    thr = st_state["ld"][i % NB]
                    pe_now = s.sPE.v

                    def _(v, ld=ld, thr=thr, pe_now=pe_now):
                        v.wait_ge(ld.h, thr)
                        v.wait_ge(s.sPE.h, pe_now)
                        s.sV.inc(v.tensor_copy(stq[:, 0:ncols], buf[:, 0:ncols]))
                    s.ve(_)
                    st_state["c"][i % NB] = s.sV.v

                def quant_stage(buf, ncols, fold):
                    i = st_state["i"] - 1
                    ld = lds[i % NB]
                    thr = st_state["ld"][i % NB]
                    sp_now = s.sP.v

                    def _pre(v, ld=ld, thr=thr, sp_now=sp_now):
                        v.wait_ge(ld.h, thr)
                        v.wait_ge(s.sP.h, sp_now)
                    qw = s.emit_quant2(buf, ncols, stq, buf, qt1, qt12,
                                       bmw, qr2w, qshw, fold, pre=_pre)
                    st_state["c"][i % NB] = s.sV.v
                    return qw

                for oc in range(NKC):
                    b = load_rows(s.wq_in[oc * 128:(oc + 1) * 128, :], D)
                    cast_stage(b, D)
                    s.transpose_blocks(stq, NKC, s.wq_T,
                                       [(oc * 8 + kc) * 128 for kc in range(NKC)],
                                       evac="s")

                kvi = st_state["i"]
                kvbuf, kvld = bufs[kvi % NB], lds[kvi % NB]
                kvgate = st_state["c"][kvi % NB]
                if kvgate is None:
                    kvgate = s.sV.v

                def _(q, kvld=kvld, kvgate=kvgate):
                    q.wait_ge(s.sV.h, kvgate)
                    q.dma_start(kvbuf[0:HD, 0:D], s.wk_in).then_inc(kvld.h, 16)
                    q.dma_start(kvbuf[64:128, 0:D], s.wv_in).then_inc(kvld.h, 16)
                    kvld.v += 32
                qes[kvi % NB](_)
                st_state["ld"][kvi % NB] = kvld.v
                st_state["i"] += 1
                cast_stage(kvbuf, D)
                for kc in range(NKC):
                    def _(t, kc=kc):
                        t.wait_ge(s.sV.h, s.sV.v)
                        t.wait_ge(s.sS.h, s.sS.v)
                        s.sPE.inc(t.transpose(s.ps_T[:, 0:128],
                                              stq[:, kc * 128:(kc + 1) * 128],
                                              s.ident[:]))
                    s.te(_)

                    def _(v, kc=kc):
                        v.wait_ge(s.sPE.h, s.sPE.v)
                        v.tensor_copy(s.kv_T[:, kc * 128:(kc + 1) * 128], s.ps_T[:, 0:128])
                        v.tensor_copy(s.kv2_T[:, kc * 128:kc * 128 + 64],
                                      s.ps_T[:, 64:128])
                        s.sV.inc(v.tensor_copy(s.kv2_T[:, kc * 128 + 64:(kc + 1) * 128],
                                               s.ps_T[:, 0:64]))
                    s.ve(_)

                for oc in range(NKC):
                    b = load_rows(s.wf_in[oc * 128:(oc + 1) * 128, :], D)
                    cast_stage(b, D)
                    s.transpose_blocks(stq, NKC, s.wf_T,
                                       [kc * 1024 + oc * 128 for kc in range(NKC)],
                                       evac="s")

                for w_src, w_dst, fold in ((s.wg_in, s.wg_q, 1.0), (s.wu_in, s.wu_q, 0.5)):
                    for ic in range(NIC):
                        b = load_rows(w_src[ic * 128:(ic + 1) * 128, :], D)
                        qw = quant_stage(b, D, fold)
                        s.transpose_blocks(stq, NKC, w_dst,
                                           [kc * II + ic * 128 for kc in range(NKC)],
                                           evac="s", wait_p=qw)

                for oc in range(NKC):
                    for hf in range(2):
                        b = load_rows(s.wd_in[oc * 128:(oc + 1) * 128,
                                              hf * WHALF:(hf + 1) * WHALF], WHALF)
                        qw = quant_stage(b, WHALF, 1.0)
                        s.transpose_blocks(stq, WHALF // 128, s.wd_q,
                                           [(hf * 11 + ic) * 1024 + oc * 128
                                            for ic in range(WHALF // 128)],
                                           evac="s", wait_p=qw)

    def emit_ln(self, src, dst, wait_out=False):
        s = self

        def _(v):
            s.sV.inc(v.tensor_reduce(s.stat[:, 0:1], src[:], AX.X, ALU.add))
        s.ve(_)

        def _(sc):
            sc.wait_ge(s.sV.h, s.sV.v)
            s.sS.inc(sc.activation(s.bfE[:], src[:], AF.Square,
                                   accum_out=s.stat[:, 3:4]))
        s.se(_)

        def _(v):
            v.wait_ge(s.sS.h, s.sS.v)
            v.tensor_scalar(s.stat[:, 1:2], s.stat[:, 0:1], 1.0 / D, None, ALU.mult)
            v.tensor_scalar(s.stat[:, 2:3], s.stat[:, 1:2], s.stat[:, 1:2], -EPS,
                            ALU.mult, ALU.add)
            s.sV.inc(v.tensor_scalar(s.stat[:, 4:5], s.stat[:, 3:4], 1.0 / D,
                                     s.stat[:, 2:3], ALU.mult, ALU.subtract))
        s.ve(_)

        def _(p):
            p.wait_ge(s.sV.h, s.sV.v)
            s.sP.inc(p.tensor_tensor(s.stat[:, 5:6], s.stat[:, 4:5],
                                     s.consts[:, 1:2], ALU.pow))
        s.po(_)

        def _(v):
            v.wait_ge(s.sP.h, s.sP.v)
            if wait_out:
                v.wait_ge(s.sOUT.h, s.sOUT.v)
            s.sV.inc(v.tensor_scalar(dst[:], src[:], s.stat[:, 1:2], s.stat[:, 5:6],
                                     ALU.subtract, ALU.mult))
        s.ve(_)


    def emit_head(self, nti):
        s = self
        xb = s.xb[nti % 2]
        xsem = s.sX if nti % 2 == 0 else s.sX2
        xw = s.x_ld[nti % 2]
        seq = nti // NT
        sp_now = s.sP.v

        def _(v):
            v.wait_ge(xsem.h, xw)
            s.sV.inc(v.tensor_reduce(s.stat[:, 8:9], xb[:], AX.X, ALU.add))
        s.ve(_)

        def _(sc):
            sc.wait_ge(xsem.h, xw)
            sc.wait_ge(s.sP.h, sp_now)
            s.sS.inc(sc.activation(s.bfE[:], xb[:], AF.Square,
                                   accum_out=s.stat[:, 11:12]))
        s.se(_)

        def _(v):
            v.wait_ge(s.sS.h, s.sS.v)
            v.tensor_scalar(s.stat[:, 9:10], s.stat[:, 8:9], 1.0 / D, None,
                            ALU.mult)
            v.tensor_scalar(s.stat[:, 10:11], s.stat[:, 9:10], s.stat[:, 9:10],
                            -EPS, ALU.mult, ALU.add)
            s.sV.inc(v.tensor_scalar(s.stat[:, 12:13], s.stat[:, 11:12], 1.0 / D,
                                     s.stat[:, 10:11], ALU.mult, ALU.subtract))
        s.ve(_)

        def _(p):
            p.wait_ge(s.sV.h, s.sV.v)
            s.sP.inc(p.tensor_tensor(s.stat[:, 13:14], s.stat[:, 12:13],
                                     s.consts[:, 1:2], ALU.pow))
        s.po(_)
        pw = s.sP.v

        def _(v):
            v.wait_ge(s.sP.h, pw)
            v.wait_ge(s.sPE.h, s.sPE.v)
            s.sV.inc(v.tensor_scalar(s.bfA[:], xb[:], s.stat[:, 9:10],
                                     s.stat[:, 13:14], ALU.subtract, ALU.mult))
        s.ve(_)

        s.transpose_blocks(s.bfA, NKC, s.bfB, [kc * 128 for kc in range(NKC)],
                           evac="s")

        def _(v):
            v.wait_ge(s.sS.h, s.sS.v)
            a = None
            for kc in range(NKC):
                a = v.tensor_scalar(s.bfB[:, kc * 128:(kc + 1) * 128],
                                    s.bfB[:, kc * 128:(kc + 1) * 128],
                                    s.ca1c[:, kc * BPC + seq:kc * BPC + seq + 1],
                                    s.sac[:, kc * BPC + seq:kc * BPC + seq + 1],
                                    ALU.mult, ALU.add)
            s.sV.inc(a)
        s.ve(_)
        s.head_mod[nti] = s.sV.v

    def _dbg_dump(self, src_ap, row0):
        s = self

        def _(v):
            s.sV.inc(v.tensor_copy(s.h2buf[:], src_ap))
        s.ve(_)

        def _(q):
            q.wait_ge(s.sV.h, s.sV.v)
            q.wait_ge(s.sOUT.h, s.sOUT.v)
            q.dma_start(s.out_d[row0:row0 + 128, :], s.h2buf[:]).then_inc(s.sOUT.h, 16)
            s.sOUT.v += 16
        s.qe(_)

    def emit_tile(self, seq, qt):
        s = self
        row0 = seq * T + qt * 128
        ti = seq * NT + qt
        kts = [k for k in (qt - 2, qt - 1, qt) if k >= 0]
        slot = qt % 4
        s.xbuf = s.xb[ti % 2]

        if ti + 1 < BPC * NT:
            nti = ti + 1
            nrow = (nti // NT) * T + (nti % NT) * 128
            gate = s.res1_cnt.get(ti - 1)
            sem2 = s.sX if nti % 2 == 0 else s.sX2

            def _(q, nti=nti, nrow=nrow, gate=gate, sem2=sem2):
                if gate is None:
                    q.wait_ge(s.sV.h, s.sV.v)
                else:
                    q.wait_ge(s.sP.h, gate)
                q.dma_start(s.xb[nti % 2][:],
                            s.x_in[nrow:nrow + 128, :]).then_inc(sem2.h, 16)
                sem2.v += 16
            s.qe(_)
            s.x_ld[nti % 2] = sem2.v

        xsem = s.sX if ti % 2 == 0 else s.sX2
        xw = s.x_ld[ti % 2]

        def _(v):
            v.wait_ge(xsem.h, xw)
        s.ve(_)

        def _(sc):
            sc.wait_ge(xsem.h, xw)
        s.se(_)

        if ti == 0 and s.debug_stop != "noop":
            s.emit_head(0)

        if s.debug_stop == "noop":
            def _(v):
                s.sV.inc(v.tensor_copy(s.h2buf[:], s.xbuf[:]))
            s.ve(_)

            def _(q):
                q.wait_ge(s.sV.h, s.sV.v)
                q.wait_ge(s.sOUT.h, s.sOUT.v)
                q.dma_start(s.out_d[row0:row0 + 128, :], s.h2buf[:]).then_inc(s.sOUT.h, 16)
                s.sOUT.v += 16
            s.qe(_)
            return

        if s.debug_stop == "ln1":
            self._dbg_dump(s.bfB[:], row0)
            return

        qkv_gate = max(s.head_mod.get(ti, s.sV.v), s.prev_psfree)

        def _(t, qkv_gate=qkv_gate):
            t.wait_ge(s.sV.h, qkv_gate)
            a = None
            for mc in range(8):
                for kc in range(NKC):
                    a = t.matmul(s.ps_A[:, mc * 128:(mc + 1) * 128],
                                 s.wq_T[:, (mc * 8 + kc) * 128:(mc * 8 + kc + 1) * 128],
                                 s.bfB[:, kc * 128:(kc + 1) * 128],
                                 start=(kc == 0 and mc % 4 == 0), stop=True,
                                 skip_group_check=True)
            for kc in range(NKC):
                a = t.matmul(s.ps_pv[:, 0:128],
                             s.kv_T[:, kc * 128:(kc + 1) * 128],
                             s.bfB[:, kc * 128:(kc + 1) * 128],
                             start=(kc == 0), stop=True, skip_group_check=True)
            for kc in range(NKC):
                a = t.matmul(s.ps_pv[:, 128:256],
                             s.kv2_T[:, kc * 128:(kc + 1) * 128],
                             s.bfB[:, kc * 128:(kc + 1) * 128],
                             start=False, stop=True, skip_group_check=True)
            s.sPE.inc(a)
        s.te(_)

        def _(e):
            e.wait_ge(s.sPE.h, s.sPE.v)
            e.copy(s.bfC[:], s.ps_A[:])
            e.copy(s.kk[0:64, slot * 128:(slot + 1) * 128], s.ps_pv[0:64, 0:128])
            e.copy(s.kk[64:128, slot * 128:(slot + 1) * 128],
                   s.ps_pv[64:128, 128:256])
            s.sS.inc(e.copy(s.bfA[64:128, 0:128], s.ps_pv[64:128, 0:128]))
        s.se(_)

        def _(t):
            t.wait_ge(s.sS.h, s.sS.v)
            s.sPE.inc(t.transpose(s.ps_T[:, 0:64], s.bfA[64:128, 0:128],
                                  s.ident[64:128, 64:128]))
        s.te(_)

        def _(e):
            e.wait_ge(s.sPE.h, s.sPE.v)
            s.sS.inc(e.copy(s.v_all[:, slot * 65:slot * 65 + 64],
                            s.ps_T[:, 0:64]))
        s.se(_)

        if s.debug_stop == "qkv":
            self._dbg_dump(s.bfC[:], row0)
            return

        groups = []
        for kt in kts:
            mask = s.mnew if kt == qt else (s.mold if kt == qt - 2 else None)
            for grp in range(2):
                groups.append((kt, grp, mask))
        exp_w, msk_w = {}, {}

        def emit_scores(gi):
            kt, grp, mask = groups[gi]
            kslot = kt % 4
            ps = s.ps_A if gi % 2 == 0 else s.ps_B
            pbuf = s.probsT[:, 0:1024] if gi % 2 == 0 else s.probsT2[:, 0:1024]

            def _(t):
                if gi == 0:
                    t.wait_ge(s.sV.h, s.sV.v)
                    t.wait_ge(s.sS.h, s.sS.v)
                if gi == 1:
                    t.wait_ge(s.sV.h, s.r2v_prev)
                if gi >= 2:
                    t.wait_ge(s.sS.h, exp_w[gi - 2])
                a = None
                for h8 in range(8):
                    h = grp * 8 + h8
                    base = (h % 2) * 64
                    col = (h8 % 2) * 512 + (h8 // 2) * 128
                    a = t.matmul(ps[:, col:col + 128],
                                 s.kk[base:base + 64,
                                      kslot * 128:(kslot + 1) * 128],
                                 s.bfC[base:base + 64,
                                       (h // 2) * 128:(h // 2 + 1) * 128],
                                 start=(h8 in (0, 1)), stop=True,
                                 skip_group_check=True)
                s.sPE.inc(a)
            s.te(_)

            def _(sc):
                sc.wait_ge(s.sPE.h, s.sPE.v)
                s.sS.inc(sc.activation(pbuf, ps[:], AF.Exp,
                                       scale=0.125, bias=s.consts[:, 0:1]))
            s.se(_)
            exp_w[gi] = s.sS.v

            if mask is not None:
                def _(v):
                    v.wait_ge(s.sS.h, s.sS.v)
                    mb = mask[:].unsqueeze(1).broadcast_to([128, 8, 128])
                    pvw = pbuf.rearrange("p (h c) -> p h c", c=128)
                    s.sV.inc(v.tensor_tensor(pvw, pvw, mb, ALU.mult))
                s.ve(_)
                msk_w[gi] = s.sV.v

        def emit_pv(gi):
            kt, grp, mask = groups[gi]
            kslot = kt % 4
            pbuf = s.probsT[:, 0:1024] if gi % 2 == 0 else s.probsT2[:, 0:1024]

            def _(t):
                if mask is not None:
                    t.wait_ge(s.sV.h, msk_w[gi])
                else:
                    t.wait_ge(s.sS.h, exp_w[gi])
                a = None
                for h8 in range(8):
                    h = grp * 8 + h8
                    col = (h8 % 2) * 512 + (h8 // 2) * 128
                    a = t.matmul(s.ps_pv[:, pv_off(h):pv_off(h) + 65],
                                 pbuf[:, col:col + 128],
                                 s.v_all[:, kslot * 65:(kslot + 1) * 65],
                                 start=(gi in (0, 1) and h in (0, 7, 14)),
                                 stop=True, skip_group_check=True)
                s.sPE.inc(a)
            s.te(_)

        for gi in range(len(groups)):
            emit_scores(gi)
            if gi >= 1:
                emit_pv(gi - 1)
        emit_pv(len(groups) - 1)

        def _(v):
            v.wait_ge(s.sPE.h, s.sPE.v)
            for b_ in range(3):
                n = (7, 7, 2)[b_]
                src = s.ps_pv[:, b_ * 512 + 64: b_ * 512 + 64 + n * 65]
                v.tensor_copy(s.sums[:, b_ * 7:b_ * 7 + n].unsqueeze(2),
                              src.rearrange("p (h c) -> p h c", c=65)[:, :, 0:1])
            v.reciprocal(s.rsum[:], s.sums[:])
            a = None
            for b_ in range(3):
                n = (7, 7, 2)[b_]
                pvv = s.ps_pv[:, b_ * 512: b_ * 512 + n * 65].rearrange(
                    "p (h c) -> p h c", c=65)[:, :, 0:64]
                rb = s.rsum[:, b_ * 7:b_ * 7 + n].unsqueeze(2).broadcast_to([128, n, 64])
                dst = s.bfA[:, b_ * 7 * 64: b_ * 7 * 64 + n * 64].rearrange(
                    "p (h c) -> p h c", c=64)
                a = v.tensor_tensor(dst, pvv, rb, ALU.mult)
            s.sV.inc(a)
        s.ve(_)

        if s.debug_stop in ("smax", "smk"):
            def _(v):
                v.wait_ge(s.sS.h, s.sS.v)
                pass
            s.ve(_)
            self._dbg_dump(s.probsT[:].rearrange("p (a b) -> p (a b)") if False else s.probsT[:], row0)
            return

        if s.debug_stop == "pv":
            self._dbg_dump(s.bfA[:], row0)
            return

        s.transpose_blocks(s.bfA, NKC, s.bfB, [kc * 128 for kc in range(NKC)],
                           evac="s")

        def _(t):
            t.wait_ge(s.sV.h, s.sV.v)
            t.wait_ge(s.sS.h, s.sS.v)
            a = None
            for kc in range(NKC):
                for half in range(2):
                    a = t.matmul(s.ps_B[:, half * 512:(half + 1) * 512],
                                 s.bfB[:, kc * 128:(kc + 1) * 128],
                                 s.wf_T[:, kc * 1024 + half * 512:
                                        kc * 1024 + (half + 1) * 512],
                                 start=(kc == 0), stop=True, skip_group_check=True)
            for half in range(2):
                a = t.matmul(s.ps_B[:, half * 512:(half + 1) * 512],
                             s.onesb[:], s.bfrow[:, half * 512:(half + 1) * 512],
                             start=False, stop=True, skip_group_check=True)
            s.sPE.inc(a)
        s.te(_)

        def _(v):
            v.wait_ge(s.sPE.h, s.sPE.v)
            v.wait_ge(s.sOUT.h, s.sOUT.v)
            s.sV.inc(v.tensor_tensor(s.h2buf[:], s.ps_B[:], s.gab[:], ALU.mult))
        s.ve(_)
        r1v = s.sV.v

        def _(p):
            p.wait_ge(s.sV.h, r1v)
            s.sP.inc(p.tensor_tensor(s.x1buf[:], s.h2buf[:], s.xbuf[:], ALU.add))
        s.po(_)
        s.res1_cnt[ti] = s.sP.v
        res1_p = s.sP.v

        if s.debug_stop == "attn":
            def _(q):
                q.wait_ge(s.sV.h, s.sV.v)
                q.wait_ge(s.sP.h, s.sP.v)
                q.dma_start(s.out_d[row0:row0 + 128, :], s.x1buf[:]).then_inc(s.sOUT.h, 16)
                s.sOUT.v += 16
            s.qe(_)

            def _(v):
                v.wait_ge(s.sOUT.h, s.sOUT.v)
            s.ve(_)
            return

        if ti + 1 < BPC * NT:
            s.emit_head(ti + 1)

        def _(v):
            v.wait_ge(s.sP.h, res1_p)
            s.sV.inc(v.tensor_reduce(s.stat[:, 0:1], s.x1buf[:], AX.X, ALU.add))
        s.ve(_)

        def _(sc):
            sc.wait_ge(s.sP.h, res1_p)
            s.sS.inc(sc.activation(s.bfE[:], s.x1buf[:], AF.Square,
                                   accum_out=s.stat[:, 3:4]))
        s.se(_)

        def _(v):
            v.wait_ge(s.sS.h, s.sS.v)
            v.tensor_scalar(s.stat[:, 1:2], s.stat[:, 0:1], 1.0 / D, None, ALU.mult)
            v.tensor_scalar(s.stat[:, 2:3], s.stat[:, 1:2], s.stat[:, 1:2], -EPS,
                            ALU.mult, ALU.add)
            s.sV.inc(v.tensor_scalar(s.stat[:, 4:5], s.stat[:, 3:4], 1.0 / D,
                                     s.stat[:, 2:3], ALU.mult, ALU.subtract))
        s.ve(_)

        def _(p):
            p.wait_ge(s.sV.h, s.sV.v)
            s.sP.inc(p.tensor_tensor(s.stat[:, 5:6], s.stat[:, 4:5],
                                     s.consts[:, 1:2], ALU.pow))
        s.po(_)

        def _(v):
            v.wait_ge(s.sP.h, s.sP.v)
            v.wait_ge(s.sOUT.h, s.sOUT.v)
            v.scalar_tensor_tensor(s.h2buf[:], s.x1buf[:], s.stat[:, 1:2],
                                   s.cbb[:], ALU.subtract, ALU.mult)
            s.sV.inc(v.scalar_tensor_tensor(s.h2buf[:], s.h2buf[:], s.stat[:, 5:6],
                                            s.sbb[:], ALU.mult, ALU.add))
        s.ve(_)

        h2q_w = s.emit_quant2(s.h2buf, D, s.bfE, s.h2buf, s.bfD, s.bfE,
                              s.bmm, s.qr2, s.qsh, 1.0)

        s.transpose_blocks(s.bfE, NKC, s.bfC, [kc * 128 for kc in range(NKC)],
                           evac="s", wait_p=h2q_w)

        if s.debug_stop == "h2q":
            self._dbg_dump(s.bfC[:], row0)
            return

        pieces = PIECES
        if s.debug_stop == "gate1":
            pieces = PIECES[:1]
        nP = len(pieces)
        ps_free = {}
        quant_w = {}
        evac_w = {}
        gu_w = {}
        up_w = {}
        hT_w = {}

        def gu_ps(idx):
            return s.ps_A if idx % 2 == 0 else s.ps_pv

        def emit_gateup(idx):
            p0_, pn = pieces[idx]

            def _(t):
                if idx < 2:
                    t.wait_ge(s.sV.h, s.sV.v)
                    t.wait_ge(s.sS.h, s.sS.v)
                else:
                    t.wait_ge(s.sV.h, ps_free[idx - 2])
                a = None
                ps = gu_ps(idx)
                for kc in range(NKC):
                    a = t.matmul(ps[:, 0:pn],
                                 s.bfC[:, kc * 128:(kc + 1) * 128],
                                 s.wg_q[:, kc * II + p0_: kc * II + p0_ + pn],
                                 start=(kc == 0), stop=True, skip_group_check=True)
                s.sPE.inc(a)
            s.te(_)
            gu_w[idx] = s.sPE.v

            def _(t):
                a = None
                ps = gu_ps(idx)
                for kc in range(NKC):
                    a = t.matmul(ps[:, 512:512 + pn],
                                 s.bfC[:, kc * 128:(kc + 1) * 128],
                                 s.wu_q[:, kc * II + p0_: kc * II + p0_ + pn],
                                 start=(kc == 0), stop=True, skip_group_check=True)
                s.sPE.inc(a)
            s.te(_)
            up_w[idx] = s.sPE.v

        tanh_w = {}

        def emit_tanh(idx):
            p0_, pn = pieces[idx]

            def _(sc):
                sc.wait_ge(s.sPE.h, gu_w[idx])
                if idx >= 1:
                    sc.wait_ge(s.sP.h, quant_w[idx - 1])
                s.sS.inc(sc.activation(s.tf_a[:, 0:pn], gu_ps(idx)[:, 0:pn],
                                       AF.Tanh, scale=0.5))
            s.se(_)
            tanh_w[idx] = s.sS.v

        def emit_siluquant(idx):
            p0_, pn = pieces[idx]

            def _(v):
                v.wait_ge(s.sS.h, tanh_w[idx])
                v.wait_ge(s.sPE.h, up_w[idx])
                ps = gu_ps(idx)
                v.scalar_tensor_tensor(s.tf_b[:, 0:pn], s.tf_a[:, 0:pn], 1.0,
                                       ps[:, 0:pn], ALU.add, ALU.mult)
                s.sV.inc(v.tensor_tensor(s.tf_b[:, 0:pn], s.tf_b[:, 0:pn],
                                         ps[:, 512:512 + pn], ALU.mult))
            s.ve(_)
            ps_free[idx] = s.sV.v

            quant_w[idx] = s.emit_quant2(s.tf_b, pn, s.hmT, s.tf_a,
                                         s.bfE[:, 512:1024], s.bfE[:, 0:512],
                                         s.bmm, s.qr2, s.qsh, 1.0)

        def emit_hmidT(idx):
            p0_, pn = pieces[idx]
            nsub = pn // 128

            def _(t):
                t.wait_ge(s.sP.h, quant_w[idx])
                t.wait_ge(s.sS.h, s.sS.v)
                a = None
                for j in range(nsub):
                    a = t.transpose(s.ps_T[:, j * 128:(j + 1) * 128],
                                    s.hmT[:, j * 128:(j + 1) * 128],
                                    s.ident[:])
                s.sPE.inc(a)
            s.te(_)
            hT_w[idx] = s.sPE.v

            def _(e):
                e.wait_ge(s.sPE.h, hT_w[idx])
                a = None
                for j in range(nsub):
                    a = e.copy(s.hmq[:, j * 128:(j + 1) * 128],
                               s.ps_T[:, j * 128:(j + 1) * 128])
                s.sS.inc(a)
            s.se(_)
            evac_w[idx] = s.sS.v

        def emit_down(idx):
            p0_, pn = pieces[idx]

            def _(t):
                t.wait_ge(s.sS.h, evac_w[idx])
                a = None
                for j in range(pn // 128):
                    ic = p0_ // 128 + j
                    for half in range(2):
                        a = t.matmul(s.ps_B[:, half * 512:(half + 1) * 512],
                                     s.hmq[:, j * 128:(j + 1) * 128],
                                     s.wd_q[:, ic * 1024 + half * 512:
                                            ic * 1024 + (half + 1) * 512],
                                     start=(idx == 0 and j == 0), stop=True,
                                     skip_group_check=True)
                s.sPE.inc(a)
            s.te(_)

        for idx in range(nP + 1):
            if idx < nP:
                emit_gateup(idx)
                emit_tanh(idx)
            if idx >= 1:
                emit_hmidT(idx - 1)
            if idx < nP:
                emit_siluquant(idx)
            if idx >= 1:
                emit_down(idx - 1)
        if ps_free:
            s.prev_psfree = max(ps_free.values())

        def _(v):
            v.wait_ge(s.sPE.h, s.sPE.v)
            s.sV.inc(v.tensor_tensor(s.h2buf[:], s.ps_B[:], s.gbb[:], ALU.mult))
        s.ve(_)
        r2v = s.sV.v
        s.r2v_prev = r2v

        def _(p):
            p.wait_ge(s.sV.h, r2v)
            s.sP.inc(p.tensor_tensor(s.h2buf[:], s.h2buf[:], s.x1buf[:], ALU.add))
        s.po(_)

        def _(q):
            q.wait_ge(s.sP.h, s.sP.v)
            q.dma_start(s.out_d[row0:row0 + 128, :], s.h2buf[:]).then_inc(s.sOUT.h, 16)
            s.sOUT.v += 16
        s.qe(_)

    def seq_setup(self, seq):
        s = self
        with s.nc.Block() as blk:
            s.blk = blk
            p0rowb = s.bfD[0:1, :]
            for slot_, dst in ((0, s.cbb), (1, s.sbb), (2, s.gab), (3, s.gbb)):
                def _(q, slot_=slot_):
                    q.wait_ge(s.sV.h, s.sV.v)
                    q.wait_ge(s.sPE.h, s.sPE.v)
                    q.dma_start(p0rowb,
                                s.rows_d[seq:seq + 1, slot_ * D:(slot_ + 1) * D]
                                ).then_inc(s.sLD.h, 16)
                    s.sLD.v += 16
                s.qe(_)

                def _(t):
                    t.wait_ge(s.sLD.h, s.sLD.v)
                    t.matmul(s.ps_B[:, 0:512], s.onesb[:], p0rowb[:, 0:512],
                             start=True, stop=True, skip_group_check=True)
                    a = t.matmul(s.ps_B[:, 512:1024], s.onesb[:], p0rowb[:, 512:1024],
                                 start=True, stop=True, skip_group_check=True)
                    s.sPE.inc(a)
                s.te(_)

                def _(v, dst=dst):
                    v.wait_ge(s.sPE.h, s.sPE.v)
                    s.sV.inc(v.tensor_copy(dst[:], s.ps_B[:]))
                s.ve(_)

    def build(self):
        self.declare_io()
        with self.es:
            self.declare_persistent()
            self.phase0()
            self.phase1_mod()
            self.phase2_weights()
            self.declare_main_bufs()

            with self.nc.Block() as blk:
                self.blk = blk

                def _(v):
                    v.memset(
                        self.v_all[:].rearrange("p (s c) -> p s c", c=65)[:, :, 64:65],
                        1.0)
                    v.memset(self.stat[:], 0.0)
                self.ve(_)

                def _(q):
                    q.dma_start(self.xb[0][:],
                                self.x_in[0:128, :]).then_inc(self.sX.h, 16)
                    self.sX.v += 16
                self.qe(_)
                self.x_ld[0] = self.sX.v

            for seq in range(BPC):
                self.seq_setup(seq)
                with self.nc.Block() as blk:
                    self.blk = blk
                    for qt in range(NT):
                        self.emit_tile(seq, qt)

            with self.nc.Block() as blk:
                self.blk = blk

                def _(q):
                    q.wait_ge(self.sOUT.h, self.sOUT.v)
                self.qe(_)
        self.nc.compile()
        return self.nc


_NC_CACHE = {}


def _get_nc(debug_stop=None):
    if debug_stop not in _NC_CACHE:
        _NC_CACHE[debug_stop] = KB(debug_stop).build()
    return _NC_CACHE[debug_stop]


def kernel(L0_attn_x, L0_attn_segment_pos, L0_attn_Wq, L0_attn_Wk, L0_attn_Wv,
           L0_attn_Wf, L0_attn_bf, L0_ffn_w_gate, L0_ffn_w_up, L0_ffn_w_down,
           L0_c, L0_wmod, _debug_stop=None):
    nc = _get_nc(_debug_stop)

    assert np.asarray(L0_attn_segment_pos).shape[-1] == T
    x = np.ascontiguousarray(np.asarray(L0_attn_x, dtype=np.float32))
    ident = np.eye(128, dtype=ml_dtypes.bfloat16)
    jj, ii = np.meshgrid(np.arange(128), np.arange(128), indexing="ij")
    mnew = (jj <= ii).astype(ml_dtypes.bfloat16)
    mold = (jj >= ii).astype(ml_dtypes.bfloat16)
    shared = dict(
        wq=np.ascontiguousarray(np.asarray(L0_attn_Wq, dtype=np.float32)),
        wk=np.ascontiguousarray(np.asarray(L0_attn_Wk, dtype=np.float32)),
        wv=np.ascontiguousarray(np.asarray(L0_attn_Wv, dtype=np.float32)),
        wf=np.ascontiguousarray(np.asarray(L0_attn_Wf, dtype=np.float32)),
        bfv=np.ascontiguousarray(np.asarray(L0_attn_bf, dtype=np.float32)).reshape(1, D),
        wg=np.ascontiguousarray(np.asarray(L0_ffn_w_gate, dtype=np.float32)),
        wu=np.ascontiguousarray(np.asarray(L0_ffn_w_up, dtype=np.float32)),
        wd=np.ascontiguousarray(np.asarray(L0_ffn_w_down, dtype=np.float32)),
        wmod=np.ascontiguousarray(np.asarray(L0_wmod, dtype=np.float32)),
        ident=ident, mnew=np.ascontiguousarray(mnew), mold=np.ascontiguousarray(mold),
        identf=np.eye(BPC, dtype=np.float32),
    )
    c_full = np.asarray(L0_c, dtype=np.float32)
    in_maps = []
    for core in range(NCORES):
        m = dict(shared)
        m["x"] = np.ascontiguousarray(
            x[core * BPC:(core + 1) * BPC].reshape(BPC * T, D))
        m["c"] = np.ascontiguousarray(c_full[core * BPC:(core + 1) * BPC])
        in_maps.append(m)

    res = run_bass_kernel_spmd(nc, in_maps, core_ids=list(range(NCORES)))
    out = np.concatenate(
        [r["out"].reshape(BPC, T, D) for r in res.results], axis=0)
    return out.astype(np.float32)


# version 14
# speedup vs baseline: 1.0934x; 1.0404x over previous
import numpy as np
import ml_dtypes
from contextlib import ExitStack

import concourse.bacc as bacc
import concourse.mybir as mybir
from concourse.bass_utils import run_bass_kernel_spmd
from concourse import dve_ops
from concourse.dve_op import DveOp
from concourse.dve_spec import Spec, Src0, Src1, C0, C1, C2, maxx, minn
from concourse.dve_spec import lower as dve_lower
from concourse.dve_table_gen import dve_ver_for

F32 = mybir.dt.float32
BF16 = mybir.dt.bfloat16
AF = mybir.ActivationFunctionType
ALU = mybir.AluOpType
AX = mybir.AxisListType

B, T, D = 16, 2048, 1024
NCORES, BPC = 8, 2
NT = T // 128
NH, HD = 16, 64
II = 2816
NKC = D // 128
NIC = II // 128
EPS = 1e-5
PIECES = [(0, 512), (512, 512), (1024, 512), (1536, 512), (2048, 512), (2560, 256)]

C_RND1 = float(1.5 * 2.0**23)
C_RND2 = float(1.5 * 2.0**24)


def _rnd(x, g):
    return np.round(x / g) * g


def _ref_t1(in0, in1, s0, s1, imm2):
    return _rnd(np.clip(in0, -s0, s0), 1.0).astype(in0.dtype)


def _ref_t12(in0, in1, s0, s1, imm2):
    m4 = np.clip(in0, -s0, s0)
    return (_rnd(np.clip(in0 - m4, -s0, s0), 2.0) + in1).astype(in0.dtype)


def _ref_fin(in0, in1, s0, s1, imm2):
    t3 = ((in0 > s0).astype(in0.dtype) - (in0 < -s0).astype(in0.dtype)) * imm2
    return (t3 + in1).astype(in0.dtype)


_VER = dve_ver_for("TRN2")


def _mk_op(name, spec, ref):
    sha = dve_lower(spec, ver=_VER, perf_en=False, name=name).sha(_VER)
    return DveOp(name, spec, subdim=False, uops_sha={_VER: sha}, reference=ref)


_c4 = minn(maxx(Src0, C2), C0)
OP_T1 = _mk_op("ANT_FP4_T1", Spec(body=(_c4 + C1) - C1), _ref_t1)
_d2 = minn(maxx(Src0 - _c4, C2), C0)
OP_T12 = _mk_op("ANT_FP4_T12", Spec(body=((_d2 + C1) - C1) + Src1), _ref_t12)
_g3 = (Src0 > C0) - (Src0 < C1)
OP_FIN = _mk_op("ANT_FP4_FIN", Spec(body=_g3 * C2 + Src1), _ref_fin)


def _register_ops():
    victims = ["DVEHASH32L_SEED", "DVEHASH32L_BODY", "DVEHASH32L_DRAIN"]
    for op, victim in zip((OP_T1, OP_T12, OP_FIN), victims):
        if op.name in dve_ops._SUB_OPCODE_FOR_NAME:
            continue
        row = dve_ops._SUB_OPCODE_FOR_NAME.pop(victim)
        idx = next(i for i, o in enumerate(dve_ops.OPS) if o.name == victim)
        dve_ops.OPS[idx] = op
        dve_ops._SUB_OPCODE_FOR_NAME[op.name] = row
        dve_ops.CUSTOM_DVE_SPECS.pop(victim, None)
        dve_ops.CUSTOM_DVE_REFS.pop(victim, None)
        dve_ops.CUSTOM_DVE_SPECS[op.name] = op.spec
        dve_ops.CUSTOM_DVE_REFS[op.name] = op.ref


_register_ops()


_OUT_KW = ("out", "accum_out")


def _ap_rng(x):
    t = getattr(x, "tensor", None)
    name = getattr(t, "name", None)
    if name is None:
        return None
    try:
        off = x.offset
        reach = 1
        for step, count in list(x.ap)[1:]:
            reach += (count - 1) * step
        return (name, off, off + reach)
    except Exception:
        return (name, 0, 1 << 60)


def _overlaps(a, bs):
    for b in bs:
        if a[0] == b[0] and a[1] < b[2] and b[1] < a[2]:
            return True
    return False


class DrainWrap:

    _OPS = frozenset((
        "tensor_tensor", "tensor_scalar", "tensor_copy", "tensor_reduce",
        "reduce_max", "reduce_sum", "reciprocal", "memset", "_custom_dve",
        "scalar_tensor_tensor", "activation", "select", "copy",
    ))

    def __init__(self, eng, state):
        object.__setattr__(self, "_e", eng)
        object.__setattr__(self, "_st", state)

    def __getattr__(self, name):
        a = getattr(self._e, name)
        if name not in self._OPS:
            return a

        def f(*args, **kw):
            st = self._st
            rngs = [r for r in (_ap_rng(x) for x in args) if r] +                    [r for r in (_ap_rng(v) for v in kw.values()) if r]
            if name == "_custom_dve":
                outs = [_ap_rng(kw[k]) for k in _OUT_KW if kw.get(k) is not None]
            else:
                outs = [_ap_rng(args[0])]
                if kw.get("accum_out") is not None:
                    outs.append(_ap_rng(kw["accum_out"]))
            outs = [o for o in outs if o]
            ins = [r for r in rngs if r not in outs]
            if (any(_overlaps(o, st["w"]) or _overlaps(o, st["r"]) for o in outs)
                    or any(_overlaps(i, st["w"]) for i in ins)):
                self._e.drain()
                st["w"].clear()
                st["r"].clear()
            st["w"] += outs
            st["r"] += ins
            return a(*args, **kw)
        return f


class Ctr:
    def __init__(self, handle, step=1):
        self.h, self.v, self.step = handle, 0, step

    def inc(self, inst):
        inst.then_inc(self.h, self.step)
        self.v += self.step


def pv_off(h):
    b = h // 7
    return b * 512 + (h - 7 * b) * 65


class KB:

    def __init__(self, debug_stop=None):
        self.debug_stop = debug_stop
        self.nc = bacc.Bacc("TRN2", target_bir_lowering=False)
        self.es = ExitStack()
        self.blk = None
        self._haz = {e: {"w": [], "r": []} for e in ("v", "s", "p")}

    def te(self, f):
        self.blk.tensor(f)

    def ve(self, f):
        self.blk.vector(lambda e: f(DrainWrap(e, self._haz["v"])))

    def se(self, f):
        self.blk.scalar(lambda e: f(DrainWrap(e, self._haz["s"])))

    def po(self, f):
        self.blk.gpsimd(lambda e: f(DrainWrap(e, self._haz["p"])))

    def qe(self, f):
        self.blk.sync(f)

    def sbuf(self, name, shape, dt=F32, stack=None):
        return (stack or self.es).enter_context(self.nc.sbuf_tensor(name, shape, dt))

    def psum(self, name, shape, dt=F32):
        return self.es.enter_context(self.nc.psum_tensor(name, shape, dt))

    def sem(self, name, step=1):
        return Ctr(self.es.enter_context(self.nc.semaphore(name)), step)

    def declare_io(self):
        nc = self.nc
        self.x_in = nc.dram_tensor("x", [BPC * T, D], F32, kind="ExternalInput").ap()
        self.wq_in = nc.dram_tensor("wq", [D, D], F32, kind="ExternalInput").ap()
        self.wk_in = nc.dram_tensor("wk", [HD, D], F32, kind="ExternalInput").ap()
        self.wv_in = nc.dram_tensor("wv", [HD, D], F32, kind="ExternalInput").ap()
        self.wf_in = nc.dram_tensor("wf", [D, D], F32, kind="ExternalInput").ap()
        self.bf_in = nc.dram_tensor("bfv", [1, D], F32, kind="ExternalInput").ap()
        self.wg_in = nc.dram_tensor("wg", [II, D], F32, kind="ExternalInput").ap()
        self.wu_in = nc.dram_tensor("wu", [II, D], F32, kind="ExternalInput").ap()
        self.wd_in = nc.dram_tensor("wd", [D, II], F32, kind="ExternalInput").ap()
        self.c_in = nc.dram_tensor("c", [BPC, D], F32, kind="ExternalInput").ap()
        self.wmod_in = nc.dram_tensor("wmod", [D, 6 * D], F32, kind="ExternalInput").ap()
        self.ident_in = nc.dram_tensor("ident", [128, 128], BF16, kind="ExternalInput").ap()
        self.mnew_in = nc.dram_tensor("mnew", [128, 128], BF16, kind="ExternalInput").ap()
        self.mold_in = nc.dram_tensor("mold", [128, 128], BF16, kind="ExternalInput").ap()
        self.identf_in = nc.dram_tensor("identf", [BPC, BPC], F32, kind="ExternalInput").ap()
        self.out_d = nc.dram_tensor("out", [BPC * T, D], F32, kind="ExternalOutput").ap()

    def declare_persistent(self):
        sbuf, psum, sem = self.sbuf, self.psum, self.sem
        self.wq_T = sbuf("wq_T", [128, 64 * 128], BF16)
        self.kv_T = sbuf("kv_T", [128, 8 * 128], BF16)
        self.kv2_T = sbuf("kv2_T", [128, 8 * 128], BF16)
        self.wf_T = sbuf("wf_T", [128, 8 * 1024], BF16)
        self.wg_q = sbuf("wg_q", [128, 8 * II], BF16)
        self.wu_q = sbuf("wu_q", [128, 8 * II], BF16)
        self.wd_q = sbuf("wd_q", [128, NIC * 1024], BF16)
        self.ident = sbuf("ident_s", [128, 128], BF16)
        self.mnew = sbuf("mnew_s", [128, 128], BF16)
        self.mold = sbuf("mold_s", [128, 128], BF16)
        self.onesb = self.mnew[0:1, :]
        self.identf = sbuf("identf_s", [BPC, BPC], F32)
        self.bfrow = sbuf("bfrow", [1, D], BF16)
        self.consts = sbuf("consts", [128, 4], F32)
        self.ca1c = sbuf("ca1c", [128, NKC * BPC], F32)
        self.sac = sbuf("sac", [128, NKC * BPC], F32)
        self.rows_d = self.nc.dram_tensor("rows_d", [BPC, 4 * D], BF16).ap()
        self.cbb = sbuf("cbb", [128, D], BF16)
        self.sbb = sbuf("sbb", [128, D], BF16)
        self.gab = sbuf("gab", [128, D], BF16)
        self.gbb = sbuf("gbb", [128, D], BF16)
        self.ps_A = psum("ps_A", [128, 1024], F32)
        self.ps_B = psum("ps_B", [128, 1024], F32)
        self.ps_pv = psum("ps_pv", [128, 1280], F32)
        self.ps_T = psum("ps_T", [128, 512], BF16)
        self.sLD = sem("sLD", 16)
        self.sL2 = sem("sL2", 16)
        self.sL3 = sem("sL3", 16)
        self.sL4 = sem("sL4", 16)
        self.sX = sem("sX", 16)
        self.sX2 = sem("sX2", 16)
        self.sOUT = sem("sOUT", 16)
        self.sPE = sem("sPE")
        self.sV = sem("sV")
        self.sS = sem("sS")
        self.sP = sem("sP")

    def declare_main_bufs(self):
        sbuf = self.sbuf
        self.xbuf = sbuf("xbuf", [128, D], F32)
        self.xbuf2 = sbuf("xbuf2", [128, D], F32)
        self.xb = [self.xbuf, self.xbuf2]
        self.x_ld = [0, 0]
        self.res1_cnt = {}
        self.head_mod = {}
        self.prev_psfree = 0
        self.r2v_prev = 0
        self.x1buf = sbuf("x1buf", [128, D], F32)
        self.h2buf = sbuf("h2buf", [128, D], F32)
        self.bfA = sbuf("bfA", [128, D], BF16)
        self.bfB = sbuf("bfB", [128, D], BF16)
        self.bfC = sbuf("bfC", [128, D], BF16)
        self.bfD = sbuf("bfD", [128, D], BF16)
        self.bfE = sbuf("bfE", [128, D], BF16)
        self.probsT = None
        self.kk = sbuf("kk", [128, 4 * 128], BF16)
        self.v_all = sbuf("v_all", [128, 4 * 65], BF16)
        self.stat = sbuf("stat", [128, 16], F32)
        self.bmm = sbuf("bmm", [128, 64], F32)
        self.qr2 = sbuf("qr2", [128, 64], F32)
        self.qsh = sbuf("qsh", [128, 64], BF16)
        self.sums = self.bmm[:, 48:64]
        self.rsum = self.qr2[:, 48:64]
        self.probsT2 = self.bfD
        self.probsT = self.bfB
        self.tf_a = self.h2buf[:, 512:1024]
        self.tf_b = self.h2buf[:, 0:512]
        self.hmT = self.bfD[:, 0:512]
        self.hmq = self.bfD[:, 512:1024]

    def phase0(self):
        s = self
        with s.nc.Block() as blk:
            s.blk = blk

            def _(q):
                q.dma_start(s.ident[:], s.ident_in).then_inc(s.sLD.h, 16)
                q.dma_start(s.mnew[:], s.mnew_in).then_inc(s.sLD.h, 16)
                q.dma_start(s.mold[:], s.mold_in).then_inc(s.sLD.h, 16)
                q.dma_start(s.identf[:], s.identf_in).then_inc(s.sLD.h, 16)
                s.sLD.v += 64
            s.qe(_)

            def _(sc):
                sc.preload_act_table(AF.Exp)
            s.se(_)

            def _(v):
                v.memset(s.consts[:, 0:1], -30.0)
                v.memset(s.consts[:, 1:2], -0.5)
            s.ve(_)

    def phase1_mod(self):
        s = self
        with ExitStack() as ph:
            c_sb = s.sbuf("c_sb", [BPC, D], F32, ph)
            sct = s.sbuf("sct", [BPC, D], F32, ph)
            sc_bf = s.sbuf("sc_bf", [BPC, D], BF16, ph)
            rowstg = s.sbuf("rowstg", [BPC, D], BF16, ph)
            scT = s.sbuf("scT", [128, NKC * BPC], F32, ph)
            wmod_f = s.sbuf("wmod_f", [128, 1024], F32, ph)
            wmod_g = s.sbuf("wmod_g", [128, 1024], F32, ph)
            wmod_h = s.sbuf("wmod_h", [128, 1024], F32, ph)
            wmod_i = s.sbuf("wmod_i", [128, 1024], F32, ph)
            tmpr = wmod_f[0:BPC, :]

            with s.nc.Block() as blk:
                s.blk = blk

                def _(q):
                    q.dma_start(c_sb[:], s.c_in).then_inc(s.sLD.h, 16)
                    q.dma_start(sct[0:1, :], s.bf_in).then_inc(s.sLD.h, 16)
                    s.sLD.v += 32
                s.qe(_)

                def _(v):
                    v.wait_ge(s.sLD.h, s.sLD.v)
                    s.sV.inc(v.tensor_copy(s.bfrow[:], sct[0:1, :]))
                s.ve(_)

                def _(sc):
                    sc.wait_ge(s.sV.h, s.sV.v)
                    s.sS.inc(sc.activation(sct[:], c_sb[:], AF.Tanh, scale=0.5))
                s.se(_)

                def _(v):
                    v.wait_ge(s.sS.h, s.sS.v)
                    v.tensor_scalar(sct[:], sct[:], 0.5, 0.5, ALU.mult, ALU.add)
                    s.sV.inc(v.tensor_tensor(sct[:], sct[:], c_sb[:], ALU.mult))
                s.ve(_)

                def _(t):
                    t.wait_ge(s.sV.h, s.sV.v)
                    a = None
                    for kc in range(NKC):
                        a = t.transpose(s.ps_A[:, kc * BPC:(kc + 1) * BPC],
                                        sct[:, kc * 128:(kc + 1) * 128],
                                        s.identf[:])
                    s.sPE.inc(a)
                s.te(_)

                def _(v):
                    v.wait_ge(s.sPE.h, s.sPE.v)
                    s.sV.inc(v.tensor_copy(scT[:], s.ps_A[:, 0:NKC * BPC]))
                s.ve(_)

                WNB = 4
                wbufs = [wmod_f, wmod_g, wmod_h, wmod_i]
                wlds = [s.sLD, s.sL2, s.sL3, s.sL4]
                wqes = [s.qe, s.po, s.qe, s.po]
                wm_ld = [0] * WNB
                wm_pe = [0] * WNB
                wi = 0
                for pair in range(3):
                    for sub in range(2):
                        vec = pair * 2 + sub
                        ps = s.ps_A if sub == 0 else s.ps_B
                        for kc in range(NKC):
                            buf, ld = wbufs[wi % WNB], wlds[wi % WNB]
                            sv_now, ss_now = s.sV.v, s.sS.v

                            def _(q, vec=vec, kc=kc, wi=wi, ld=ld, buf=buf,
                                  sv_now=sv_now, ss_now=ss_now):
                                q.wait_ge(s.sPE.h, wm_pe[wi % WNB])
                                q.wait_ge(s.sV.h, sv_now)
                                q.wait_ge(s.sS.h, ss_now)
                                q.dma_start(buf[:],
                                            s.wmod_in[kc * 128:(kc + 1) * 128,
                                                      vec * D:(vec + 1) * D]
                                            ).then_inc(ld.h, 16)
                                ld.v += 16
                            wqes[wi % WNB](_)
                            wm_ld[wi % WNB] = ld.v

                            def _(t, ps=ps, kc=kc, buf=buf, wi=wi, ld=ld,
                                  sv_now=sv_now):
                                t.wait_ge(ld.h, wm_ld[wi % WNB])
                                t.wait_ge(s.sV.h, sv_now)
                                a = None
                                for piece in range(2):
                                    a = t.matmul(ps[0:BPC, piece * 512:(piece + 1) * 512],
                                                 scT[:, kc * BPC:(kc + 1) * BPC],
                                                 buf[:, piece * 512:(piece + 1) * 512],
                                                 start=(kc == 0), stop=(kc == NKC - 1),
                                                 skip_group_check=True)
                                s.sPE.inc(a)
                            s.te(_)
                            wm_pe[wi % WNB] = s.sPE.v
                            wi += 1

                    def _(v):
                        v.wait_ge(s.sPE.h, s.sPE.v)
                        v.tensor_copy(c_sb[:], s.ps_A[0:BPC, :])
                        s.sV.inc(v.tensor_copy(sct[:], s.ps_B[0:BPC, :]))
                    s.ve(_)

                    if pair == 0:
                        for rowbuf, dst, add1 in ((c_sb, s.sac, 0.0), (sct, s.ca1c, 1.0)):
                            for kc in range(NKC):
                                def _(t, rowbuf=rowbuf, kc=kc):
                                    t.wait_ge(s.sV.h, s.sV.v)
                                    s.sPE.inc(t.transpose(
                                        s.ps_B[:, 512:512 + BPC],
                                        rowbuf[:, kc * 128:(kc + 1) * 128],
                                        s.identf[:]))
                                s.te(_)

                                def _(v, dst=dst, add1=add1, kc=kc):
                                    v.wait_ge(s.sPE.h, s.sPE.v)
                                    a = None
                                    for sq in range(BPC):
                                        a = v.tensor_scalar(
                                            dst[:, kc * BPC + sq: kc * BPC + sq + 1],
                                            s.ps_B[:, 512 + sq:512 + sq + 1],
                                            add1, None, ALU.add)
                                    s.sV.inc(a)
                                s.ve(_)
                    elif pair == 1:
                        def _(sc):
                            sc.wait_ge(s.sV.h, s.sV.v)
                            s.sS.inc(sc.activation(tmpr, c_sb[:], AF.Sign))
                        s.se(_)

                        def _(v):
                            v.wait_ge(s.sS.h, s.sS.v)
                            v.tensor_tensor(c_sb[:], c_sb[:], tmpr, ALU.mult)
                            v.tensor_scalar(c_sb[:], c_sb[:], 0.25, None, ALU.max)
                            v.tensor_tensor(sc_bf[:], c_sb[:], tmpr, ALU.mult)
                            s.sV.inc(v.tensor_copy(rowstg[:], sct[:]))
                        s.ve(_)

                        def _(q):
                            q.wait_ge(s.sV.h, s.sV.v)
                            q.dma_start(s.rows_d[:, 2 * D:3 * D], sc_bf[:]).then_inc(s.sX.h, 16)
                            q.dma_start(s.rows_d[:, 1 * D:2 * D], rowstg[:]).then_inc(s.sX.h, 16)
                            s.sX.v += 32
                            q.wait_ge(s.sX.h, s.sX.v)
                        s.qe(_)
                    else:
                        def _(sc):
                            sc.wait_ge(s.sV.h, s.sV.v)
                            s.sS.inc(sc.activation(tmpr, sct[:], AF.Sign))
                        s.se(_)

                        def _(v):
                            v.wait_ge(s.sS.h, s.sS.v)
                            v.wait_ge(s.sX.h, s.sX.v)
                            v.tensor_scalar(sc_bf[:], c_sb[:], 1.0, None, ALU.add)
                            v.tensor_tensor(sct[:], sct[:], tmpr, ALU.mult)
                            v.tensor_scalar(sct[:], sct[:], 0.25, None, ALU.max)
                            s.sV.inc(v.tensor_tensor(rowstg[:], sct[:], tmpr, ALU.mult))
                        s.ve(_)

                        def _(q):
                            q.wait_ge(s.sV.h, s.sV.v)
                            q.dma_start(s.rows_d[:, 0 * D:1 * D], sc_bf[:]).then_inc(s.sX.h, 16)
                            q.dma_start(s.rows_d[:, 3 * D:4 * D], rowstg[:]).then_inc(s.sX.h, 16)
                            s.sX.v += 32
                            q.wait_ge(s.sX.h, s.sX.v)
                        s.qe(_)

    def emit_quant(self, v, src_f32, F, out_bf, w_scr, t1_bf, t12_bf,
                   bm, r2, sh, fold=1.0):
        nb = F // 16
        src = src_f32[:, 0:F]
        v.tensor_reduce(bm[:, 0:nb], src.rearrange("p (n k) -> p n k", k=16),
                        AX.X, ALU.max, apply_absolute_value=True)
        v.tensor_scalar(bm[:, 0:nb], bm[:, 0:nb], 1.0 / 12.0, 5e-13, ALU.mult, ALU.max)
        v.tensor_scalar(sh[:, 0:nb], bm[:, 0:nb], fold, None, ALU.mult)
        v.reciprocal(r2[:, 0:nb], bm[:, 0:nb])
        rb = r2[:, 0:nb].unsqueeze(2).broadcast_to([128, nb, 16])
        v.tensor_tensor(w_scr[:, 0:F].rearrange("p (n k) -> p n k", k=16),
                        src.rearrange("p (n k) -> p n k", k=16), rb, ALU.mult)
        v._custom_dve(OP_T1, out=t1_bf[:, 0:F], in0=w_scr[:, 0:F],
                      s0=4.0, s1=C_RND1, imm2=-4.0)
        v._custom_dve(OP_T12, out=t12_bf[:, 0:F], in0=w_scr[:, 0:F],
                      in1=t1_bf[:, 0:F], s0=4.0, s1=C_RND2, imm2=-4.0)
        v._custom_dve(OP_FIN, out=t1_bf[:, 0:F], in0=w_scr[:, 0:F],
                      in1=t12_bf[:, 0:F], s0=10.0, s1=-10.0, imm2=4.0)
        shb = sh[:, 0:nb].unsqueeze(2).broadcast_to([128, nb, 16])
        return v.tensor_tensor(out_bf[:, 0:F].rearrange("p (n k) -> p n k", k=16),
                               t1_bf[:, 0:F].rearrange("p (n k) -> p n k", k=16),
                               shb, ALU.mult)

    def emit_quant2(self, src_f32, F, out_bf, w_scr, t1_bf, t12_bf,
                    bm, r2, sh, fold=1.0, pre=None):
        s = self
        nb = F // 16

        def _(v):
            if pre is not None:
                pre(v)
            srcv = src_f32[:, 0:F]
            v.tensor_reduce(bm[:, 0:nb], srcv.rearrange("p (n k) -> p n k", k=16),
                            AX.X, ALU.max, apply_absolute_value=True)
            v.tensor_scalar(bm[:, 0:nb], bm[:, 0:nb], 1.0 / 12.0, 5e-13,
                            ALU.mult, ALU.max)
            v.tensor_scalar(sh[:, 0:nb], bm[:, 0:nb], fold, None, ALU.mult)
            s.sV.inc(v.reciprocal(r2[:, 0:nb], bm[:, 0:nb]))
        s.ve(_)

        pe_now = s.sPE.v

        def _(p):
            p.wait_ge(s.sV.h, s.sV.v)
            p.wait_ge(s.sPE.h, pe_now)
            rb = r2[:, 0:nb].unsqueeze(2).broadcast_to([128, nb, 16])
            s.sP.inc(p.tensor_tensor(
                w_scr[:, 0:F].rearrange("p (n k) -> p n k", k=16),
                src_f32[:, 0:F].rearrange("p (n k) -> p n k", k=16),
                rb, ALU.mult))
        s.po(_)

        def _(v):
            v.wait_ge(s.sP.h, s.sP.v)
            v._custom_dve(OP_T1, out=t1_bf[:, 0:F], in0=w_scr[:, 0:F],
                          s0=4.0, s1=C_RND1, imm2=-4.0)
            v._custom_dve(OP_T12, out=t12_bf[:, 0:F], in0=w_scr[:, 0:F],
                          in1=t1_bf[:, 0:F], s0=4.0, s1=C_RND2, imm2=-4.0)
            s.sV.inc(v._custom_dve(OP_FIN, out=t1_bf[:, 0:F], in0=w_scr[:, 0:F],
                                   in1=t12_bf[:, 0:F], s0=10.0, s1=-10.0, imm2=4.0))
        s.ve(_)

        def _(p):
            p.wait_ge(s.sV.h, s.sV.v)
            p.wait_ge(s.sPE.h, pe_now)
            shb = sh[:, 0:nb].unsqueeze(2).broadcast_to([128, nb, 16])
            s.sP.inc(p.tensor_tensor(
                out_bf[:, 0:F].rearrange("p (n k) -> p n k", k=16),
                t1_bf[:, 0:F].rearrange("p (n k) -> p n k", k=16),
                shb, ALU.mult))
        s.po(_)
        return s.sP.v

    def transpose_blocks(self, src_bf, n_blocks, dst, dst_offs, evac="v",
                         wait_p=None):
        s = self
        for g0 in range(0, n_blocks, 4):
            gn = min(4, n_blocks - g0)

            def _(t, g0=g0, gn=gn):
                t.wait_ge(s.sV.h, s.sV.v)
                if wait_p is not None:
                    t.wait_ge(s.sP.h, wait_p)
                if evac == "s":
                    t.wait_ge(s.sS.h, s.sS.v)
                a = None
                for j in range(gn):
                    a = t.transpose(s.ps_T[:, j * 128:(j + 1) * 128],
                                    src_bf[:, (g0 + j) * 128:(g0 + j + 1) * 128],
                                    s.ident[:])
                s.sPE.inc(a)
            s.te(_)

            def _(e, g0=g0, gn=gn):
                e.wait_ge(s.sPE.h, s.sPE.v)
                cp = e.tensor_copy if evac == "v" else e.copy
                a = None
                for j in range(gn):
                    a = cp(dst[:, dst_offs[g0 + j]:dst_offs[g0 + j] + 128],
                           s.ps_T[:, j * 128:(j + 1) * 128])
                (s.sV if evac == "v" else s.sS).inc(a)
            (s.ve if evac == "v" else s.se)(_)

    def phase2_weights(self):
        s = self
        with ExitStack() as ph:
            WHALF = II // 2
            stg = s.sbuf("stg", [128, WHALF], F32, ph)
            stg2 = s.sbuf("stg2", [128, WHALF], F32, ph)
            stg3 = s.sbuf("stg3", [128, WHALF], F32, ph)
            stq = s.sbuf("stq", [128, WHALF], BF16, ph)
            qt1 = s.sbuf("qt1", [128, WHALF], BF16, ph)
            qt12 = s.sbuf("qt12", [128, WHALF], BF16, ph)
            bmw = s.sbuf("bmw", [128, WHALF // 16], F32, ph)
            qr2w = s.sbuf("qr2w", [128, WHALF // 16], F32, ph)
            qshw = s.sbuf("qshw", [128, WHALF // 16], BF16, ph)

            with s.nc.Block() as blk:
                s.blk = blk
                NB = 3
                bufs = [stg, stg2, stg3]
                lds = [s.sLD, s.sL2, s.sL3]
                qes = [s.qe, s.po, s.qe]
                st_state = {"i": 0, "ld": [0] * NB, "c": [None] * NB}

                def load_rows(src_ap, ncols):
                    i = st_state["i"]
                    buf, ld = bufs[i % NB], lds[i % NB]
                    gate = st_state["c"][i % NB]
                    if gate is None:
                        gate = s.sV.v

                    def _(q, ld=ld, gate=gate):
                        q.wait_ge(s.sV.h, gate)
                        q.dma_start(buf[:, 0:ncols], src_ap).then_inc(ld.h, 16)
                        ld.v += 16
                    qes[i % NB](_)
                    st_state["ld"][i % NB] = ld.v
                    st_state["i"] += 1
                    return buf

                def cast_stage(buf, ncols):
                    i = st_state["i"] - 1
                    ld = lds[i % NB]
                    thr = st_state["ld"][i % NB]
                    pe_now = s.sPE.v

                    def _(v, ld=ld, thr=thr, pe_now=pe_now):
                        v.wait_ge(ld.h, thr)
                        v.wait_ge(s.sPE.h, pe_now)
                        s.sV.inc(v.tensor_copy(stq[:, 0:ncols], buf[:, 0:ncols]))
                    s.ve(_)
                    st_state["c"][i % NB] = s.sV.v

                def quant_stage(buf, ncols, fold):
                    i = st_state["i"] - 1
                    ld = lds[i % NB]
                    thr = st_state["ld"][i % NB]
                    sp_now = s.sP.v

                    def _pre(v, ld=ld, thr=thr, sp_now=sp_now):
                        v.wait_ge(ld.h, thr)
                        v.wait_ge(s.sP.h, sp_now)
                    qw = s.emit_quant2(buf, ncols, stq, buf, qt1, qt12,
                                       bmw, qr2w, qshw, fold, pre=_pre)
                    st_state["c"][i % NB] = s.sV.v
                    return qw

                for oc in range(NKC):
                    b = load_rows(s.wq_in[oc * 128:(oc + 1) * 128, :], D)
                    cast_stage(b, D)
                    s.transpose_blocks(stq, NKC, s.wq_T,
                                       [(oc * 8 + kc) * 128 for kc in range(NKC)],
                                       evac="s")

                kvi = st_state["i"]
                kvbuf, kvld = bufs[kvi % NB], lds[kvi % NB]
                kvgate = st_state["c"][kvi % NB]
                if kvgate is None:
                    kvgate = s.sV.v

                def _(q, kvld=kvld, kvgate=kvgate):
                    q.wait_ge(s.sV.h, kvgate)
                    q.dma_start(kvbuf[0:HD, 0:D], s.wk_in).then_inc(kvld.h, 16)
                    q.dma_start(kvbuf[64:128, 0:D], s.wv_in).then_inc(kvld.h, 16)
                    kvld.v += 32
                qes[kvi % NB](_)
                st_state["ld"][kvi % NB] = kvld.v
                st_state["i"] += 1
                cast_stage(kvbuf, D)
                for kc in range(NKC):
                    def _(t, kc=kc):
                        t.wait_ge(s.sV.h, s.sV.v)
                        t.wait_ge(s.sS.h, s.sS.v)
                        s.sPE.inc(t.transpose(s.ps_T[:, 0:128],
                                              stq[:, kc * 128:(kc + 1) * 128],
                                              s.ident[:]))
                    s.te(_)

                    def _(v, kc=kc):
                        v.wait_ge(s.sPE.h, s.sPE.v)
                        v.tensor_copy(s.kv_T[:, kc * 128:(kc + 1) * 128], s.ps_T[:, 0:128])
                        v.tensor_copy(s.kv2_T[:, kc * 128:kc * 128 + 64],
                                      s.ps_T[:, 64:128])
                        s.sV.inc(v.tensor_copy(s.kv2_T[:, kc * 128 + 64:(kc + 1) * 128],
                                               s.ps_T[:, 0:64]))
                    s.ve(_)

                for oc in range(NKC):
                    b = load_rows(s.wf_in[oc * 128:(oc + 1) * 128, :], D)
                    cast_stage(b, D)
                    s.transpose_blocks(stq, NKC, s.wf_T,
                                       [kc * 1024 + oc * 128 for kc in range(NKC)],
                                       evac="s")

                for w_src, w_dst, fold in ((s.wg_in, s.wg_q, 1.0), (s.wu_in, s.wu_q, 0.5)):
                    for ic in range(NIC):
                        b = load_rows(w_src[ic * 128:(ic + 1) * 128, :], D)
                        qw = quant_stage(b, D, fold)
                        s.transpose_blocks(stq, NKC, w_dst,
                                           [kc * II + ic * 128 for kc in range(NKC)],
                                           evac="s", wait_p=qw)

                for oc in range(NKC):
                    for hf in range(2):
                        b = load_rows(s.wd_in[oc * 128:(oc + 1) * 128,
                                              hf * WHALF:(hf + 1) * WHALF], WHALF)
                        qw = quant_stage(b, WHALF, 1.0)
                        s.transpose_blocks(stq, WHALF // 128, s.wd_q,
                                           [(hf * 11 + ic) * 1024 + oc * 128
                                            for ic in range(WHALF // 128)],
                                           evac="s", wait_p=qw)

    def emit_ln(self, src, dst, wait_out=False):
        s = self

        def _(v):
            s.sV.inc(v.tensor_reduce(s.stat[:, 0:1], src[:], AX.X, ALU.add))
        s.ve(_)

        def _(sc):
            sc.wait_ge(s.sV.h, s.sV.v)
            s.sS.inc(sc.activation(s.bfE[:], src[:], AF.Square,
                                   accum_out=s.stat[:, 3:4]))
        s.se(_)

        red_w = s.sV.v

        def _(p):
            p.wait_ge(s.sS.h, s.sS.v)
            p.wait_ge(s.sV.h, red_w)
            p.tensor_scalar(s.stat[:, 1:2], s.stat[:, 0:1], 1.0 / D, None, ALU.mult)
            p.tensor_scalar(s.stat[:, 2:3], s.stat[:, 1:2], s.stat[:, 1:2], -EPS,
                            ALU.mult, ALU.add)
            p.tensor_scalar(s.stat[:, 4:5], s.stat[:, 3:4], 1.0 / D,
                            s.stat[:, 2:3], ALU.mult, ALU.subtract)
            s.sP.inc(p.tensor_tensor(s.stat[:, 5:6], s.stat[:, 4:5],
                                     s.consts[:, 1:2], ALU.pow))
        s.po(_)

        def _(v):
            v.wait_ge(s.sP.h, s.sP.v)
            if wait_out:
                v.wait_ge(s.sOUT.h, s.sOUT.v)
            s.sV.inc(v.tensor_scalar(dst[:], src[:], s.stat[:, 1:2], s.stat[:, 5:6],
                                     ALU.subtract, ALU.mult))
        s.ve(_)


    def emit_head(self, nti):
        s = self
        xb = s.xb[nti % 2]
        xsem = s.sX if nti % 2 == 0 else s.sX2
        xw = s.x_ld[nti % 2]
        seq = nti // NT
        sp_now = s.sP.v

        def _(v):
            v.wait_ge(xsem.h, xw)
            s.sV.inc(v.tensor_reduce(s.stat[:, 8:9], xb[:], AX.X, ALU.add))
        s.ve(_)

        def _(sc):
            sc.wait_ge(xsem.h, xw)
            sc.wait_ge(s.sP.h, sp_now)
            s.sS.inc(sc.activation(s.bfE[:], xb[:], AF.Square,
                                   accum_out=s.stat[:, 11:12]))
        s.se(_)

        def _(v):
            v.wait_ge(s.sS.h, s.sS.v)
            v.tensor_scalar(s.stat[:, 9:10], s.stat[:, 8:9], 1.0 / D, None,
                            ALU.mult)
            v.tensor_scalar(s.stat[:, 10:11], s.stat[:, 9:10], s.stat[:, 9:10],
                            -EPS, ALU.mult, ALU.add)
            s.sV.inc(v.tensor_scalar(s.stat[:, 12:13], s.stat[:, 11:12], 1.0 / D,
                                     s.stat[:, 10:11], ALU.mult, ALU.subtract))
        s.ve(_)

        def _(p):
            p.wait_ge(s.sV.h, s.sV.v)
            s.sP.inc(p.tensor_tensor(s.stat[:, 13:14], s.stat[:, 12:13],
                                     s.consts[:, 1:2], ALU.pow))
        s.po(_)
        pw = s.sP.v

        def _(v):
            v.wait_ge(s.sP.h, pw)
            v.wait_ge(s.sPE.h, s.sPE.v)
            s.sV.inc(v.tensor_scalar(s.bfA[:], xb[:], s.stat[:, 9:10],
                                     s.stat[:, 13:14], ALU.subtract, ALU.mult))
        s.ve(_)

        s.transpose_blocks(s.bfA, NKC, s.bfB, [kc * 128 for kc in range(NKC)],
                           evac="s")

        def _(v):
            v.wait_ge(s.sS.h, s.sS.v)
            a = None
            for kc in range(NKC):
                a = v.tensor_scalar(s.bfB[:, kc * 128:(kc + 1) * 128],
                                    s.bfB[:, kc * 128:(kc + 1) * 128],
                                    s.ca1c[:, kc * BPC + seq:kc * BPC + seq + 1],
                                    s.sac[:, kc * BPC + seq:kc * BPC + seq + 1],
                                    ALU.mult, ALU.add)
            s.sV.inc(a)
        s.ve(_)
        s.head_mod[nti] = s.sV.v

    def _dbg_dump(self, src_ap, row0):
        s = self

        def _(v):
            s.sV.inc(v.tensor_copy(s.h2buf[:], src_ap))
        s.ve(_)

        def _(q):
            q.wait_ge(s.sV.h, s.sV.v)
            q.wait_ge(s.sOUT.h, s.sOUT.v)
            q.dma_start(s.out_d[row0:row0 + 128, :], s.h2buf[:]).then_inc(s.sOUT.h, 16)
            s.sOUT.v += 16
        s.qe(_)

    def emit_tile(self, seq, qt):
        s = self
        row0 = seq * T + qt * 128
        ti = seq * NT + qt
        kts = [k for k in (qt - 2, qt - 1, qt) if k >= 0]
        slot = qt % 4
        s.xbuf = s.xb[ti % 2]

        if ti + 1 < BPC * NT:
            nti = ti + 1
            nrow = (nti // NT) * T + (nti % NT) * 128
            gate = s.res1_cnt.get(ti - 1)
            sem2 = s.sX if nti % 2 == 0 else s.sX2

            def _(q, nti=nti, nrow=nrow, gate=gate, sem2=sem2):
                if gate is None:
                    q.wait_ge(s.sV.h, s.sV.v)
                else:
                    q.wait_ge(s.sP.h, gate)
                q.dma_start(s.xb[nti % 2][:],
                            s.x_in[nrow:nrow + 128, :]).then_inc(sem2.h, 16)
                sem2.v += 16
            s.qe(_)
            s.x_ld[nti % 2] = sem2.v

        xsem = s.sX if ti % 2 == 0 else s.sX2
        xw = s.x_ld[ti % 2]

        def _(v):
            v.wait_ge(xsem.h, xw)
        s.ve(_)

        def _(sc):
            sc.wait_ge(xsem.h, xw)
        s.se(_)

        if ti == 0 and s.debug_stop != "noop":
            s.emit_head(0)

        if s.debug_stop == "noop":
            def _(v):
                s.sV.inc(v.tensor_copy(s.h2buf[:], s.xbuf[:]))
            s.ve(_)

            def _(q):
                q.wait_ge(s.sV.h, s.sV.v)
                q.wait_ge(s.sOUT.h, s.sOUT.v)
                q.dma_start(s.out_d[row0:row0 + 128, :], s.h2buf[:]).then_inc(s.sOUT.h, 16)
                s.sOUT.v += 16
            s.qe(_)
            return

        if s.debug_stop == "ln1":
            self._dbg_dump(s.bfB[:], row0)
            return

        qkv_gate = max(s.head_mod.get(ti, s.sV.v), s.prev_psfree)

        def _(t, qkv_gate=qkv_gate):
            t.wait_ge(s.sV.h, qkv_gate)
            a = None
            for mc in range(8):
                for kc in range(NKC):
                    a = t.matmul(s.ps_A[:, mc * 128:(mc + 1) * 128],
                                 s.wq_T[:, (mc * 8 + kc) * 128:(mc * 8 + kc + 1) * 128],
                                 s.bfB[:, kc * 128:(kc + 1) * 128],
                                 start=(kc == 0 and mc % 4 == 0), stop=True,
                                 skip_group_check=True)
            for kc in range(NKC):
                a = t.matmul(s.ps_pv[:, 0:128],
                             s.kv_T[:, kc * 128:(kc + 1) * 128],
                             s.bfB[:, kc * 128:(kc + 1) * 128],
                             start=(kc == 0), stop=True, skip_group_check=True)
            for kc in range(NKC):
                a = t.matmul(s.ps_pv[:, 128:256],
                             s.kv2_T[:, kc * 128:(kc + 1) * 128],
                             s.bfB[:, kc * 128:(kc + 1) * 128],
                             start=False, stop=True, skip_group_check=True)
            s.sPE.inc(a)
        s.te(_)

        def _(e):
            e.wait_ge(s.sPE.h, s.sPE.v)
            e.copy(s.bfC[:], s.ps_A[:])
            e.copy(s.kk[0:64, slot * 128:(slot + 1) * 128], s.ps_pv[0:64, 0:128])
            e.copy(s.kk[64:128, slot * 128:(slot + 1) * 128],
                   s.ps_pv[64:128, 128:256])
            s.sS.inc(e.copy(s.bfA[64:128, 0:128], s.ps_pv[64:128, 0:128]))
        s.se(_)

        def _(t):
            t.wait_ge(s.sS.h, s.sS.v)
            s.sPE.inc(t.transpose(s.ps_T[:, 0:64], s.bfA[64:128, 0:128],
                                  s.ident[64:128, 64:128]))
        s.te(_)

        def _(e):
            e.wait_ge(s.sPE.h, s.sPE.v)
            s.sS.inc(e.copy(s.v_all[:, slot * 65:slot * 65 + 64],
                            s.ps_T[:, 0:64]))
        s.se(_)

        if s.debug_stop == "qkv":
            self._dbg_dump(s.bfC[:], row0)
            return

        groups = []
        for kt in kts:
            mask = s.mnew if kt == qt else (s.mold if kt == qt - 2 else None)
            for grp in range(2):
                groups.append((kt, grp, mask))
        exp_w, msk_w = {}, {}

        def emit_scores(gi):
            kt, grp, mask = groups[gi]
            kslot = kt % 4
            ps = s.ps_A if gi % 2 == 0 else s.ps_B
            pbuf = s.probsT[:, 0:1024] if gi % 2 == 0 else s.probsT2[:, 0:1024]

            def _(t):
                if gi == 0:
                    t.wait_ge(s.sV.h, s.sV.v)
                    t.wait_ge(s.sS.h, s.sS.v)
                if gi == 1:
                    t.wait_ge(s.sV.h, s.r2v_prev)
                if gi >= 2:
                    t.wait_ge(s.sS.h, exp_w[gi - 2])
                a = None
                for h8 in range(8):
                    h = grp * 8 + h8
                    base = (h % 2) * 64
                    col = (h8 % 2) * 512 + (h8 // 2) * 128
                    a = t.matmul(ps[:, col:col + 128],
                                 s.kk[base:base + 64,
                                      kslot * 128:(kslot + 1) * 128],
                                 s.bfC[base:base + 64,
                                       (h // 2) * 128:(h // 2 + 1) * 128],
                                 start=(h8 in (0, 1)), stop=True,
                                 skip_group_check=True)
                s.sPE.inc(a)
            s.te(_)

            def _(sc):
                sc.wait_ge(s.sPE.h, s.sPE.v)
                s.sS.inc(sc.activation(pbuf, ps[:], AF.Exp,
                                       scale=0.125, bias=s.consts[:, 0:1]))
            s.se(_)
            exp_w[gi] = s.sS.v

            if mask is not None:
                def _(v):
                    v.wait_ge(s.sS.h, s.sS.v)
                    mb = mask[:].unsqueeze(1).broadcast_to([128, 8, 128])
                    pvw = pbuf.rearrange("p (h c) -> p h c", c=128)
                    s.sV.inc(v.tensor_tensor(pvw, pvw, mb, ALU.mult))
                s.ve(_)
                msk_w[gi] = s.sV.v

        def emit_pv(gi):
            kt, grp, mask = groups[gi]
            kslot = kt % 4
            pbuf = s.probsT[:, 0:1024] if gi % 2 == 0 else s.probsT2[:, 0:1024]

            def _(t):
                if mask is not None:
                    t.wait_ge(s.sV.h, msk_w[gi])
                else:
                    t.wait_ge(s.sS.h, exp_w[gi])
                a = None
                for h8 in range(8):
                    h = grp * 8 + h8
                    col = (h8 % 2) * 512 + (h8 // 2) * 128
                    a = t.matmul(s.ps_pv[:, pv_off(h):pv_off(h) + 65],
                                 pbuf[:, col:col + 128],
                                 s.v_all[:, kslot * 65:(kslot + 1) * 65],
                                 start=(gi in (0, 1) and h in (0, 7, 14)),
                                 stop=True, skip_group_check=True)
                s.sPE.inc(a)
            s.te(_)

        for gi in range(len(groups)):
            emit_scores(gi)
            if gi >= 1:
                emit_pv(gi - 1)
        emit_pv(len(groups) - 1)

        def _(v):
            v.wait_ge(s.sPE.h, s.sPE.v)
            for b_ in range(3):
                n = (7, 7, 2)[b_]
                src = s.ps_pv[:, b_ * 512 + 64: b_ * 512 + 64 + n * 65]
                v.tensor_copy(s.sums[:, b_ * 7:b_ * 7 + n].unsqueeze(2),
                              src.rearrange("p (h c) -> p h c", c=65)[:, :, 0:1])
            v.reciprocal(s.rsum[:], s.sums[:])
            a = None
            for b_ in range(3):
                n = (7, 7, 2)[b_]
                pvv = s.ps_pv[:, b_ * 512: b_ * 512 + n * 65].rearrange(
                    "p (h c) -> p h c", c=65)[:, :, 0:64]
                rb = s.rsum[:, b_ * 7:b_ * 7 + n].unsqueeze(2).broadcast_to([128, n, 64])
                dst = s.bfA[:, b_ * 7 * 64: b_ * 7 * 64 + n * 64].rearrange(
                    "p (h c) -> p h c", c=64)
                a = v.tensor_tensor(dst, pvv, rb, ALU.mult)
            s.sV.inc(a)
        s.ve(_)

        if s.debug_stop in ("smax", "smk"):
            def _(v):
                v.wait_ge(s.sS.h, s.sS.v)
                pass
            s.ve(_)
            self._dbg_dump(s.probsT[:].rearrange("p (a b) -> p (a b)") if False else s.probsT[:], row0)
            return

        if s.debug_stop == "pv":
            self._dbg_dump(s.bfA[:], row0)
            return

        s.transpose_blocks(s.bfA, NKC, s.bfB, [kc * 128 for kc in range(NKC)],
                           evac="s")

        def _(t):
            t.wait_ge(s.sV.h, s.sV.v)
            t.wait_ge(s.sS.h, s.sS.v)
            a = None
            for kc in range(NKC):
                for half in range(2):
                    a = t.matmul(s.ps_B[:, half * 512:(half + 1) * 512],
                                 s.bfB[:, kc * 128:(kc + 1) * 128],
                                 s.wf_T[:, kc * 1024 + half * 512:
                                        kc * 1024 + (half + 1) * 512],
                                 start=(kc == 0), stop=True, skip_group_check=True)
            for half in range(2):
                a = t.matmul(s.ps_B[:, half * 512:(half + 1) * 512],
                             s.onesb[:], s.bfrow[:, half * 512:(half + 1) * 512],
                             start=False, stop=True, skip_group_check=True)
            s.sPE.inc(a)
        s.te(_)

        def _(v):
            v.wait_ge(s.sPE.h, s.sPE.v)
            v.wait_ge(s.sOUT.h, s.sOUT.v)
            s.sV.inc(v.tensor_tensor(s.h2buf[:], s.ps_B[:], s.gab[:], ALU.mult))
        s.ve(_)
        r1v = s.sV.v

        def _(p):
            p.wait_ge(s.sV.h, r1v)
            s.sP.inc(p.tensor_tensor(s.x1buf[:], s.h2buf[:], s.xbuf[:], ALU.add))
        s.po(_)
        s.res1_cnt[ti] = s.sP.v
        res1_p = s.sP.v

        if s.debug_stop == "attn":
            def _(q):
                q.wait_ge(s.sV.h, s.sV.v)
                q.wait_ge(s.sP.h, s.sP.v)
                q.dma_start(s.out_d[row0:row0 + 128, :], s.x1buf[:]).then_inc(s.sOUT.h, 16)
                s.sOUT.v += 16
            s.qe(_)

            def _(v):
                v.wait_ge(s.sOUT.h, s.sOUT.v)
            s.ve(_)
            return

        if ti + 1 < BPC * NT:
            s.emit_head(ti + 1)

        def _(v):
            v.wait_ge(s.sP.h, res1_p)
            s.sV.inc(v.tensor_reduce(s.stat[:, 0:1], s.x1buf[:], AX.X, ALU.add))
        s.ve(_)

        def _(sc):
            sc.wait_ge(s.sP.h, res1_p)
            s.sS.inc(sc.activation(s.bfE[:], s.x1buf[:], AF.Square,
                                   accum_out=s.stat[:, 3:4]))
        s.se(_)

        red_w = s.sV.v

        def _(p):
            p.wait_ge(s.sS.h, s.sS.v)
            p.wait_ge(s.sV.h, red_w)
            p.tensor_scalar(s.stat[:, 1:2], s.stat[:, 0:1], 1.0 / D, None, ALU.mult)
            p.tensor_scalar(s.stat[:, 2:3], s.stat[:, 1:2], s.stat[:, 1:2], -EPS,
                            ALU.mult, ALU.add)
            p.tensor_scalar(s.stat[:, 4:5], s.stat[:, 3:4], 1.0 / D,
                            s.stat[:, 2:3], ALU.mult, ALU.subtract)
            s.sP.inc(p.tensor_tensor(s.stat[:, 5:6], s.stat[:, 4:5],
                                     s.consts[:, 1:2], ALU.pow))
        s.po(_)

        def _(v):
            v.wait_ge(s.sP.h, s.sP.v)
            v.wait_ge(s.sOUT.h, s.sOUT.v)
            v.scalar_tensor_tensor(s.h2buf[:], s.x1buf[:], s.stat[:, 1:2],
                                   s.cbb[:], ALU.subtract, ALU.mult)
            s.sV.inc(v.scalar_tensor_tensor(s.h2buf[:], s.h2buf[:], s.stat[:, 5:6],
                                            s.sbb[:], ALU.mult, ALU.add))
        s.ve(_)

        h2q_w = s.emit_quant2(s.h2buf, D, s.bfE, s.h2buf, s.bfD, s.bfE,
                              s.bmm, s.qr2, s.qsh, 1.0)

        s.transpose_blocks(s.bfE, NKC, s.bfC, [kc * 128 for kc in range(NKC)],
                           evac="s", wait_p=h2q_w)

        if s.debug_stop == "h2q":
            self._dbg_dump(s.bfC[:], row0)
            return

        pieces = PIECES
        if s.debug_stop == "gate1":
            pieces = PIECES[:1]
        nP = len(pieces)
        ps_free = {}
        quant_w = {}
        evac_w = {}
        gu_w = {}
        up_w = {}
        hT_w = {}

        def gu_ps(idx):
            return s.ps_A if idx % 2 == 0 else s.ps_pv

        def emit_gateup(idx):
            p0_, pn = pieces[idx]

            def _(t):
                if idx < 2:
                    t.wait_ge(s.sS.h, s.sS.v)
                else:
                    t.wait_ge(s.sV.h, ps_free[idx - 2])
                a = None
                ps = gu_ps(idx)
                for kc in range(NKC):
                    a = t.matmul(ps[:, 0:pn],
                                 s.bfC[:, kc * 128:(kc + 1) * 128],
                                 s.wg_q[:, kc * II + p0_: kc * II + p0_ + pn],
                                 start=(kc == 0), stop=True, skip_group_check=True)
                s.sPE.inc(a)
            s.te(_)
            gu_w[idx] = s.sPE.v

            def _(t):
                a = None
                ps = gu_ps(idx)
                for kc in range(NKC):
                    a = t.matmul(ps[:, 512:512 + pn],
                                 s.bfC[:, kc * 128:(kc + 1) * 128],
                                 s.wu_q[:, kc * II + p0_: kc * II + p0_ + pn],
                                 start=(kc == 0), stop=True, skip_group_check=True)
                s.sPE.inc(a)
            s.te(_)
            up_w[idx] = s.sPE.v

        tanh_w = {}

        def emit_tanh(idx):
            p0_, pn = pieces[idx]

            def _(sc):
                sc.wait_ge(s.sPE.h, gu_w[idx])
                if idx >= 1:
                    sc.wait_ge(s.sP.h, quant_w[idx - 1])
                s.sS.inc(sc.activation(s.tf_a[:, 0:pn], gu_ps(idx)[:, 0:pn],
                                       AF.Tanh, scale=0.5))
            s.se(_)
            tanh_w[idx] = s.sS.v

        def emit_siluquant(idx):
            p0_, pn = pieces[idx]

            def _(v):
                v.wait_ge(s.sS.h, tanh_w[idx])
                v.wait_ge(s.sPE.h, up_w[idx])
                ps = gu_ps(idx)
                v.scalar_tensor_tensor(s.tf_b[:, 0:pn], s.tf_a[:, 0:pn], 1.0,
                                       ps[:, 0:pn], ALU.add, ALU.mult)
                s.sV.inc(v.tensor_tensor(s.tf_b[:, 0:pn], s.tf_b[:, 0:pn],
                                         ps[:, 512:512 + pn], ALU.mult))
            s.ve(_)
            ps_free[idx] = s.sV.v

            quant_w[idx] = s.emit_quant2(s.tf_b, pn, s.hmT, s.tf_a,
                                         s.bfE[:, 512:1024], s.bfE[:, 0:512],
                                         s.bmm, s.qr2, s.qsh, 1.0)

        def emit_hmidT(idx):
            p0_, pn = pieces[idx]
            nsub = pn // 128

            def _(t):
                t.wait_ge(s.sP.h, quant_w[idx])
                t.wait_ge(s.sS.h, s.sS.v)
                a = None
                for j in range(nsub):
                    a = t.transpose(s.ps_T[:, j * 128:(j + 1) * 128],
                                    s.hmT[:, j * 128:(j + 1) * 128],
                                    s.ident[:])
                s.sPE.inc(a)
            s.te(_)
            hT_w[idx] = s.sPE.v

            def _(e):
                e.wait_ge(s.sPE.h, hT_w[idx])
                a = None
                for j in range(nsub):
                    a = e.copy(s.hmq[:, j * 128:(j + 1) * 128],
                               s.ps_T[:, j * 128:(j + 1) * 128])
                s.sS.inc(a)
            s.se(_)
            evac_w[idx] = s.sS.v

        def emit_down(idx):
            p0_, pn = pieces[idx]

            def _(t):
                t.wait_ge(s.sS.h, evac_w[idx])
                a = None
                for j in range(pn // 128):
                    ic = p0_ // 128 + j
                    for half in range(2):
                        a = t.matmul(s.ps_B[:, half * 512:(half + 1) * 512],
                                     s.hmq[:, j * 128:(j + 1) * 128],
                                     s.wd_q[:, ic * 1024 + half * 512:
                                            ic * 1024 + (half + 1) * 512],
                                     start=(idx == 0 and j == 0), stop=True,
                                     skip_group_check=True)
                s.sPE.inc(a)
            s.te(_)

        for idx in range(nP + 1):
            if idx < nP:
                emit_gateup(idx)
                emit_tanh(idx)
            if idx >= 1:
                emit_hmidT(idx - 1)
            if idx < nP:
                emit_siluquant(idx)
            if idx >= 1:
                emit_down(idx - 1)
        if ps_free:
            s.prev_psfree = max(ps_free.values())

        def _(v):
            v.wait_ge(s.sPE.h, s.sPE.v)
            s.sV.inc(v.tensor_tensor(s.h2buf[:], s.ps_B[:], s.gbb[:], ALU.mult))
        s.ve(_)
        r2v = s.sV.v
        s.r2v_prev = r2v

        def _(p):
            p.wait_ge(s.sV.h, r2v)
            s.sP.inc(p.tensor_tensor(s.h2buf[:], s.h2buf[:], s.x1buf[:], ALU.add))
        s.po(_)

        def _(q):
            q.wait_ge(s.sP.h, s.sP.v)
            q.dma_start(s.out_d[row0:row0 + 128, :], s.h2buf[:]).then_inc(s.sOUT.h, 16)
            s.sOUT.v += 16
        s.qe(_)

    def seq_setup(self, seq):
        s = self
        with s.nc.Block() as blk:
            s.blk = blk
            p0rowb = s.bfD[0:1, :]
            for slot_, dst in ((0, s.cbb), (1, s.sbb), (2, s.gab), (3, s.gbb)):
                def _(q, slot_=slot_):
                    q.wait_ge(s.sV.h, s.sV.v)
                    q.wait_ge(s.sPE.h, s.sPE.v)
                    q.dma_start(p0rowb,
                                s.rows_d[seq:seq + 1, slot_ * D:(slot_ + 1) * D]
                                ).then_inc(s.sLD.h, 16)
                    s.sLD.v += 16
                s.qe(_)

                def _(t):
                    t.wait_ge(s.sLD.h, s.sLD.v)
                    t.matmul(s.ps_B[:, 0:512], s.onesb[:], p0rowb[:, 0:512],
                             start=True, stop=True, skip_group_check=True)
                    a = t.matmul(s.ps_B[:, 512:1024], s.onesb[:], p0rowb[:, 512:1024],
                                 start=True, stop=True, skip_group_check=True)
                    s.sPE.inc(a)
                s.te(_)

                def _(v, dst=dst):
                    v.wait_ge(s.sPE.h, s.sPE.v)
                    s.sV.inc(v.tensor_copy(dst[:], s.ps_B[:]))
                s.ve(_)

    def build(self):
        self.declare_io()
        with self.es:
            self.declare_persistent()
            self.phase0()
            self.phase1_mod()
            self.phase2_weights()
            self.declare_main_bufs()

            with self.nc.Block() as blk:
                self.blk = blk

                def _(v):
                    v.memset(
                        self.v_all[:].rearrange("p (s c) -> p s c", c=65)[:, :, 64:65],
                        1.0)
                    v.memset(self.stat[:], 0.0)
                self.ve(_)

                def _(q):
                    q.dma_start(self.xb[0][:],
                                self.x_in[0:128, :]).then_inc(self.sX.h, 16)
                    self.sX.v += 16
                self.qe(_)
                self.x_ld[0] = self.sX.v

            for seq in range(BPC):
                self.seq_setup(seq)
                with self.nc.Block() as blk:
                    self.blk = blk
                    for qt in range(NT):
                        self.emit_tile(seq, qt)

            with self.nc.Block() as blk:
                self.blk = blk

                def _(q):
                    q.wait_ge(self.sOUT.h, self.sOUT.v)
                self.qe(_)
        self.nc.compile()
        return self.nc


_NC_CACHE = {}


def _get_nc(debug_stop=None):
    if debug_stop not in _NC_CACHE:
        _NC_CACHE[debug_stop] = KB(debug_stop).build()
    return _NC_CACHE[debug_stop]


def kernel(L0_attn_x, L0_attn_segment_pos, L0_attn_Wq, L0_attn_Wk, L0_attn_Wv,
           L0_attn_Wf, L0_attn_bf, L0_ffn_w_gate, L0_ffn_w_up, L0_ffn_w_down,
           L0_c, L0_wmod, _debug_stop=None):
    nc = _get_nc(_debug_stop)

    assert np.asarray(L0_attn_segment_pos).shape[-1] == T
    x = np.ascontiguousarray(np.asarray(L0_attn_x, dtype=np.float32))
    ident = np.eye(128, dtype=ml_dtypes.bfloat16)
    jj, ii = np.meshgrid(np.arange(128), np.arange(128), indexing="ij")
    mnew = (jj <= ii).astype(ml_dtypes.bfloat16)
    mold = (jj >= ii).astype(ml_dtypes.bfloat16)
    shared = dict(
        wq=np.ascontiguousarray(np.asarray(L0_attn_Wq, dtype=np.float32)),
        wk=np.ascontiguousarray(np.asarray(L0_attn_Wk, dtype=np.float32)),
        wv=np.ascontiguousarray(np.asarray(L0_attn_Wv, dtype=np.float32)),
        wf=np.ascontiguousarray(np.asarray(L0_attn_Wf, dtype=np.float32)),
        bfv=np.ascontiguousarray(np.asarray(L0_attn_bf, dtype=np.float32)).reshape(1, D),
        wg=np.ascontiguousarray(np.asarray(L0_ffn_w_gate, dtype=np.float32)),
        wu=np.ascontiguousarray(np.asarray(L0_ffn_w_up, dtype=np.float32)),
        wd=np.ascontiguousarray(np.asarray(L0_ffn_w_down, dtype=np.float32)),
        wmod=np.ascontiguousarray(np.asarray(L0_wmod, dtype=np.float32)),
        ident=ident, mnew=np.ascontiguousarray(mnew), mold=np.ascontiguousarray(mold),
        identf=np.eye(BPC, dtype=np.float32),
    )
    c_full = np.asarray(L0_c, dtype=np.float32)
    in_maps = []
    for core in range(NCORES):
        m = dict(shared)
        m["x"] = np.ascontiguousarray(
            x[core * BPC:(core + 1) * BPC].reshape(BPC * T, D))
        m["c"] = np.ascontiguousarray(c_full[core * BPC:(core + 1) * BPC])
        in_maps.append(m)

    res = run_bass_kernel_spmd(nc, in_maps, core_ids=list(range(NCORES)))
    out = np.concatenate(
        [r["out"].reshape(BPC, T, D) for r in res.results], axis=0)
    return out.astype(np.float32)
